# version 07
# baseline (speedup 1.0000x reference)
import numpy as np
import ml_dtypes
from contextlib import ExitStack

import concourse.bass as bass
import concourse.mybir as mybir
from concourse.bass_utils import run_bass_kernel_spmd

F32 = mybir.dt.float32
BF16 = mybir.dt.bfloat16
I32 = mybir.dt.int32
AF = mybir.ActivationFunctionType
ALU = mybir.AluOpType
AX = mybir.AxisListType

S = 4096
DM = 2048
NH = 16
NKV = 4
HD = 128
FF = 8192
TOK = 2048
TILE = 512
NT = TOK // TILE
SUB = TILE // 128
NST_ALL = S // TILE
NC_DM = DM // 128
NC_FF = FF // 128
KT = S // 128

SERIAL_POINTS = set()
DUMP_MASK = set(range(7))

SM = 1.0 / float(np.sqrt(128.0))
CF = float(np.float32(1.0 / np.sqrt(FF)))
EPS_Q = 1e-5
EPS_LN = 1e-5


def _hadamard(n):
    h = np.array([[1.0]])
    while h.shape[0] < n:
        h = np.block([[h, h], [h, -h]])
    return h


class Bld:
    ENGS = ("tensor", "vector", "scalar", "sync", "gpsimd")

    def __init__(self, nc, stack):
        self.nc = nc
        self.stack = stack
        self.ops = []
        self.semv = {}
        self.ord_sem = {}
        self.ord_cnt = {}
        for eng in ("vector", "scalar", "tensor"):
            self.ord_sem[eng] = None

    def sem(self, name):
        h = self.stack.enter_context(self.nc.semaphore(name))
        self.semv[id(h)] = 0
        return h

    def on(self, eng, fn):
        self.ops.append((eng, fn))

    def bump(self, sem, k=1):
        self.semv[id(sem)] += k
        return self.semv[id(sem)]

    def val(self, sem):
        return self.semv[id(sem)]

    def dma(self, q, out, in_, sem):
        tgt = self.bump(sem, 16)
        self.on(q, lambda e, o=out, i=in_, s=sem: e.dma_start(out=o, in_=i).then_inc(s, 16))
        return tgt

    def wait(self, eng, sem, v=None):
        v = self.val(sem) if v is None else v
        if v > 0:
            self.on(eng, lambda e, s=sem, vv=v: e.wait_ge(s, vv))
        return v

    def init_ord(self):
        self.last_marker = {}
        for eng in ("vector", "scalar"):
            self.ord_sem[eng] = self.sem(f"s_ord_{eng}")
            self.last_marker[eng] = None

    def op(self, eng, fn, sem=None, k=1):
        osem = self.ord_sem.get(eng)
        prev = self.last_marker.get(eng) if osem is not None else None
        if sem is not None:
            tgt = self.bump(sem, k)
            marker = (sem, tgt)
        else:
            tgt = None
            marker = None
        if osem is not None:
            if marker is None:
                marker = (osem, self.bump(osem, 1))
            self.last_marker[eng] = marker
            msem, mval = marker
            use_ord = (sem is None)

        def g(e, f=fn, s=sem, kk=k, prev=prev, osem=osem):
            if prev is not None:
                e.wait_ge(prev[0], prev[1])
            ins = f(e)
            if osem is not None and s is None:
                ins.then_inc(osem, 1)
            if s is not None:
                ins.then_inc(s, kk)
        self.on(eng, g)
        return tgt

    def barrier(self, sem):
        tgt = self.val(sem) + 5
        for eng in self.ENGS:
            self.bump(sem, 1)
            self.on(eng, lambda e, s=sem: e.nop().then_inc(s, 1))
        for eng in self.ENGS:
            self.wait(eng, sem, tgt)
        return tgt

    def run(self, block):
        def body(eng_name):
            def f(e):
                for name, fn in self.ops:
                    if name == eng_name:
                        fn(e)
            return f
        block.tensor(body("tensor"))
        block.vector(body("vector"))
        block.scalar(body("scalar"))
        block.sync(body("sync"))
        block.gpsimd(body("gpsimd"))


def build_program(debug_phase=None):
    nc = bass.Bass("TRN2", target_bir_lowering=False)
    stack = ExitStack()

    x_all = nc.dram_tensor("x_all", [S, DM], F32, kind="ExternalInput").ap()
    x_own = nc.dram_tensor("x_own", [TOK, DM], F32, kind="ExternalInput").ap()
    wq = nc.dram_tensor("wq", [DM, DM], F32, kind="ExternalInput").ap()
    wk = nc.dram_tensor("wk", [DM, 512], F32, kind="ExternalInput").ap()
    wv = nc.dram_tensor("wv", [DM, 512], F32, kind="ExternalInput").ap()
    wo = nc.dram_tensor("wo", [DM, DM], F32, kind="ExternalInput").ap()
    bq = nc.dram_tensor("bq", [DM], F32, kind="ExternalInput").ap()
    bk = nc.dram_tensor("bk", [512], F32, kind="ExternalInput").ap()
    bv = nc.dram_tensor("bv", [512], F32, kind="ExternalInput").ap()
    bo = nc.dram_tensor("bo", [DM], F32, kind="ExternalInput").ap()
    wg = nc.dram_tensor("wg", [FF, DM], F32, kind="ExternalInput").ap()
    wu = nc.dram_tensor("wu", [FF, DM], F32, kind="ExternalInput").ap()
    wd = nc.dram_tensor("wd", [DM, FF], F32, kind="ExternalInput").ap()
    h128_in = nc.dram_tensor("h128", [128, 128], BF16, kind="ExternalInput").ap()
    eyeb_in = nc.dram_tensor("eyeb", [128, 128], BF16, kind="ExternalInput").ap()
    eyef_in = nc.dram_tensor("eyef", [128, 128], F32, kind="ExternalInput").ap()
    onesb_in = nc.dram_tensor("onesb", [128, 128], BF16, kind="ExternalInput").ap()
    onesf_in = nc.dram_tensor("onesf", [128, 128], F32, kind="ExternalInput").ap()

    out = nc.dram_tensor("out", [TOK, DM], F32, kind="ExternalOutput").ap()

    dbg = "ExternalOutput" if debug_phase == "A" else "Internal"
    q_scr = nc.dram_tensor("q_scr", [NH, NT, 128, TILE], BF16, kind=dbg).ap()
    k_scr = nc.dram_tensor("k_scr", [NKV, NST_ALL, 128, TILE], BF16, kind=dbg).ap()
    v_scr = nc.dram_tensor("v_scr", [KT, 128, 512], BF16, kind=dbg).ap()
    wgT_scr = nc.dram_tensor("wgT_scr", [NC_FF, 128, DM], BF16, kind=dbg).ap()
    wuT_scr = nc.dram_tensor("wuT_scr", [NC_FF, 128, DM], BF16, kind=dbg).ap()
    wdT_scr = nc.dram_tensor("wdT_scr", [NC_DM, 128, FF], BF16, kind=dbg).ap()
    woB_scr = nc.dram_tensor("woB_scr", [NC_DM, 128, DM], BF16, kind=dbg).ap()
    s1_scr = nc.dram_tensor("s1_scr", [SUB, 128, DM], F32,
                            kind="ExternalOutput" if debug_phase in ("B1", "B4", "BX") else "Internal").ap()
    if debug_phase == "A":
        dbgA = nc.dram_tensor("dbgA", [128, 256], F32, kind="ExternalOutput").ap()
    dbgF = None
    dbgS = None
    if debug_phase in ("B1", "B4", "BX"):
        dbgF = nc.dram_tensor("dbgF", [6, 128, 66 * 512], BF16, kind="ExternalOutput").ap()
        dbgS = nc.dram_tensor("dbgS", [128, 64], F32, kind="ExternalOutput").ap()

    with stack:
        bld = Bld(nc, stack)
        s_ld = bld.sem("s_ld")
        s_ld2 = bld.sem("s_ld2")
        s_ld3 = bld.sem("s_ld3")
        s_st = bld.sem("s_st")
        s_te = bld.sem("s_te")
        s_ve = bld.sem("s_ve")
        s_se = bld.sem("s_se")
        s_ph = bld.sem("s_ph")
        bld.init_ord()

        def sbuf(name, cols, dt=BF16, rows=128, st=stack):
            return st.enter_context(nc.sbuf_tensor(name, [rows, cols], dt))

        h128 = sbuf("c_h128", 128)
        eyeb = sbuf("c_eyeb", 128)
        eyef = sbuf("c_eyef", 128, F32)
        onesb = sbuf("c_onesb", 128)
        onesf = sbuf("c_onesf", 128, F32)
        bq_t = sbuf("bq_t", NH, F32)
        bk_t = sbuf("bk_t", NKV, F32)
        bo_t = sbuf("bo_t", NC_DM, F32)
        bv_bc = sbuf("bv_bc", 512, F32)
        negB = sbuf("negB", NH * NT, F32)
        eps_t = sbuf("eps_t", 1, F32)
        neg1_t = sbuf("neg1_t", 1, F32)
        sw_g = sbuf("sw_g", 1, F32)
        sw_u = sbuf("sw_u", 1, F32)
        sw_d = sbuf("sw_d", 1, F32)
        isw_g = sbuf("isw_g", 1, F32)
        isw_u = sbuf("isw_u", 1, F32)
        isw_d = sbuf("isw_d", 1, F32)
        strow = sbuf("strow", 512, F32, rows=1)
        st1 = sbuf("st1", 16, F32)
        qmax2 = sbuf("qmax2", NH * NT, F32, rows=1)
        kmax2 = sbuf("kmax2", NKV, F32, rows=1)

        pb = [stack.enter_context(nc.psum_tensor(f"pb{i}", [128, 512], F32)) for i in range(6)]
        pbf = stack.enter_context(nc.psum_tensor("pbf", [128, 512], BF16))
        p1 = stack.enter_context(nc.psum_tensor("p1", [1, 512], F32))

        for t in (h128_in, eyeb_in, eyef_in, onesb_in, onesf_in):
            pass
        bld.dma("sync", h128[:], h128_in, s_ld)
        bld.dma("sync", eyeb[:], eyeb_in, s_ld)
        bld.dma("sync", eyef[:], eyef_in, s_ld)
        bld.dma("sync", onesb[:], onesb_in, s_ld)
        bld.dma("sync", onesf[:], onesf_in, s_ld)
        for h in range(NH):
            bld.dma("sync", bq_t[:, h:h + 1], bq[h * 128:(h + 1) * 128].rearrange("(p o) -> p o", o=1), s_ld)
            bld.dma("sync", bo_t[:, h:h + 1], bo[h * 128:(h + 1) * 128].rearrange("(p o) -> p o", o=1), s_ld)
        for h in range(NKV):
            bld.dma("sync", bk_t[:, h:h + 1], bk[h * 128:(h + 1) * 128].rearrange("(p o) -> p o", o=1), s_ld)
        bld.dma("sync", strow[0:1, :], bv.rearrange("(o f) -> o f", o=1), s_ld)
        tgt_consts = bld.val(s_ld)
        bld.wait("vector", s_ld, tgt_consts)
        bld.op("vector", lambda e: e.memset(eps_t[:], EPS_LN))
        bld.op("vector", lambda e: e.memset(neg1_t[:], -1.0))
        bld.op("vector", lambda e: e.memset(kmax2[:], 0.0), s_ve)
        bld.wait("scalar", s_ld, tgt_consts)
        bld.wait("tensor", s_ld, tgt_consts)
        bld.op("tensor", lambda e: e.matmul(pb[0][:], onesf[0:1, :], strow[0:1, :], start=True, stop=True), s_te)
        bld.wait("vector", s_te)
        bld.op("vector", lambda e: e.tensor_copy(out=bv_bc[:], in_=pb[0][:]), s_ve)

        lvlA = ExitStack()
        wqb = lvlA.enter_context(nc.sbuf_tensor("wqb", [128, NC_DM * DM], BF16))
        wkb = lvlA.enter_context(nc.sbuf_tensor("wkb", [128, NC_DM * 512], BF16))
        wvb = lvlA.enter_context(nc.sbuf_tensor("wvb", [128, NC_DM * 512], BF16))
        xa_f = [lvlA.enter_context(nc.sbuf_tensor(f"xa_f{i}", [128, DM], F32)) for i in range(2)]
        xa_b = lvlA.enter_context(nc.sbuf_tensor("xa_b", [128, DM], BF16))
        xf = lvlA.enter_context(nc.sbuf_tensor("xf", [128, NC_DM * TILE], BF16))
        ev_b = [lvlA.enter_context(nc.sbuf_tensor(f"ev_b{i}", [128, 512], BF16)) for i in range(2)]
        sq_b = lvlA.enter_context(nc.sbuf_tensor("sq_b", [128, 512], BF16))
        xf_view = xf.rearrange("p (c f) -> p c f", c=NC_DM)
        pbf_view = pbf.rearrange("p (j f) -> p j f", j=4)

        for c in range(NC_DM):
            bld.dma("gpsimd", wqb[:, c * DM:(c + 1) * DM], wq[c * 128:(c + 1) * 128, :], s_ld3)
            bld.dma("gpsimd", wkb[:, c * 512:(c + 1) * 512], wk[c * 128:(c + 1) * 128, :], s_ld3)
            bld.dma("gpsimd", wvb[:, c * 512:(c + 1) * 512], wv[c * 128:(c + 1) * 128, :], s_ld3)
        tgt_w = bld.val(s_ld3)
        bld.wait("tensor", s_ld3, tgt_w)

        def emit_x_sub(src_rows, i_sub, buf_i):
            bld.wait("sync", s_ve)
            t_l = bld.dma("sync", xa_f[buf_i][:], src_rows, s_ld)
            bld.wait("vector", s_ld, t_l)
            bld.wait("vector", s_te)
            bld.op("vector", lambda e, b=buf_i: e.tensor_copy(out=xa_b[:], in_=xa_f[b][:]), s_ve)
            bld.wait("tensor", s_ve)
            for cg in range(4):
                bld.wait("tensor", s_ve)
                for j in range(4):
                    c = cg * 4 + j
                    bld.op("tensor", lambda e, c=c, j=j: e.transpose(
                        pbf[:, j * 128:(j + 1) * 128], xa_b[:, c * 128:(c + 1) * 128], eyeb[:]),
                        s_te if j == 3 else None)
                bld.wait("vector", s_te)
                bld.op("vector", lambda e, cg=cg, i=i_sub: e.tensor_copy(
                    out=xf_view[:, cg * 4:(cg + 1) * 4, i * 128:(i + 1) * 128], in_=pbf_view[:]), s_ve)

        ev_i = [0]

        def proj_group(bank, emit_mms, bias_col_ap, dst, norm_col, kind):
            bld.wait("tensor", s_se)
            bld.wait("tensor", s_ve)
            emit_mms(bank)
            b_ev = ev_b[ev_i[0] % 2]
            ev_i[0] += 1
            bld.wait("scalar", s_te)
            bld.wait("scalar", s_st)
            if kind == "v":
                bld.wait("vector", s_te)
                bld.wait("vector", s_st)
                bld.op("vector", lambda e, b=bank, o=b_ev: e.tensor_tensor(
                    out=o[:], in0=b[:], in1=bv_bc[:], op=ALU.add), s_ve)
                bld.wait("sync", s_ve)
                bld.dma("sync", dst, b_ev[:], s_st)
                return
            bld.op("scalar", lambda e, b=bank, o=b_ev, ba=bias_col_ap: e.activation(
                out=o[:], in_=b[:], func=AF.Identity, bias=ba), s_se)
            bld.wait("sync", s_se)
            bld.dma("sync", dst, b_ev[:], s_st)
            bld.wait("scalar", s_te)
            bld.op("scalar", lambda e, i=b_ev: e.activation(out=sq_b[:], in_=i[:], func=AF.Square), s_se)
            bld.wait("tensor", s_se)
            bld.wait("tensor", s_ve)
            bld.op("tensor", lambda e: e.matmul(p1[0:1, :], onesb[:, 0:1], sq_b[:], start=True, stop=True), s_te)
            bld.wait("vector", s_te)
            if kind == "q":
                bld.op("vector", lambda e, col=norm_col: e.tensor_reduce(
                    out=qmax2[0:1, col:col + 1], in_=p1[0:1, :], axis=AX.X, op=ALU.max), s_ve)
            else:
                bld.op("vector", lambda e: e.tensor_reduce(
                    out=st1[0:1, 0:1], in_=p1[0:1, :], axis=AX.X, op=ALU.max), s_ve)
                bld.wait("vector", s_ve)
                bld.op("vector", lambda e, col=norm_col: e.tensor_tensor(
                    out=kmax2[0:1, col:col + 1], in0=kmax2[0:1, col:col + 1], in1=st1[0:1, 0:1],
                    op=ALU.max), s_ve)

        rot = [0]

        def next_bank():
            b = pb[rot[0] % 4]
            rot[0] += 1
            return b

        for st in range(NST_ALL):
            for i in range(SUB):
                emit_x_sub(x_all[(st * TILE + i * 128):(st * TILE + (i + 1) * 128), :], i, i % 2)
            for kv in range(NKV):
                def mm_k(bank, kv=kv):
                    for c in range(NC_DM):
                        bld.op("tensor", lambda e, b=bank, c=c, kv=kv: e.matmul(
                            b[:], wkb[:, c * 512 + kv * 128: c * 512 + (kv + 1) * 128], xf_view[:, c, :],
                            start=(c == 0), stop=(c == NC_DM - 1)), s_te if c == NC_DM - 1 else None)
                proj_group(next_bank(), mm_k, bk_t[:, kv:kv + 1], k_scr[kv, st], kv, "k")
            for i in range(SUB):
                def mm_v(bank, i=i):
                    for c in range(NC_DM):
                        bld.op("tensor", lambda e, b=bank, c=c, i=i: e.matmul(
                            b[:], xf_view[:, c, i * 128:(i + 1) * 128], wvb[:, c * 512:(c + 1) * 512],
                            start=(c == 0), stop=(c == NC_DM - 1)), s_te if c == NC_DM - 1 else None)
                proj_group(next_bank(), mm_v, None, v_scr[st * SUB + i], None, "v")

        for st in range(NT):
            for i in range(SUB):
                emit_x_sub(x_own[(st * TILE + i * 128):(st * TILE + (i + 1) * 128), :], i, i % 2)
            for h in range(NH):
                def mm_q(bank, h=h):
                    for c in range(NC_DM):
                        bld.op("tensor", lambda e, b=bank, c=c, h=h: e.matmul(
                            b[:], wqb[:, c * DM + h * 128: c * DM + (h + 1) * 128], xf_view[:, c, :],
                            start=(c == 0), stop=(c == NC_DM - 1)), s_te if c == NC_DM - 1 else None)
                proj_group(next_bank(), mm_q, bq_t[:, h:h + 1], q_scr[h, st], h * NT + st, "q")

        bld.wait("vector", s_ve)
        for kv in range(NKV):
            bld.op("vector", lambda e, kv=kv: e.tensor_scalar(
                out=qmax2[0:1, kv * 4 * NT:(kv + 1) * 4 * NT], in0=qmax2[0:1, kv * 4 * NT:(kv + 1) * 4 * NT],
                scalar1=kmax2[0:1, kv:kv + 1], scalar2=None, op0=ALU.mult), s_ve)
        bld.wait("scalar", s_ve)
        bld.op("scalar", lambda e: e.activation(out=strow[0:1, 0:NH * NT], in_=qmax2[0:1, :], func=AF.Ln), s_se)
        bld.wait("scalar", s_se)
        bld.op("scalar", lambda e: e.activation(out=strow[0:1, 0:NH * NT], in_=strow[0:1, 0:NH * NT],
                                                func=AF.Exp, scale=0.5), s_se)
        bld.wait("vector", s_se)
        bld.op("vector", lambda e: e.tensor_scalar(
            out=strow[0:1, 0:NH * NT], in0=strow[0:1, 0:NH * NT], scalar1=-SM, scalar2=None,
            op0=ALU.mult), s_ve)
        bld.wait("tensor", s_ve)
        bld.op("tensor", lambda e: e.matmul(pb[4][:, 0:NH * NT], onesf[0:1, :], strow[0:1, 0:NH * NT],
                                            start=True, stop=True), s_te)
        bld.wait("vector", s_te)
        bld.op("vector", lambda e: e.tensor_copy(out=negB[:], in_=pb[4][:, 0:NH * NT]), s_ve)

        bld.wait("sync", s_st)
        bld.wait("sync", s_ld)
        bld.wait("sync", s_ld2)
        bld.wait("sync", s_ld3)
        bld.barrier(s_ph)
        lvlA.close()
        lvlA = ExitStack()

        wo_stage = lvlA.enter_context(nc.sbuf_tensor("wo_stage", [128, DM], F32))
        wo_bf = lvlA.enter_context(nc.sbuf_tensor("wo_bf", [128, DM], BF16))
        for dm in range(NC_DM):
            bld.wait("scalar", s_ve)
            t_l = 0
            for hd in range(NC_DM):
                t_l = bld.dma("scalar", wo_stage[:, hd * 128:(hd + 1) * 128],
                              wo[hd * 128:(hd + 1) * 128, dm * 128:(dm + 1) * 128], s_ld2)
            bld.wait("vector", s_ld2, t_l)
            bld.wait("vector", s_st)
            bld.op("vector", lambda e: e.tensor_copy(out=wo_bf[:], in_=wo_stage[:]), s_ve)
            bld.wait("sync", s_ve)
            bld.dma("sync", woB_scr[dm], wo_bf[:], s_st)

        st_f = [lvlA.enter_context(nc.sbuf_tensor(f"st_f{i}", [128, DM], F32)) for i in range(2)]
        ab_b = lvlA.enter_context(nc.sbuf_tensor("ab_b", [128, DM], BF16))

        def mat_pieces(wsrc):
            n_rows, width = wsrc.shape
            for si in range(n_rows // 128):
                for qj in range(width // DM):
                    yield wsrc[si * 128:(si + 1) * 128, qj * DM:(qj + 1) * DM], si, qj

        mats = [("g", wg, FF * DM, sw_g, isw_g),
                ("u", wu, FF * DM, sw_u, isw_u),
                ("d", wd, DM * FF, sw_d, isw_d)]

        for tag, wsrc, nel, sw_pp, isw_pp in mats:
            pieces = list(mat_pieces(wsrc))
            for pi, (rows, si, qj) in enumerate(pieces):
                bld.wait("sync", s_se)
                t_l = bld.dma("sync", st_f[pi % 2][:], rows, s_ld)
                bld.wait("scalar", s_ld, t_l)
                bld.wait("scalar", s_te)
                bld.op("scalar", lambda e, b=pi % 2: e.activation(
                    out=ab_b[:], in_=st_f[b][:], func=AF.Abs), s_se)
                bld.wait("tensor", s_se)
                for q in range(4):
                    fq = (pi == 0 and q == 0)
                    lq = (pi == len(pieces) - 1 and q == 3)
                    bld.op("tensor", lambda e, q=q, f=fq, l=lq: e.matmul(
                        p1[0:1, :], onesb[:, 0:1], ab_b[:, q * 512:(q + 1) * 512],
                        start=f, stop=l), s_te if q == 3 else None)
            bld.wait("vector", s_te)
            bld.op("vector", lambda e: e.tensor_reduce(out=st1[0:1, 0:1], in_=p1[0:1, :], axis=AX.X, op=ALU.add), s_ve)
            bld.wait("vector", s_ve)
            bld.op("vector", lambda e, n=nel: e.tensor_scalar(
                out=st1[0:1, 1:2], in0=st1[0:1, 0:1], scalar1=1.0 / n, scalar2=EPS_Q,
                op0=ALU.mult, op1=ALU.max), s_ve)
            bld.wait("vector", s_ve)
            bld.op("vector", lambda e: e.reciprocal(out=st1[0:1, 2:3], in_=st1[0:1, 1:2]), s_ve)
            bld.wait("vector", s_ve)
            bld.op("vector", lambda e: e.reciprocal(out=st1[0:1, 3:4], in_=st1[0:1, 2:3]), s_ve)
            bld.wait("tensor", s_ve)
            bld.op("tensor", lambda e: e.matmul(pb[4][:, 0:1], onesf[0:1, :], st1[0:1, 2:3], start=True, stop=True))
            bld.op("tensor", lambda e: e.matmul(pb[5][:, 0:1], onesf[0:1, :], st1[0:1, 3:4], start=True, stop=True), s_te)
            bld.wait("vector", s_te)
            bld.op("vector", lambda e, d=sw_pp: e.tensor_copy(out=d[:], in_=pb[4][:, 0:1]))
            bld.op("vector", lambda e, d=isw_pp: e.tensor_copy(out=d[:], in_=pb[5][:, 0:1]), s_ve)
            bld.wait("tensor", s_ve)

        v_f = lvlA.enter_context(nc.sbuf_tensor("v_f", [128, DM], F32))
        a_f = lvlA.enter_context(nc.sbuf_tensor("a_f", [128, DM], F32))
        t_bt = lvlA.enter_context(nc.sbuf_tensor("t_bt", [128, DM], BF16))
        tT_b = lvlA.enter_context(nc.sbuf_tensor("tT_b", [128, DM], BF16))
        guard_sw = bld.val(s_ve)
        bld.wait("vector", s_ve, guard_sw)

        def tern_piece(rows, sw_pp, dst, buf_i):
            bld.wait("sync", s_ve)
            t_l = bld.dma("sync", st_f[buf_i][:], rows, s_ld)
            bld.wait("vector", s_ld, t_l)
            bld.op("vector", lambda e, b=buf_i, s=sw_pp: e.tensor_scalar(
                out=v_f[:], in0=st_f[b][:], scalar1=s[:], scalar2=None, op0=ALU.mult))
            bld.op("vector", lambda e: e.tensor_scalar(
                out=a_f[:], in0=v_f[:], scalar1=-0.5, scalar2=None, op0=ALU.is_le))
            bld.wait("vector", s_te)
            bld.op("vector", lambda e: e.scalar_tensor_tensor(
                out=t_bt[:], in0=v_f[:], scalar=0.5, in1=a_f[:],
                op0=ALU.is_ge, op1=ALU.subtract), s_ve)
            bld.wait("tensor", s_ve)
            for cg in range(4):
                bld.wait("tensor", s_ve)
                for j in range(4):
                    c = cg * 4 + j
                    bld.op("tensor", lambda e, c=c, j=j: e.transpose(
                        pbf[:, j * 128:(j + 1) * 128], t_bt[:, c * 128:(c + 1) * 128], eyeb[:]),
                        s_te if j == 3 else None)
                bld.wait("vector", s_te)
                bld.wait("vector", s_st)
                bld.op("vector", lambda e, cg=cg: e.tensor_copy(
                    out=tT_b[:, cg * 512:(cg + 1) * 512], in_=pbf[:]), s_ve)
            bld.wait("sync", s_ve)
            bld.dma("sync", dst, tT_b[:], s_st)

        for rows, si, qj in mat_pieces(wg):
            tern_piece(rows, sw_g, wgT_scr[si], (si) % 2)
        for rows, si, qj in mat_pieces(wu):
            tern_piece(rows, sw_u, wuT_scr[si], (si) % 2)
        for rows, si, qj in mat_pieces(wd):
            tern_piece(rows, sw_d, wdT_scr[si].rearrange("p (q f) -> p q f", q=4)[:, qj, :], (si * 4 + qj) % 2)

        t_all_st = bld.val(s_st)
        for eng in Bld.ENGS:
            bld.wait(eng, s_st, t_all_st)
        lvlA.close()
        bld.barrier(s_ph)

        if debug_phase == "A":
            dbg_t = sbuf("dbg_t", 256, F32)
            bld.op("vector", lambda e: e.memset(dbg_t[:], 0.0))
            bld.op("vector", lambda e: e.tensor_copy(out=dbg_t[0:1, 128:192], in_=qmax2[0:1, :]))
            bld.op("vector", lambda e: e.tensor_copy(out=dbg_t[0:1, 192:196], in_=kmax2[0:1, :]))
            bld.op("vector", lambda e: e.tensor_copy(out=dbg_t[:, 0:NH * NT], in_=negB[:]))
            bld.op("vector", lambda e: e.tensor_copy(out=dbg_t[:, 64:65], in_=sw_g[:]))
            bld.op("vector", lambda e: e.tensor_copy(out=dbg_t[:, 65:66], in_=sw_u[:]))
            bld.op("vector", lambda e: e.tensor_copy(out=dbg_t[:, 66:67], in_=sw_d[:]))
            bld.op("vector", lambda e: e.tensor_copy(out=dbg_t[:, 67:68], in_=isw_d[:]))
            bld.op("vector", lambda e: e.tensor_copy(out=dbg_t[:, 68:69], in_=bv_bc[:, 3:4]), s_ve)
            bld.wait("sync", s_ve)
            bld.dma("sync", dbgA, dbg_t[:], s_st)
            bld.wait("sync", s_st)
            z_t = sbuf("z_t", DM, F32)
            bld.op("vector", lambda e: e.memset(z_t[:], 0.0), s_ve)
            bld.wait("sync", s_ve)
            for i in range(TOK // 128):
                bld.dma("sync", out[i * 128:(i + 1) * 128, :], z_t[:], s_st)
            bld.wait("sync", s_st)
            block = stack.enter_context(nc.Block())
            bld.run(block)
            return nc

        build_phase_b(nc, stack, bld, dict(
            n_tiles=(1 if debug_phase == "B1" else NT),
            dump_tt=(0 if debug_phase == "B1" else (NT - 1 if debug_phase == "B4" else (-2 if debug_phase == "BX" else -1))),
            dbgF=dbgF, dbgS=dbgS,
            out=out, x_own=x_own, q_scr=q_scr, k_scr=k_scr, v_scr=v_scr,
            wgT_scr=wgT_scr, wuT_scr=wuT_scr, wdT_scr=wdT_scr, woB_scr=woB_scr,
            s1_scr=s1_scr,
            h128=h128, eyeb=eyeb, eyef=eyef, onesb=onesb, onesf=onesf,
            bo_t=bo_t, negB=negB, eps_t=eps_t, neg1_t=neg1_t,
            isw_g=isw_g, isw_u=isw_u, isw_d=isw_d,
            strow=strow, st1=st1, pb=pb, pbf=pbf, p1=p1,
            s_ld=s_ld, s_ld2=s_ld2, s_ld3=s_ld3, s_st=s_st,
            s_te=s_te, s_ve=s_ve, s_se=s_se, s_ph=s_ph,
        ))
        block = stack.enter_context(nc.Block())
        bld.run(block)
    return nc


def build_phase_b(nc, stack, bld, env):
    g = env
    out, x_own = g["out"], g["x_own"]
    q_scr, k_scr, v_scr = g["q_scr"], g["k_scr"], g["v_scr"]
    wgT_scr, wuT_scr, wdT_scr, woB_scr = g["wgT_scr"], g["wuT_scr"], g["wdT_scr"], g["woB_scr"]
    s1_scr = g["s1_scr"]
    h128, eyeb, eyef, onesb, onesf = g["h128"], g["eyeb"], g["eyef"], g["onesb"], g["onesf"]
    bo_t, negB, eps_t, neg1_t = g["bo_t"], g["negB"], g["eps_t"], g["neg1_t"]
    isw_g, isw_u, isw_d = g["isw_g"], g["isw_u"], g["isw_d"]
    strow, st1, pb, pbf, p1 = g["strow"], g["st1"], g["pb"], g["pbf"], g["p1"]
    s_ld, s_ld2, s_ld3, s_st = g["s_ld"], g["s_ld2"], g["s_ld3"], g["s_st"]
    s_te, s_ve, s_se, s_ph = g["s_te"], g["s_ve"], g["s_se"], g["s_ph"]

    s_sc = bld.sem("s_sc")
    s_ex = bld.sem("s_ex")
    s_pv = bld.sem("s_pv")
    s_sm = bld.sem("s_sm")
    s_pp = bld.sem("s_pp")

    def sbufB(st_, name, cols, dt=BF16, rows=128):
        return st_.enter_context(nc.sbuf_tensor(name, [rows, cols], dt))

    lvlB = ExitStack()
    xres = [sbufB(lvlB, f"xres{i}", DM, F32) for i in range(2)]
    sq_f = sbufB(lvlB, "sq_f", DM, F32)
    br_sub = sbufB(lvlB, "br_sub", DM, BF16)
    stt = sbufB(lvlB, "stt", 32, F32)
    inv1 = sbufB(lvlB, "inv1", SUB, F32)
    cs2 = sbufB(lvlB, "cs2", SUB, F32)
    s2c = sbufB(lvlB, "s2c", SUB, F32)
    inv1B = sbufB(lvlB, "inv1B", 512, F32)
    s2B = sbufB(lvlB, "s2B", 512, F32)
    cs2B = sbufB(lvlB, "cs2B", 512, F32)

    C_SX, C_SX2, C_MEAN, C_EX2, C_M2T, C_VAR, C_LNV, C_RSTD, C_NMR = range(9)
    C_MRAW, C_M, C_RS, C_SACT = 9, 10, 11, 12

    def col(c):
        return stt[:, c:c + 1]

    def pp(eng, fn):
        bld.wait(eng, s_pp)
        return bld.op(eng, fn, s_pp)

    dbgF, dbgS = g.get("dbgF"), g.get("dbgS")
    dump_tt = g.get("dump_tt", -1)
    global DUMP_MASK
    cur_tt = [0]

    def spoint(k):
        if k not in SERIAL_POINTS:
            return
        for eng in ("vector", "scalar", "tensor"):
            sem = {"vector": s_ve, "scalar": s_se, "tensor": s_te}[eng]
            bld.op(eng, lambda e: e.nop(), sem)
        for eng in ("vector", "scalar", "tensor", "sync"):
            bld.wait(eng, s_ve)
            bld.wait(eng, s_se)
            bld.wait(eng, s_te)

    def _dump_on(k):
        return dbgF is not None and (cur_tt[0] == dump_tt or dump_tt == -2) and k in DUMP_MASK

    def dump_slots(k, pos_list):
        if not _dump_on(k):
            return
        for eng in ("vector", "scalar", "tensor"):
            pass
        bld.wait("sync", s_ve)
        bld.wait("sync", s_se)
        bld.wait("sync", s_te)
        for oc in range(NC_FF):
            bld.dma("sync", dbgF[k][:, oc * 512:(oc + 1) * 512],
                    Xs_cur[0][:, pos_list[oc] * 512:(pos_list[oc] + 1) * 512], s_st)
        bld.wait("sync", s_st)
        for eng in ("vector", "scalar", "tensor"):
            bld.wait(eng, s_st)

    def dump_tile(k, ap_src, width):
        if not _dump_on(k):
            return
        bld.wait("sync", s_ve)
        bld.wait("sync", s_se)
        bld.wait("sync", s_te)
        bld.dma("sync", dbgF[k][:, 0:width], ap_src, s_st)
        bld.wait("sync", s_st)
        for eng in ("vector", "scalar", "tensor"):
            bld.wait(eng, s_st)

    Xs_cur = [None]

    def ln_block(raw_slice, dst_slice, sub_i):
        bld.wait("scalar", s_ve)
        bld.op("scalar", lambda e: e.activation(out=xres[sub_i % 2][:], in_=raw_slice, func=AF.Square), s_se)
        bld.op("vector", lambda e: e.tensor_reduce(out=col(C_SX), in_=raw_slice, axis=AX.X, op=ALU.add), s_ve)
        bld.wait("vector", s_se)
        bld.op("vector", lambda e: e.tensor_reduce(out=col(C_SX2), in_=xres[sub_i % 2][:], axis=AX.X, op=ALU.add), s_ve)
        pp("vector", lambda e: e.tensor_scalar(out=col(C_MEAN), in0=col(C_SX), scalar1=1.0 / DM, scalar2=None, op0=ALU.mult))
        pp("vector", lambda e: e.tensor_scalar(out=col(C_EX2), in0=col(C_SX2), scalar1=1.0 / DM, scalar2=None, op0=ALU.mult))
        pp("vector", lambda e: e.tensor_scalar(out=col(C_M2T), in0=col(C_MEAN), scalar1=col(C_MEAN), scalar2=None, op0=ALU.mult))
        pp("vector", lambda e: e.tensor_tensor(out=col(C_VAR), in0=col(C_EX2), in1=col(C_M2T), op=ALU.subtract))
        bld.wait("scalar", s_pp)
        pp("scalar", lambda e: e.activation(out=col(C_LNV), in_=col(C_VAR), func=AF.Ln, bias=eps_t[:]))
        pp("scalar", lambda e: e.activation(out=col(C_RSTD), in_=col(C_LNV), func=AF.Exp, scale=-0.5))
        bld.wait("vector", s_pp)
        pp("vector", lambda e: e.tensor_scalar(out=col(C_NMR), in0=col(C_RSTD), scalar1=col(C_MEAN), scalar2=-1.0,
                                               op0=ALU.mult, op1=ALU.mult))
        bld.wait("scalar", s_pp)
        bld.op("scalar", lambda e: e.activation(out=dst_slice, in_=raw_slice, func=AF.Identity,
                                                scale=col(C_RSTD), bias=col(C_NMR)), s_se)

    for TT in range(g.get("n_tiles", NT)):
        cur_tt[0] = TT
        lvlAt = ExitStack()
        stream1 = sbufB(lvlAt, f"stream1_{TT}", SUB * DM, F32)
        v_sb = sbufB(lvlAt, f"v_sb_{TT}", KT * 512)
        k_sb = sbufB(lvlAt, f"k_sb_{TT}", NKV * S)
        q_sb = sbufB(lvlAt, f"q_sb_{TT}", NH * 512)
        pT = sbufB(lvlAt, f"pT_{TT}", 2 * 512)
        sumC = sbufB(lvlAt, f"sumC_{TT}", 512)
        OT = sbufB(lvlAt, f"OT_{TT}", NH * 512)
        wos = sbufB(lvlAt, f"wos_{TT}", 2 * DM)
        brF = sbufB(lvlAt, f"brF_{TT}", NC_DM * 512)
        recipB = sbufB(lvlAt, f"recipB_{TT}", 512, F32)

        for kt in range(KT):
            bld.dma("sync", v_sb[:, kt * 512:(kt + 1) * 512], v_scr[kt], s_ld)
        for kv in range(NKV):
            for st in range(NST_ALL):
                bld.dma("scalar", k_sb[:, kv * S + st * 512: kv * S + (st + 1) * 512], k_scr[kv, st], s_ld2)
        for h in range(NH):
            bld.dma("gpsimd", q_sb[:, h * 512:(h + 1) * 512], q_scr[h, TT], s_ld3)
        bld.wait("tensor", s_ld)
        bld.wait("tensor", s_ld2)
        bld.wait("tensor", s_ld3)

        for kv in range(NKV):
            for gq in range(4):
                h = kv * 4 + gq
                ncol = h * NT + TT
                bld.wait("tensor", s_se)
                bld.wait("tensor", s_ve)
                bld.wait("vector", s_te)
                bld.wait("scalar", s_pv)
                bld.wait("scalar", s_sm)
                sc_t = [None] * KT
                ex_t = [None] * KT
                pv_t = [None] * KT
                sm_t = [None] * KT

                def mm_sc(kt, kv=kv, h=h):
                    sc_t[kt] = bld.op("tensor", lambda e, kt=kt: e.matmul(
                        pb[kt % 2][:], k_sb[:, kv * S + kt * 128: kv * S + (kt + 1) * 128],
                        q_sb[:, h * 512:(h + 1) * 512], start=True, stop=True,
                        skip_group_check=True), s_sc)

                mm_sc(0)
                mm_sc(1)
                for kt in range(KT):
                    bld.wait("scalar", s_sc, sc_t[kt])
                    if kt >= 2:
                        bld.wait("scalar", s_pv, pv_t[kt - 2])
                        bld.wait("scalar", s_sm, sm_t[kt - 2])
                    ex_t[kt] = bld.op("scalar", lambda e, kt=kt, ncol=ncol: e.activation(
                        out=pT[:, (kt % 2) * 512:(kt % 2 + 1) * 512], in_=pb[kt % 2][:],
                        func=AF.Exp, scale=SM, bias=negB[:, ncol:ncol + 1]), s_ex)
                    bld.wait("tensor", s_ex, ex_t[kt])
                    pv_t[kt] = bld.op("tensor", lambda e, kt=kt, kv=kv: e.matmul(
                        pb[2][:], v_sb[:, kt * 512 + kv * 128: kt * 512 + (kv + 1) * 128],
                        pT[:, (kt % 2) * 512:(kt % 2 + 1) * 512],
                        start=(kt == 0), stop=(kt == KT - 1), skip_group_check=True), s_pv)
                    if kt + 2 < KT:
                        mm_sc(kt + 2)
                    bld.wait("vector", s_ex, ex_t[kt])
                    if kt == 0:
                        sm_t[kt] = bld.op("vector", lambda e: e.tensor_copy(
                            out=sumC[:], in_=pT[:, 0:512]), s_sm)
                    else:
                        sm_t[kt] = bld.op("vector", lambda e, kt=kt: e.tensor_tensor(
                            out=sumC[:], in0=sumC[:], in1=pT[:, (kt % 2) * 512:(kt % 2 + 1) * 512],
                            op=ALU.add), s_sm)
                bld.wait("tensor", s_sm, sm_t[KT - 1])
                bld.wait("tensor", s_ve)
                t_ones = bld.op("tensor", lambda e: e.matmul(
                    p1[0:1, :], onesb[:, 0:1], sumC[:], start=True, stop=True), s_te)
                bld.wait("vector", s_te, t_ones)
                t_rc = bld.op("vector", lambda e: e.reciprocal(out=strow[0:1, :], in_=p1[0:1, :]), s_ve)
                bld.wait("tensor", s_ve, t_rc)
                t_bc = bld.op("tensor", lambda e: e.matmul(
                    pb[4][:], onesf[0:1, :], strow[0:1, :], start=True, stop=True), s_te)
                bld.wait("vector", s_te, t_bc)
                bld.op("vector", lambda e: e.tensor_copy(out=recipB[:], in_=pb[4][:]), s_ve)
                bld.wait("vector", s_pv, pv_t[KT - 1])
                bld.op("vector", lambda e, h=h: e.tensor_tensor(
                    out=OT[:, h * 512:(h + 1) * 512], in0=pb[2][:], in1=recipB[:], op=ALU.mult), s_ve)

        mm_t = [None] * NC_DM
        ev_t = [None] * NC_DM
        bld.wait("sync", s_ve)
        for dm in range(NC_DM):
            if dm >= 2:
                bld.wait("sync", s_te, mm_t[dm - 2])
            ld_t = bld.dma("sync", wos[:, (dm % 2) * DM:(dm % 2 + 1) * DM], woB_scr[dm], s_ld)
            bld.wait("tensor", s_ld, ld_t)
            if dm >= 2:
                bld.wait("tensor", s_se, ev_t[dm - 2])
            for hd in range(NC_DM):
                tv = bld.op("tensor", lambda e, dm=dm, hd=hd: e.matmul(
                    pb[5 if dm % 2 == 0 else 3][:],
                    wos[:, (dm % 2) * DM + hd * 128:(dm % 2) * DM + (hd + 1) * 128],
                    OT[:, hd * 512:(hd + 1) * 512],
                    start=(hd == 0), stop=(hd == NC_DM - 1), skip_group_check=True),
                    s_te if hd == NC_DM - 1 else None)
                if hd == NC_DM - 1:
                    mm_t[dm] = tv
            bld.wait("scalar", s_te, mm_t[dm])
            ev_t[dm] = bld.op("scalar", lambda e, dm=dm: e.activation(
                out=brF[:, dm * 512:(dm + 1) * 512], in_=pb[5 if dm % 2 == 0 else 3][:],
                func=AF.Identity, bias=bo_t[:, dm:dm + 1]), s_se)

        for i in range(SUB):
            bld.wait("sync", s_ve)
            t_x = bld.dma("sync", xres[i % 2][:], x_own[(TT * TILE + i * 128):(TT * TILE + (i + 1) * 128), :], s_ld)
            bld.wait("tensor", s_se)
            for cg in range(4):
                bld.wait("tensor", s_ve)
                for j in range(4):
                    dm = cg * 4 + j
                    bld.op("tensor", lambda e, dm=dm, j=j, i=i: e.transpose(
                        pbf[:, j * 128:(j + 1) * 128],
                        brF[:, dm * 512 + i * 128: dm * 512 + (i + 1) * 128], eyeb[:]),
                        s_te if j == 3 else None)
                bld.wait("vector", s_te)
                bld.op("vector", lambda e, cg=cg: e.tensor_copy(
                    out=br_sub.rearrange("p (c f) -> p c f", c=NC_DM)[:, cg * 4:(cg + 1) * 4, :],
                    in_=pbf.rearrange("p (j f) -> p j f", j=4)[:]), s_ve)
            bld.wait("vector", s_ld, t_x)
            bld.wait("vector", s_se)
            bld.op("vector", lambda e, i=i: e.tensor_tensor(
                out=sq_f[:], in0=xres[i % 2][:], in1=br_sub[:], op=ALU.add), s_ve)
            ln_block(sq_f[:], stream1[:, i * DM:(i + 1) * DM], i)
            bld.wait("sync", s_se)
            bld.dma("sync", s1_scr[i], stream1[:, i * DM:(i + 1) * DM], s_st)

        bld.wait("sync", s_st)
        bld.barrier(s_ph)
        lvlAt.close()

        lvlF = ExitStack()
        Xs = sbufB(lvlF, f"Xs_{TT}", 66 * 512)
        Xs_cur[0] = Xs
        xq_f = sbufB(lvlF, f"xq_f_{TT}", NC_DM * 512)
        arena = sbufB(lvlF, f"arena_{TT}", 4 * 2048)
        dslab2 = sbufB(lvlF, f"dslab2_{TT}", FF)
        k0i_t = sbufB(lvlF, f"k0i_t_{TT}", DM, I32)
        s1f = sbufB(lvlF, f"s1f_{TT}", DM, F32)
        uq = sbufB(lvlF, f"uq_{TT}", 512, F32)
        k0q = sbufB(lvlF, f"k0q_{TT}", 512, I32)
        dq = sbufB(lvlF, f"dq_{TT}", 512, F32)
        gs = sbufB(lvlF, f"gs_{TT}", 2 * 512)
        tu = sbufB(lvlF, f"tu_{TT}", 2 * 512, F32)
        maxC = sbufB(lvlF, f"maxC_{TT}", 512, F32)
        brD = sbufB(lvlF, f"brD_{TT}", NC_DM * 512)

        for i in range(SUB):
            bld.wait("sync", s_ve)
            t_s1 = bld.dma("sync", s1f[:], s1_scr[i], s_ld)
            bld.wait("vector", s_ld, t_s1)
            pp("vector", lambda e: e.tensor_reduce(
                out=col(C_MRAW), in_=s1f[:], axis=AX.X, op=ALU.max, apply_absolute_value=True))
            pp("vector", lambda e: e.tensor_scalar(out=col(C_M), in0=col(C_MRAW), scalar1=EPS_Q,
                                                   scalar2=None, op0=ALU.max))
            pp("vector", lambda e: e.reciprocal(out=col(C_RS), in_=col(C_M)))
            pp("vector", lambda e: e.tensor_scalar(out=col(C_SACT), in0=col(C_RS), scalar1=128.0,
                                                   scalar2=None, op0=ALU.mult))
            pp("vector", lambda e, i=i: e.tensor_scalar(out=inv1[:, i:i + 1], in0=col(C_M),
                                                        scalar1=1.0 / 128.0, scalar2=None, op0=ALU.mult))
            bld.wait("vector", s_te)
            bld.op("vector", lambda e: e.tensor_scalar(
                out=sq_f[:], in0=s1f[:], scalar1=col(C_SACT), scalar2=None, op0=ALU.mult))
            bld.op("vector", lambda e: e.tensor_copy(out=k0i_t[:], in_=sq_f[:]))
            bld.op("vector", lambda e, i=i: e.tensor_tensor(
                out=xres[i % 2][:], in0=sq_f[:], in1=k0i_t[:], op=ALU.subtract))
            bld.op("vector", lambda e, i=i: e.scalar_tensor_tensor(
                out=sq_f[:], in0=xres[i % 2][:], scalar=0.5, in1=k0i_t[:], op0=ALU.is_gt, op1=ALU.add))
            bld.op("vector", lambda e, i=i: e.scalar_tensor_tensor(
                out=br_sub[:], in0=xres[i % 2][:], scalar=-0.5, in1=sq_f[:], op0=ALU.is_ge, op1=ALU.add))
            bld.op("vector", lambda e: e.tensor_scalar(
                out=sq_f[:], in0=br_sub[:], scalar1=128.0, scalar2=-127.0, op0=ALU.min, op1=ALU.max), s_ve)
            bld.wait("scalar", s_ve)
            bld.op("scalar", lambda e: e.activation(out=br_sub[:], in_=sq_f[:], func=AF.Identity,
                                                    bias=neg1_t[:]), s_se)
            bld.wait("tensor", s_se)
            for cg in range(4):
                bld.wait("tensor", s_ve)
                for j in range(4):
                    c = cg * 4 + j
                    bld.op("tensor", lambda e, c=c, j=j: e.transpose(
                        pbf[:, j * 128:(j + 1) * 128], br_sub[:, c * 128:(c + 1) * 128], eyeb[:]),
                        s_te if j == 3 else None)
                bld.wait("vector", s_te)
                bld.op("vector", lambda e, cg=cg, i=i: e.tensor_copy(
                    out=xq_f.rearrange("p (c f) -> p c f", c=NC_DM)[:, cg * 4:(cg + 1) * 4, i * 128:(i + 1) * 128],
                    in_=pbf.rearrange("p (j f) -> p j f", j=4)[:]), s_ve)

        spoint(0)
        dump_tile(0, xq_f[:], NC_DM * 512)
        bld.wait("tensor", s_pp)
        bld.wait("tensor", s_ve)
        for i in range(SUB):
            bld.op("tensor", lambda e, i=i: e.transpose(
                p1[0:1, i * 128:(i + 1) * 128], inv1[:, i:i + 1], eyef[:]),
                s_te if i == SUB - 1 else None)
        bld.wait("vector", s_te)
        t_cp = bld.op("vector", lambda e: e.tensor_copy(out=strow[0:1, :], in_=p1[0:1, :]), s_ve)
        bld.wait("tensor", s_ve, t_cp)
        t_bc = bld.op("tensor", lambda e: e.matmul(pb[4][:], onesf[0:1, :], strow[0:1, :],
                                                   start=True, stop=True), s_te)
        bld.wait("vector", s_te, t_bc)
        t_i1 = bld.op("vector", lambda e: e.tensor_copy(out=inv1B[:], in_=pb[4][:]), s_ve)
        bld.wait("tensor", s_ve, t_i1)

        gmm = [None] * NC_FF
        umm = [None] * NC_FF
        gev = [None] * NC_FF
        uev = [None] * NC_FF
        xev = [None] * NC_FF
        bld.wait("vector", s_ve)
        for oc in range(NC_FF):
            sl = oc % 2
            if oc >= 2:
                bld.wait("sync", s_te, umm[oc - 2])
            bld.dma("sync", arena[:, sl * 2048:(sl + 1) * 2048], wgT_scr[oc], s_ld)
            t_u = bld.dma("sync", arena[:, (2 + sl) * 2048:(3 + sl) * 2048], wuT_scr[oc], s_ld)
            bld.wait("tensor", s_ld, t_u)
            if oc >= 2:
                bld.wait("tensor", s_ve, xev[oc - 2])
            for hc in range(NC_DM):
                tv = bld.op("tensor", lambda e, hc=hc, sl=sl: e.matmul(
                    pb[sl][:], arena[:, sl * 2048 + hc * 128: sl * 2048 + (hc + 1) * 128],
                    xq_f[:, hc * 512:(hc + 1) * 512],
                    start=(hc == 0), stop=(hc == NC_DM - 1), skip_group_check=True),
                    s_te if hc == NC_DM - 1 else None)
                if hc == NC_DM - 1:
                    gmm[oc] = tv
            for hc in range(NC_DM):
                tv = bld.op("tensor", lambda e, hc=hc, sl=sl: e.matmul(
                    pb[2 + sl][:], arena[:, (2 + sl) * 2048 + hc * 128: (2 + sl) * 2048 + (hc + 1) * 128],
                    xq_f[:, hc * 512:(hc + 1) * 512],
                    start=(hc == 0), stop=(hc == NC_DM - 1), skip_group_check=True),
                    s_te if hc == NC_DM - 1 else None)
                if hc == NC_DM - 1:
                    umm[oc] = tv
            bld.wait("vector", s_te, gmm[oc])
            bld.wait("vector", s_se)
            bld.op("vector", lambda e, sl=sl: e.scalar_tensor_tensor(
                out=tu[:, sl * 512:(sl + 1) * 512], in0=pb[sl][:], scalar=isw_g[:], in1=inv1B[:],
                op0=ALU.mult, op1=ALU.mult), s_ve)
            bld.wait("scalar", s_ve)
            gev[oc] = bld.op("scalar", lambda e, sl=sl: e.activation(
                out=gs[:, sl * 512:(sl + 1) * 512], in_=tu[:, sl * 512:(sl + 1) * 512], func=AF.Silu), s_se)
            bld.wait("vector", s_te, umm[oc])
            bld.wait("vector", s_se, gev[oc])
            uev[oc] = bld.op("vector", lambda e, sl=sl: e.scalar_tensor_tensor(
                out=tu[:, sl * 512:(sl + 1) * 512], in0=pb[2 + sl][:], scalar=isw_u[:], in1=inv1B[:],
                op0=ALU.mult, op1=ALU.mult), s_ve)
            xev[oc] = bld.op("vector", lambda e, oc=oc, sl=sl: e.tensor_tensor(
                out=Xs[:, oc * 512:(oc + 1) * 512], in0=gs[:, sl * 512:(sl + 1) * 512],
                in1=tu[:, sl * 512:(sl + 1) * 512], op=ALU.mult), s_ve)

        spoint(1)
        dump_slots(1, list(range(NC_FF)))
        sa_ev = [None] * NC_FF
        for oc in range(NC_FF):
            bank = 4 + (oc % 2)
            if oc >= 2:
                bld.wait("tensor", s_ve, sa_ev[oc - 2])
            bld.wait("tensor", s_ve, xev[oc])
            bld.op("tensor", lambda e, oc=oc, bank=bank: e.matmul(
                pb[bank][:], h128[:], Xs[:, oc * 512:(oc + 1) * 512],
                start=True, stop=True, skip_group_check=True), s_te)
            bld.wait("vector", s_te)
            sa_ev[oc] = bld.op("vector", lambda e, oc=oc, bank=bank: e.tensor_copy(
                out=Xs[:, oc * 512:(oc + 1) * 512], in_=pb[bank][:]), s_ve)

        spoint(2)
        dump_slots(2, list(range(NC_FF)))
        def slot(pos_idx):
            return Xs[:, pos_idx * 512:(pos_idx + 1) * 512]

        pos = list(range(NC_FF))
        spare = [NC_FF, NC_FF + 1]
        for sstride in (32, 16, 8, 4, 2, 1):
            for i0 in range(NC_FF):
                if i0 & sstride:
                    continue
                j0 = i0 | sstride
                pa, pbx = pos[i0], pos[j0]
                ns0, ns1 = spare
                bld.op("vector", lambda e, a=pa, b=pbx, o=ns0: e.tensor_tensor(
                    out=slot(o), in0=slot(a), in1=slot(b), op=ALU.add))
                bld.op("vector", lambda e, a=pa, b=pbx, o=ns1: e.tensor_tensor(
                    out=slot(o), in0=slot(a), in1=slot(b), op=ALU.subtract))
                pos[i0], pos[j0] = ns0, ns1
                spare = [pa, pbx]
        bld.op("vector", lambda e: e.engine_nop(), s_ve)
        spoint(3)

        if True:
            dump_slots(3, list(pos))
        bld.op("vector", lambda e: e.memset(maxC[:], 0.0))
        ab_t = [None] * NC_FF
        mx_t = [None] * NC_FF
        for oc in range(NC_FF):
            sl = oc % 2
            if oc >= 2:
                bld.wait("scalar", s_sm, mx_t[oc - 2])
            bld.wait("scalar", s_ve)
            ab_t[oc] = bld.op("scalar", lambda e, oc=oc, sl=sl: e.activation(
                out=gs[:, sl * 512:(sl + 1) * 512], in_=slot(pos[oc]), func=AF.Abs), s_se)
            bld.wait("vector", s_se, ab_t[oc])
            mx_t[oc] = bld.op("vector", lambda e, sl=sl: e.tensor_tensor(
                out=maxC[:], in0=maxC[:], in1=gs[:, sl * 512:(sl + 1) * 512], op=ALU.max), s_sm)
        bld.op("vector", lambda e: e.engine_nop(), s_ve)
        bld.wait("tensor", s_ve)
        for i in range(SUB):
            bld.op("tensor", lambda e, i=i: e.transpose(
                pb[4][:, i * 128:(i + 1) * 128], maxC[:, i * 128:(i + 1) * 128], eyef[:]),
                s_te if i == SUB - 1 else None)
        bld.wait("vector", s_te)
        for i in range(SUB):
            pp("vector", lambda e, i=i: e.tensor_reduce(
                out=col(C_MRAW), in_=pb[4][:, i * 128:(i + 1) * 128], axis=AX.X, op=ALU.max))
            pp("vector", lambda e: e.tensor_scalar(out=col(C_M), in0=col(C_MRAW), scalar1=CF,
                                                   scalar2=EPS_Q, op0=ALU.mult, op1=ALU.max))
            pp("vector", lambda e: e.reciprocal(out=col(C_RS), in_=col(C_M)))
            pp("vector", lambda e, i=i: e.tensor_scalar(out=s2c[:, i:i + 1], in0=col(C_RS), scalar1=128.0 * CF,
                                                        scalar2=None, op0=ALU.mult))
            pp("vector", lambda e: e.tensor_scalar(out=col(C_SACT), in0=col(C_M), scalar1=1.0 / 128.0,
                                                   scalar2=None, op0=ALU.mult))
            pp("vector", lambda e, i=i: e.tensor_scalar(out=cs2[:, i:i + 1], in0=col(C_SACT),
                                                        scalar1=isw_d[:], scalar2=None, op0=ALU.mult))
        bld.wait("tensor", s_pp)
        bld.wait("tensor", s_ve)
        for i in range(SUB):
            bld.op("tensor", lambda e, i=i: e.transpose(
                p1[0:1, i * 128:(i + 1) * 128], s2c[:, i:i + 1], eyef[:]))
        for i in range(SUB):
            bld.op("tensor", lambda e, i=i: e.transpose(
                pb[5][0:1, i * 128:(i + 1) * 128], cs2[:, i:i + 1], eyef[:]),
                s_te if i == SUB - 1 else None)
        bld.wait("vector", s_te)
        t_c1 = bld.op("vector", lambda e: e.tensor_copy(out=strow[0:1, :], in_=p1[0:1, :]), s_ve)
        bld.wait("tensor", s_ve, t_c1)
        t_b1 = bld.op("tensor", lambda e: e.matmul(pb[4][:], onesf[0:1, :], strow[0:1, :],
                                                   start=True, stop=True), s_te)
        bld.wait("vector", s_te, t_b1)
        bld.op("vector", lambda e: e.tensor_copy(out=s2B[:], in_=pb[4][:]))
        t_c2 = bld.op("vector", lambda e: e.tensor_copy(out=strow[0:1, :], in_=pb[5][0:1, :]), s_ve)
        bld.wait("tensor", s_ve, t_c2)
        t_b2 = bld.op("tensor", lambda e: e.matmul(pb[4][:], onesf[0:1, :], strow[0:1, :],
                                                   start=True, stop=True), s_te)
        bld.wait("vector", s_te, t_b2)
        bld.op("vector", lambda e: e.tensor_copy(out=cs2B[:], in_=pb[4][:]), s_ve)

        for oc in range(NC_FF):
            bld.op("vector", lambda e, oc=oc: e.tensor_tensor(
                out=uq[:], in0=slot(pos[oc]), in1=s2B[:], op=ALU.mult))
            bld.op("vector", lambda e: e.tensor_copy(out=k0q[:], in_=uq[:]))
            bld.op("vector", lambda e: e.tensor_tensor(out=dq[:], in0=uq[:], in1=k0q[:], op=ALU.subtract))
            bld.op("vector", lambda e: e.scalar_tensor_tensor(
                out=tu[:, 0:512], in0=dq[:], scalar=0.5, in1=k0q[:], op0=ALU.is_gt, op1=ALU.add))
            bld.op("vector", lambda e: e.scalar_tensor_tensor(
                out=uq[:], in0=dq[:], scalar=-0.5, in1=tu[:, 0:512], op0=ALU.is_ge, op1=ALU.add))
            bld.op("vector", lambda e: e.tensor_scalar(
                out=dq[:], in0=uq[:], scalar1=128.0, scalar2=-127.0, op0=ALU.min, op1=ALU.max))
            bld.op("vector", lambda e, oc=oc: e.tensor_scalar(
                out=slot(pos[oc]), in0=dq[:], scalar1=1.0, scalar2=None, op0=ALU.subtract), s_ve)

        spoint(4)
        dump_slots(4, list(pos))
        if dbgS is not None and (cur_tt[0] == dump_tt or dump_tt == -2) and 7 in DUMP_MASK:
            bld.wait("vector", s_pp)
            bld.op("vector", lambda e: e.tensor_copy(out=stt[:, 20:21], in_=isw_g[:]))
            bld.op("vector", lambda e: e.tensor_copy(out=stt[:, 21:22], in_=isw_u[:]))
            bld.op("vector", lambda e: e.tensor_copy(out=stt[:, 22:23], in_=isw_d[:]), s_ve)
            bld.wait("sync", s_ve)
            bld.dma("sync", dbgS[:, 0:4], inv1[:], s_st)
            bld.dma("sync", dbgS[:, 4:8], cs2[:], s_st)
            bld.dma("sync", dbgS[:, 8:12], s2c[:], s_st)
            bld.dma("sync", dbgS[:, 12:16], inv1B[:, 0:4], s_st)
            bld.dma("sync", dbgS[:, 16:20], s2B[:, 0:4], s_st)
            bld.dma("sync", dbgS[:, 20:24], stt[:, 20:24], s_st)
            bld.dma("sync", dbgS[:, 24:28], cs2B[:, 0:4], s_st)
            bld.wait("sync", s_st)
        spoint(5)
        dmm = [None] * NC_DM
        dev = [None] * NC_DM
        bld.wait("sync", s_te)
        bld.wait("tensor", s_ve)
        for dm in range(NC_DM):
            sl = dm % 2
            dbuf = (arena[:, 0:FF] if sl == 0 else dslab2[:, 0:FF])
            if dm >= 2:
                bld.wait("sync", s_te, dmm[dm - 2])
            dl = bld.dma("sync", dbuf, wdT_scr[dm], s_ld)
            bld.wait("tensor", s_ld, dl)
            if dm >= 2:
                bld.wait("tensor", s_ve, dev[dm - 2])
            for ic in range(NC_FF):
                tv = bld.op("tensor", lambda e, ic=ic, sl=sl, dbuf=dbuf: e.matmul(
                    pb[sl][:], dbuf[:, ic * 128:(ic + 1) * 128],
                    slot(pos[ic]), start=(ic == 0), stop=(ic == NC_FF - 1), skip_group_check=True),
                    s_te if ic == NC_FF - 1 else None)
                if ic == NC_FF - 1:
                    dmm[dm] = tv
            bld.wait("vector", s_te, dmm[dm])
            dev[dm] = bld.op("vector", lambda e, dm=dm, sl=sl: e.tensor_tensor(
                out=brD[:, dm * 512:(dm + 1) * 512], in0=pb[sl][:], in1=cs2B[:], op=ALU.mult), s_ve)

        spoint(6)
        if _dump_on(5):
            bld.wait("sync", s_ve)
            bld.dma("sync", dbgF[5][:, 0:NC_DM * 512], brD[:], s_st)
            bld.wait("sync", s_st)
            for eng in ("vector", "scalar", "tensor"):
                bld.wait(eng, s_st)
        for i in range(SUB):
            bld.wait("sync", s_ve)
            bld.wait("sync", s_st)
            bld.wait("sync", s_se)
            t_s1 = bld.dma("sync", s1f[:], s1_scr[i], s_ld)
            bld.wait("tensor", s_ve)
            for cg in range(4):
                bld.wait("tensor", s_ve)
                for j in range(4):
                    dm = cg * 4 + j
                    bld.op("tensor", lambda e, dm=dm, j=j, i=i: e.transpose(
                        pbf[:, j * 128:(j + 1) * 128],
                        brD[:, dm * 512 + i * 128: dm * 512 + (i + 1) * 128], eyeb[:]),
                        s_te if j == 3 else None)
                bld.wait("vector", s_te)
                bld.op("vector", lambda e, cg=cg: e.tensor_copy(
                    out=br_sub.rearrange("p (c f) -> p c f", c=NC_DM)[:, cg * 4:(cg + 1) * 4, :],
                    in_=pbf.rearrange("p (j f) -> p j f", j=4)[:]), s_ve)
            bld.wait("vector", s_ld, t_s1)
            bld.wait("vector", s_se)
            bld.op("vector", lambda e: e.tensor_tensor(
                out=sq_f[:], in0=s1f[:], in1=br_sub[:], op=ALU.add), s_ve)
            ln_block(sq_f[:], s1f[:], i)
            bld.wait("sync", s_se)
            bld.dma("sync", out[(TT * TILE + i * 128):(TT * TILE + (i + 1) * 128), :], s1f[:], s_st)

        bld.wait("sync", s_st)
        bld.barrier(s_ph)
        lvlF.close()

    bld.wait("sync", s_st)
    lvlB.close()


def _consts():
    h128 = _hadamard(128).astype(ml_dtypes.bfloat16)
    eyeb = np.eye(128).astype(ml_dtypes.bfloat16)
    eyef = np.eye(128, dtype=np.float32)
    onesb = np.ones((128, 128), dtype=ml_dtypes.bfloat16)
    onesf = np.ones((128, 128), dtype=np.float32)
    return h128, eyeb, eyef, onesb, onesf


def make_in_maps(x, wq_, wk_, wv_, wo_, bq_, bk_, bv_, bo_, wg_, wu_, wd_):
    h128, eyeb, eyef, onesb, onesf = _consts()
    common = dict(
        wq=np.ascontiguousarray(wq_), wk=np.ascontiguousarray(wk_),
        wv=np.ascontiguousarray(wv_), wo=np.ascontiguousarray(wo_),
        bq=np.ascontiguousarray(bq_), bk=np.ascontiguousarray(bk_),
        bv=np.ascontiguousarray(bv_), bo=np.ascontiguousarray(bo_),
        wg=np.ascontiguousarray(wg_), wu=np.ascontiguousarray(wu_),
        wd=np.ascontiguousarray(wd_),
        h128=h128, eyeb=eyeb, eyef=eyef, onesb=onesb, onesf=onesf,
    )
    in_maps = []
    for core in range(8):
        b, half = core // 2, core % 2
        m = dict(common)
        m["x_all"] = np.ascontiguousarray(x[b])
        m["x_own"] = np.ascontiguousarray(x[b, half * TOK:(half + 1) * TOK, :])
        in_maps.append(m)
    return in_maps


def kernel(L0_attn_inputs_q, L0_attn_wq, L0_attn_bq, L0_attn_wk, L0_attn_bk,
           L0_attn_wv, L0_attn_bv, L0_attn_wo, L0_attn_bo,
           L0_ffn_w_gate, L0_ffn_w_up, L0_ffn_w_down):
    x = np.ascontiguousarray(L0_attn_inputs_q)
    in_maps = make_in_maps(x, L0_attn_wq, L0_attn_wk, L0_attn_wv, L0_attn_wo,
                           L0_attn_bq, L0_attn_bk, L0_attn_bv, L0_attn_bo,
                           L0_ffn_w_gate, L0_ffn_w_up, L0_ffn_w_down)
    nc = build_program()
    res = run_bass_kernel_spmd(nc, in_maps, core_ids=list(range(8)))
    outp = np.empty((4, S, DM), dtype=np.float32)
    for core in range(8):
        b, half = core // 2, core % 2
        outp[b, half * TOK:(half + 1) * TOK, :] = np.asarray(res.results[core]["out"])
    return outp


# version 09 (final)
# speedup vs baseline: 1.0017x; 1.0017x over previous
import numpy as np
import ml_dtypes
from contextlib import ExitStack

import concourse.bass as bass
import concourse.mybir as mybir
from concourse.bass_utils import run_bass_kernel_spmd

F32 = mybir.dt.float32
BF16 = mybir.dt.bfloat16
I32 = mybir.dt.int32
AF = mybir.ActivationFunctionType
ALU = mybir.AluOpType
AX = mybir.AxisListType

S = 4096
DM = 2048
NH = 16
NKV = 4
HD = 128
FF = 8192
TOK = 2048
TILE = 512
NT = TOK // TILE
SUB = TILE // 128
NST_ALL = S // TILE
NC_DM = DM // 128
NC_FF = FF // 128
KT = S // 128

SERIAL_POINTS = set()
DUMP_MASK = set(range(7))

SM = 1.0 / float(np.sqrt(128.0))
CF = float(np.float32(1.0 / np.sqrt(FF)))
EPS_Q = 1e-5
EPS_LN = 1e-5


def _hadamard(n):
    h = np.array([[1.0]])
    while h.shape[0] < n:
        h = np.block([[h, h], [h, -h]])
    return h


class Bld:
    ENGS = ("tensor", "vector", "scalar", "sync", "gpsimd")

    def __init__(self, nc, stack):
        self.nc = nc
        self.stack = stack
        self.ops = []
        self.semv = {}
        self.ord_sem = {}
        self.ord_cnt = {}
        for eng in ("vector", "scalar", "tensor"):
            self.ord_sem[eng] = None

    def sem(self, name):
        h = self.stack.enter_context(self.nc.semaphore(name))
        self.semv[id(h)] = 0
        return h

    def on(self, eng, fn):
        self.ops.append((eng, fn))

    def bump(self, sem, k=1):
        self.semv[id(sem)] += k
        return self.semv[id(sem)]

    def val(self, sem):
        return self.semv[id(sem)]

    def dma(self, q, out, in_, sem):
        tgt = self.bump(sem, 16)
        self.on(q, lambda e, o=out, i=in_, s=sem: e.dma_start(out=o, in_=i).then_inc(s, 16))
        return tgt

    def wait(self, eng, sem, v=None):
        v = self.val(sem) if v is None else v
        if v > 0:
            self.on(eng, lambda e, s=sem, vv=v: e.wait_ge(s, vv))
        return v

    def init_ord(self):
        self.last_marker = {}
        for eng in ("vector", "scalar"):
            self.ord_sem[eng] = self.sem(f"s_ord_{eng}")
            self.last_marker[eng] = None

    def op(self, eng, fn, sem=None, k=1):
        osem = self.ord_sem.get(eng)
        prev = self.last_marker.get(eng) if osem is not None else None
        if sem is not None:
            tgt = self.bump(sem, k)
            marker = (sem, tgt)
        else:
            tgt = None
            marker = None
        if osem is not None:
            if marker is None:
                marker = (osem, self.bump(osem, 1))
            self.last_marker[eng] = marker
            msem, mval = marker
            use_ord = (sem is None)

        def g(e, f=fn, s=sem, kk=k, prev=prev, osem=osem):
            if prev is not None:
                e.wait_ge(prev[0], prev[1])
            ins = f(e)
            if osem is not None and s is None:
                ins.then_inc(osem, 1)
            if s is not None:
                ins.then_inc(s, kk)
        self.on(eng, g)
        return tgt

    def barrier(self, sem):
        tgt = self.val(sem) + 5
        for eng in self.ENGS:
            self.bump(sem, 1)
            self.on(eng, lambda e, s=sem: e.nop().then_inc(s, 1))
        for eng in self.ENGS:
            self.wait(eng, sem, tgt)
        return tgt

    def run(self, block):
        def body(eng_name):
            def f(e):
                for name, fn in self.ops:
                    if name == eng_name:
                        fn(e)
            return f
        block.tensor(body("tensor"))
        block.vector(body("vector"))
        block.scalar(body("scalar"))
        block.sync(body("sync"))
        block.gpsimd(body("gpsimd"))


def build_program(debug_phase=None):
    nc = bass.Bass("TRN2", target_bir_lowering=False)
    stack = ExitStack()

    x_all = nc.dram_tensor("x_all", [S, DM], F32, kind="ExternalInput").ap()
    x_own = nc.dram_tensor("x_own", [TOK, DM], F32, kind="ExternalInput").ap()
    wq = nc.dram_tensor("wq", [DM, DM], F32, kind="ExternalInput").ap()
    wk = nc.dram_tensor("wk", [DM, 512], F32, kind="ExternalInput").ap()
    wv = nc.dram_tensor("wv", [DM, 512], F32, kind="ExternalInput").ap()
    wo = nc.dram_tensor("wo", [DM, DM], F32, kind="ExternalInput").ap()
    bq = nc.dram_tensor("bq", [DM], F32, kind="ExternalInput").ap()
    bk = nc.dram_tensor("bk", [512], F32, kind="ExternalInput").ap()
    bv = nc.dram_tensor("bv", [512], F32, kind="ExternalInput").ap()
    bo = nc.dram_tensor("bo", [DM], F32, kind="ExternalInput").ap()
    wg = nc.dram_tensor("wg", [FF, DM], F32, kind="ExternalInput").ap()
    wu = nc.dram_tensor("wu", [FF, DM], F32, kind="ExternalInput").ap()
    wd = nc.dram_tensor("wd", [DM, FF], F32, kind="ExternalInput").ap()
    h128_in = nc.dram_tensor("h128", [128, 128], BF16, kind="ExternalInput").ap()
    eyeb_in = nc.dram_tensor("eyeb", [128, 128], BF16, kind="ExternalInput").ap()
    eyef_in = nc.dram_tensor("eyef", [128, 128], F32, kind="ExternalInput").ap()
    onesb_in = nc.dram_tensor("onesb", [128, 128], BF16, kind="ExternalInput").ap()
    onesf_in = nc.dram_tensor("onesf", [128, 128], F32, kind="ExternalInput").ap()

    out = nc.dram_tensor("out", [TOK, DM], F32, kind="ExternalOutput").ap()

    dbg = "ExternalOutput" if debug_phase == "A" else "Internal"
    q_scr = nc.dram_tensor("q_scr", [NH, NT, 128, TILE], BF16, kind=dbg).ap()
    k_scr = nc.dram_tensor("k_scr", [NKV, NST_ALL, 128, TILE], BF16, kind=dbg).ap()
    v_scr = nc.dram_tensor("v_scr", [KT, 128, 512], BF16, kind=dbg).ap()
    wgT_scr = nc.dram_tensor("wgT_scr", [NC_FF, 128, DM], BF16, kind=dbg).ap()
    wuT_scr = nc.dram_tensor("wuT_scr", [NC_FF, 128, DM], BF16, kind=dbg).ap()
    wdT_scr = nc.dram_tensor("wdT_scr", [NC_DM, 128, FF], BF16, kind=dbg).ap()
    woB_scr = nc.dram_tensor("woB_scr", [NC_DM, 128, DM], BF16, kind=dbg).ap()
    s1_scr = nc.dram_tensor("s1_scr", [SUB, 128, DM], F32,
                            kind="ExternalOutput" if debug_phase in ("B1", "B4", "BX") else "Internal").ap()
    if debug_phase == "A":
        dbgA = nc.dram_tensor("dbgA", [128, 256], F32, kind="ExternalOutput").ap()
    dbgF = None
    dbgS = None
    if debug_phase in ("B1", "B4", "BX"):
        dbgF = nc.dram_tensor("dbgF", [6, 128, 66 * 512], BF16, kind="ExternalOutput").ap()
        dbgS = nc.dram_tensor("dbgS", [128, 64], F32, kind="ExternalOutput").ap()

    with stack:
        bld = Bld(nc, stack)
        s_ld = bld.sem("s_ld")
        s_ld2 = bld.sem("s_ld2")
        s_ld3 = bld.sem("s_ld3")
        s_st = bld.sem("s_st")
        s_te = bld.sem("s_te")
        s_ve = bld.sem("s_ve")
        s_se = bld.sem("s_se")
        s_ph = bld.sem("s_ph")
        s_sp = [bld.sem("s_spA"), bld.sem("s_spB")]
        bld.init_ord()

        def sbuf(name, cols, dt=BF16, rows=128, st=stack):
            return st.enter_context(nc.sbuf_tensor(name, [rows, cols], dt))

        h128 = sbuf("c_h128", 128)
        eyeb = sbuf("c_eyeb", 128)
        eyef = sbuf("c_eyef", 128, F32)
        onesb = sbuf("c_onesb", 128)
        onesf = sbuf("c_onesf", 128, F32)
        bq_t = sbuf("bq_t", NH, F32)
        bk_t = sbuf("bk_t", NKV, F32)
        bo_t = sbuf("bo_t", NC_DM, F32)
        bv_bc = sbuf("bv_bc", 512, F32)
        negB = sbuf("negB", NH * NT, F32)
        eps_t = sbuf("eps_t", 1, F32)
        neg1_t = sbuf("neg1_t", 1, F32)
        sw_g = sbuf("sw_g", 1, F32)
        sw_u = sbuf("sw_u", 1, F32)
        sw_d = sbuf("sw_d", 1, F32)
        isw_g = sbuf("isw_g", 1, F32)
        isw_u = sbuf("isw_u", 1, F32)
        isw_d = sbuf("isw_d", 1, F32)
        strow = sbuf("strow", 512, F32, rows=1)
        st1 = sbuf("st1", 16, F32)
        qmax2 = sbuf("qmax2", NH * NT, F32, rows=1)
        kmax2 = sbuf("kmax2", NKV, F32, rows=1)

        pb = [stack.enter_context(nc.psum_tensor(f"pb{i}", [128, 512], F32)) for i in range(6)]
        pbf = stack.enter_context(nc.psum_tensor("pbf", [128, 512], BF16))
        p1 = stack.enter_context(nc.psum_tensor("p1", [1, 512], F32))

        for t in (h128_in, eyeb_in, eyef_in, onesb_in, onesf_in):
            pass
        bld.dma("sync", h128[:], h128_in, s_ld)
        bld.dma("sync", eyeb[:], eyeb_in, s_ld)
        bld.dma("sync", eyef[:], eyef_in, s_ld)
        bld.dma("sync", onesb[:], onesb_in, s_ld)
        bld.dma("sync", onesf[:], onesf_in, s_ld)
        for h in range(NH):
            bld.dma("sync", bq_t[:, h:h + 1], bq[h * 128:(h + 1) * 128].rearrange("(p o) -> p o", o=1), s_ld)
            bld.dma("sync", bo_t[:, h:h + 1], bo[h * 128:(h + 1) * 128].rearrange("(p o) -> p o", o=1), s_ld)
        for h in range(NKV):
            bld.dma("sync", bk_t[:, h:h + 1], bk[h * 128:(h + 1) * 128].rearrange("(p o) -> p o", o=1), s_ld)
        bld.dma("sync", strow[0:1, :], bv.rearrange("(o f) -> o f", o=1), s_ld)
        tgt_consts = bld.val(s_ld)
        bld.wait("vector", s_ld, tgt_consts)
        bld.op("vector", lambda e: e.memset(eps_t[:], EPS_LN))
        bld.op("vector", lambda e: e.memset(neg1_t[:], -1.0))
        bld.op("vector", lambda e: e.memset(kmax2[:], 0.0), s_ve)
        bld.wait("scalar", s_ld, tgt_consts)
        bld.wait("tensor", s_ld, tgt_consts)
        bld.op("tensor", lambda e: e.matmul(pb[0][:], onesf[0:1, :], strow[0:1, :], start=True, stop=True), s_te)
        bld.wait("vector", s_te)
        bld.op("vector", lambda e: e.tensor_copy(out=bv_bc[:], in_=pb[0][:]), s_ve)

        lvlA = ExitStack()
        wqb = lvlA.enter_context(nc.sbuf_tensor("wqb", [128, NC_DM * DM], BF16))
        wkb = lvlA.enter_context(nc.sbuf_tensor("wkb", [128, NC_DM * 512], BF16))
        wvb = lvlA.enter_context(nc.sbuf_tensor("wvb", [128, NC_DM * 512], BF16))
        xa_f = [lvlA.enter_context(nc.sbuf_tensor(f"xa_f{i}", [128, DM], F32)) for i in range(2)]
        xa_b = lvlA.enter_context(nc.sbuf_tensor("xa_b", [128, DM], BF16))
        xf = lvlA.enter_context(nc.sbuf_tensor("xf", [128, NC_DM * TILE], BF16))
        ev_b = [lvlA.enter_context(nc.sbuf_tensor(f"ev_b{i}", [128, 512], BF16)) for i in range(2)]
        sq_b = lvlA.enter_context(nc.sbuf_tensor("sq_b", [128, 512], BF16))
        xf_view = xf.rearrange("p (c f) -> p c f", c=NC_DM)
        pbf_view = pbf.rearrange("p (j f) -> p j f", j=4)

        for c in range(NC_DM):
            bld.dma("gpsimd", wqb[:, c * DM:(c + 1) * DM], wq[c * 128:(c + 1) * 128, :], s_ld3)
            bld.dma("gpsimd", wkb[:, c * 512:(c + 1) * 512], wk[c * 128:(c + 1) * 128, :], s_ld3)
            bld.dma("gpsimd", wvb[:, c * 512:(c + 1) * 512], wv[c * 128:(c + 1) * 128, :], s_ld3)
        tgt_w = bld.val(s_ld3)
        bld.wait("tensor", s_ld3, tgt_w)

        def emit_x_sub(src_rows, i_sub, buf_i):
            bld.wait("sync", s_ve)
            t_l = bld.dma("sync", xa_f[buf_i][:], src_rows, s_ld)
            bld.wait("vector", s_ld, t_l)
            bld.wait("vector", s_te)
            bld.op("vector", lambda e, b=buf_i: e.tensor_copy(out=xa_b[:], in_=xa_f[b][:]), s_ve)
            bld.wait("tensor", s_ve)
            for cg in range(4):
                bld.wait("tensor", s_ve)
                for j in range(4):
                    c = cg * 4 + j
                    bld.op("tensor", lambda e, c=c, j=j: e.transpose(
                        pbf[:, j * 128:(j + 1) * 128], xa_b[:, c * 128:(c + 1) * 128], eyeb[:]),
                        s_te if j == 3 else None)
                bld.wait("vector", s_te)
                bld.op("vector", lambda e, cg=cg, i=i_sub: e.tensor_copy(
                    out=xf_view[:, cg * 4:(cg + 1) * 4, i * 128:(i + 1) * 128], in_=pbf_view[:]), s_ve)

        ev_i = [0]

        def proj_group(bank, emit_mms, bias_col_ap, dst, norm_col, kind):
            bld.wait("tensor", s_se)
            bld.wait("tensor", s_ve)
            emit_mms(bank)
            b_ev = ev_b[ev_i[0] % 2]
            ev_i[0] += 1
            bld.wait("scalar", s_te)
            bld.wait("scalar", s_st)
            if kind == "v":
                bld.wait("vector", s_te)
                bld.wait("vector", s_st)
                bld.op("vector", lambda e, b=bank, o=b_ev: e.tensor_tensor(
                    out=o[:], in0=b[:], in1=bv_bc[:], op=ALU.add), s_ve)
                bld.wait("sync", s_ve)
                bld.dma("sync", dst, b_ev[:], s_st)
                return
            bld.op("scalar", lambda e, b=bank, o=b_ev, ba=bias_col_ap: e.activation(
                out=o[:], in_=b[:], func=AF.Identity, bias=ba), s_se)
            bld.wait("sync", s_se)
            bld.dma("sync", dst, b_ev[:], s_st)
            bld.wait("scalar", s_te)
            bld.op("scalar", lambda e, i=b_ev: e.activation(out=sq_b[:], in_=i[:], func=AF.Square), s_se)
            bld.wait("tensor", s_se)
            bld.wait("tensor", s_ve)
            bld.op("tensor", lambda e: e.matmul(p1[0:1, :], onesb[:, 0:1], sq_b[:], start=True, stop=True), s_te)
            bld.wait("vector", s_te)
            if kind == "q":
                bld.op("vector", lambda e, col=norm_col: e.tensor_reduce(
                    out=qmax2[0:1, col:col + 1], in_=p1[0:1, :], axis=AX.X, op=ALU.max), s_ve)
            else:
                bld.op("vector", lambda e: e.tensor_reduce(
                    out=st1[0:1, 0:1], in_=p1[0:1, :], axis=AX.X, op=ALU.max), s_ve)
                bld.wait("vector", s_ve)
                bld.op("vector", lambda e, col=norm_col: e.tensor_tensor(
                    out=kmax2[0:1, col:col + 1], in0=kmax2[0:1, col:col + 1], in1=st1[0:1, 0:1],
                    op=ALU.max), s_ve)

        rot = [0]

        def next_bank():
            b = pb[rot[0] % 4]
            rot[0] += 1
            return b

        for st in range(NST_ALL):
            for i in range(SUB):
                emit_x_sub(x_all[(st * TILE + i * 128):(st * TILE + (i + 1) * 128), :], i, i % 2)
            for kv in range(NKV):
                def mm_k(bank, kv=kv):
                    for c in range(NC_DM):
                        bld.op("tensor", lambda e, b=bank, c=c, kv=kv: e.matmul(
                            b[:], wkb[:, c * 512 + kv * 128: c * 512 + (kv + 1) * 128], xf_view[:, c, :],
                            start=(c == 0), stop=(c == NC_DM - 1)), s_te if c == NC_DM - 1 else None)
                proj_group(next_bank(), mm_k, bk_t[:, kv:kv + 1], k_scr[kv, st], kv, "k")
            for i in range(SUB):
                def mm_v(bank, i=i):
                    for c in range(NC_DM):
                        bld.op("tensor", lambda e, b=bank, c=c, i=i: e.matmul(
                            b[:], xf_view[:, c, i * 128:(i + 1) * 128], wvb[:, c * 512:(c + 1) * 512],
                            start=(c == 0), stop=(c == NC_DM - 1)), s_te if c == NC_DM - 1 else None)
                proj_group(next_bank(), mm_v, None, v_scr[st * SUB + i], None, "v")

        for st in range(NT):
            for i in range(SUB):
                emit_x_sub(x_own[(st * TILE + i * 128):(st * TILE + (i + 1) * 128), :], i, i % 2)
            for h in range(NH):
                def mm_q(bank, h=h):
                    for c in range(NC_DM):
                        bld.op("tensor", lambda e, b=bank, c=c, h=h: e.matmul(
                            b[:], wqb[:, c * DM + h * 128: c * DM + (h + 1) * 128], xf_view[:, c, :],
                            start=(c == 0), stop=(c == NC_DM - 1)), s_te if c == NC_DM - 1 else None)
                proj_group(next_bank(), mm_q, bq_t[:, h:h + 1], q_scr[h, st], h * NT + st, "q")

        bld.wait("vector", s_ve)
        for kv in range(NKV):
            bld.op("vector", lambda e, kv=kv: e.tensor_scalar(
                out=qmax2[0:1, kv * 4 * NT:(kv + 1) * 4 * NT], in0=qmax2[0:1, kv * 4 * NT:(kv + 1) * 4 * NT],
                scalar1=kmax2[0:1, kv:kv + 1], scalar2=None, op0=ALU.mult), s_ve)
        bld.wait("scalar", s_ve)
        bld.op("scalar", lambda e: e.activation(out=strow[0:1, 0:NH * NT], in_=qmax2[0:1, :], func=AF.Ln), s_se)
        bld.wait("scalar", s_se)
        bld.op("scalar", lambda e: e.activation(out=strow[0:1, 0:NH * NT], in_=strow[0:1, 0:NH * NT],
                                                func=AF.Exp, scale=0.5), s_se)
        bld.wait("vector", s_se)
        bld.op("vector", lambda e: e.tensor_scalar(
            out=strow[0:1, 0:NH * NT], in0=strow[0:1, 0:NH * NT], scalar1=-SM, scalar2=None,
            op0=ALU.mult), s_ve)
        bld.wait("tensor", s_ve)
        bld.op("tensor", lambda e: e.matmul(pb[4][:, 0:NH * NT], onesf[0:1, :], strow[0:1, 0:NH * NT],
                                            start=True, stop=True), s_te)
        bld.wait("vector", s_te)
        bld.op("vector", lambda e: e.tensor_copy(out=negB[:], in_=pb[4][:, 0:NH * NT]), s_ve)

        bld.wait("sync", s_st)
        bld.wait("sync", s_ld)
        bld.wait("sync", s_ld2)
        bld.wait("sync", s_ld3)
        bld.barrier(s_ph)
        lvlA.close()
        lvlA = ExitStack()

        wo_stage = lvlA.enter_context(nc.sbuf_tensor("wo_stage", [128, DM], F32))
        wo_bf = lvlA.enter_context(nc.sbuf_tensor("wo_bf", [128, DM], BF16))
        for dm in range(NC_DM):
            bld.wait("scalar", s_ve)
            t_l = 0
            for hd in range(NC_DM):
                t_l = bld.dma("scalar", wo_stage[:, hd * 128:(hd + 1) * 128],
                              wo[hd * 128:(hd + 1) * 128, dm * 128:(dm + 1) * 128], s_ld2)
            bld.wait("vector", s_ld2, t_l)
            bld.wait("vector", s_st)
            bld.op("vector", lambda e: e.tensor_copy(out=wo_bf[:], in_=wo_stage[:]), s_ve)
            bld.wait("sync", s_ve)
            bld.dma("sync", woB_scr[dm], wo_bf[:], s_st)

        st_f = [lvlA.enter_context(nc.sbuf_tensor(f"st_f{i}", [128, DM], F32)) for i in range(2)]
        ab_b = lvlA.enter_context(nc.sbuf_tensor("ab_b", [128, DM], BF16))

        def mat_pieces(wsrc):
            n_rows, width = wsrc.shape
            for si in range(n_rows // 128):
                for qj in range(width // DM):
                    yield wsrc[si * 128:(si + 1) * 128, qj * DM:(qj + 1) * DM], si, qj

        mats = [("g", wg, FF * DM, sw_g, isw_g),
                ("u", wu, FF * DM, sw_u, isw_u),
                ("d", wd, DM * FF, sw_d, isw_d)]

        for tag, wsrc, nel, sw_pp, isw_pp in mats:
            pieces = list(mat_pieces(wsrc))
            for pi, (rows, si, qj) in enumerate(pieces):
                bld.wait("sync", s_se)
                t_l = bld.dma("sync", st_f[pi % 2][:], rows, s_sp[pi % 2])
                bld.wait("scalar", s_sp[pi % 2], t_l)
                bld.wait("scalar", s_te)
                bld.op("scalar", lambda e, b=pi % 2: e.activation(
                    out=ab_b[:], in_=st_f[b][:], func=AF.Abs), s_se)
                bld.wait("tensor", s_se)
                for q in range(4):
                    fq = (pi == 0 and q == 0)
                    lq = (pi == len(pieces) - 1 and q == 3)
                    bld.op("tensor", lambda e, q=q, f=fq, l=lq: e.matmul(
                        p1[0:1, :], onesb[:, 0:1], ab_b[:, q * 512:(q + 1) * 512],
                        start=f, stop=l), s_te if q == 3 else None)
            bld.wait("vector", s_te)
            bld.op("vector", lambda e: e.tensor_reduce(out=st1[0:1, 0:1], in_=p1[0:1, :], axis=AX.X, op=ALU.add), s_ve)
            bld.wait("vector", s_ve)
            bld.op("vector", lambda e, n=nel: e.tensor_scalar(
                out=st1[0:1, 1:2], in0=st1[0:1, 0:1], scalar1=1.0 / n, scalar2=EPS_Q,
                op0=ALU.mult, op1=ALU.max), s_ve)
            bld.wait("vector", s_ve)
            bld.op("vector", lambda e: e.reciprocal(out=st1[0:1, 2:3], in_=st1[0:1, 1:2]), s_ve)
            bld.wait("vector", s_ve)
            bld.op("vector", lambda e: e.reciprocal(out=st1[0:1, 3:4], in_=st1[0:1, 2:3]), s_ve)
            bld.wait("tensor", s_ve)
            bld.op("tensor", lambda e: e.matmul(pb[4][:, 0:1], onesf[0:1, :], st1[0:1, 2:3], start=True, stop=True))
            bld.op("tensor", lambda e: e.matmul(pb[5][:, 0:1], onesf[0:1, :], st1[0:1, 3:4], start=True, stop=True), s_te)
            bld.wait("vector", s_te)
            bld.op("vector", lambda e, d=sw_pp: e.tensor_copy(out=d[:], in_=pb[4][:, 0:1]))
            bld.op("vector", lambda e, d=isw_pp: e.tensor_copy(out=d[:], in_=pb[5][:, 0:1]), s_ve)
            bld.wait("tensor", s_ve)

        v_f = lvlA.enter_context(nc.sbuf_tensor("v_f", [128, DM], F32))
        a_f = lvlA.enter_context(nc.sbuf_tensor("a_f", [128, DM], F32))
        t_bt = lvlA.enter_context(nc.sbuf_tensor("t_bt", [128, DM], BF16))
        tT_b = lvlA.enter_context(nc.sbuf_tensor("tT_b", [128, DM], BF16))
        guard_sw = bld.val(s_ve)
        bld.wait("vector", s_ve, guard_sw)

        def tern_piece(rows, sw_pp, dst, buf_i):
            bld.wait("sync", s_ve)
            t_l = bld.dma("sync", st_f[buf_i][:], rows, s_sp[buf_i])
            bld.wait("vector", s_sp[buf_i], t_l)
            bld.op("vector", lambda e, b=buf_i, s=sw_pp: e.tensor_scalar(
                out=v_f[:], in0=st_f[b][:], scalar1=s[:], scalar2=None, op0=ALU.mult))
            bld.op("vector", lambda e: e.tensor_scalar(
                out=a_f[:], in0=v_f[:], scalar1=-0.5, scalar2=None, op0=ALU.is_le))
            bld.wait("vector", s_te)
            bld.op("vector", lambda e: e.scalar_tensor_tensor(
                out=t_bt[:], in0=v_f[:], scalar=0.5, in1=a_f[:],
                op0=ALU.is_ge, op1=ALU.subtract), s_ve)
            bld.wait("tensor", s_ve)
            for cg in range(4):
                bld.wait("tensor", s_ve)
                for j in range(4):
                    c = cg * 4 + j
                    bld.op("tensor", lambda e, c=c, j=j: e.transpose(
                        pbf[:, j * 128:(j + 1) * 128], t_bt[:, c * 128:(c + 1) * 128], eyeb[:]),
                        s_te if j == 3 else None)
                bld.wait("vector", s_te)
                bld.wait("vector", s_st)
                bld.op("vector", lambda e, cg=cg: e.tensor_copy(
                    out=tT_b[:, cg * 512:(cg + 1) * 512], in_=pbf[:]), s_ve)
            bld.wait("sync", s_ve)
            bld.dma("sync", dst, tT_b[:], s_st)

        for rows, si, qj in mat_pieces(wg):
            tern_piece(rows, sw_g, wgT_scr[si], (si) % 2)
        for rows, si, qj in mat_pieces(wu):
            tern_piece(rows, sw_u, wuT_scr[si], (si) % 2)
        for rows, si, qj in mat_pieces(wd):
            tern_piece(rows, sw_d, wdT_scr[si].rearrange("p (q f) -> p q f", q=4)[:, qj, :], (si * 4 + qj) % 2)

        t_all_st = bld.val(s_st)
        for eng in Bld.ENGS:
            bld.wait(eng, s_st, t_all_st)
        lvlA.close()
        bld.barrier(s_ph)

        if debug_phase == "A":
            dbg_t = sbuf("dbg_t", 256, F32)
            bld.op("vector", lambda e: e.memset(dbg_t[:], 0.0))
            bld.op("vector", lambda e: e.tensor_copy(out=dbg_t[0:1, 128:192], in_=qmax2[0:1, :]))
            bld.op("vector", lambda e: e.tensor_copy(out=dbg_t[0:1, 192:196], in_=kmax2[0:1, :]))
            bld.op("vector", lambda e: e.tensor_copy(out=dbg_t[:, 0:NH * NT], in_=negB[:]))
            bld.op("vector", lambda e: e.tensor_copy(out=dbg_t[:, 64:65], in_=sw_g[:]))
            bld.op("vector", lambda e: e.tensor_copy(out=dbg_t[:, 65:66], in_=sw_u[:]))
            bld.op("vector", lambda e: e.tensor_copy(out=dbg_t[:, 66:67], in_=sw_d[:]))
            bld.op("vector", lambda e: e.tensor_copy(out=dbg_t[:, 67:68], in_=isw_d[:]))
            bld.op("vector", lambda e: e.tensor_copy(out=dbg_t[:, 68:69], in_=bv_bc[:, 3:4]), s_ve)
            bld.wait("sync", s_ve)
            bld.dma("sync", dbgA, dbg_t[:], s_st)
            bld.wait("sync", s_st)
            z_t = sbuf("z_t", DM, F32)
            bld.op("vector", lambda e: e.memset(z_t[:], 0.0), s_ve)
            bld.wait("sync", s_ve)
            for i in range(TOK // 128):
                bld.dma("sync", out[i * 128:(i + 1) * 128, :], z_t[:], s_st)
            bld.wait("sync", s_st)
            block = stack.enter_context(nc.Block())
            bld.run(block)
            return nc

        build_phase_b(nc, stack, bld, dict(
            n_tiles=(1 if debug_phase == "B1" else NT),
            dump_tt=(0 if debug_phase == "B1" else (NT - 1 if debug_phase == "B4" else (-2 if debug_phase == "BX" else -1))),
            dbgF=dbgF, dbgS=dbgS,
            out=out, x_own=x_own, q_scr=q_scr, k_scr=k_scr, v_scr=v_scr,
            wgT_scr=wgT_scr, wuT_scr=wuT_scr, wdT_scr=wdT_scr, woB_scr=woB_scr,
            s1_scr=s1_scr,
            h128=h128, eyeb=eyeb, eyef=eyef, onesb=onesb, onesf=onesf,
            bo_t=bo_t, negB=negB, eps_t=eps_t, neg1_t=neg1_t,
            isw_g=isw_g, isw_u=isw_u, isw_d=isw_d,
            strow=strow, st1=st1, pb=pb, pbf=pbf, p1=p1,
            s_ld=s_ld, s_ld2=s_ld2, s_ld3=s_ld3, s_st=s_st,
            s_te=s_te, s_ve=s_ve, s_se=s_se, s_ph=s_ph,
        ))
        block = stack.enter_context(nc.Block())
        bld.run(block)
    return nc


def build_phase_b(nc, stack, bld, env):
    g = env
    out, x_own = g["out"], g["x_own"]
    q_scr, k_scr, v_scr = g["q_scr"], g["k_scr"], g["v_scr"]
    wgT_scr, wuT_scr, wdT_scr, woB_scr = g["wgT_scr"], g["wuT_scr"], g["wdT_scr"], g["woB_scr"]
    s1_scr = g["s1_scr"]
    h128, eyeb, eyef, onesb, onesf = g["h128"], g["eyeb"], g["eyef"], g["onesb"], g["onesf"]
    bo_t, negB, eps_t, neg1_t = g["bo_t"], g["negB"], g["eps_t"], g["neg1_t"]
    isw_g, isw_u, isw_d = g["isw_g"], g["isw_u"], g["isw_d"]
    strow, st1, pb, pbf, p1 = g["strow"], g["st1"], g["pb"], g["pbf"], g["p1"]
    s_ld, s_ld2, s_ld3, s_st = g["s_ld"], g["s_ld2"], g["s_ld3"], g["s_st"]
    s_te, s_ve, s_se, s_ph = g["s_te"], g["s_ve"], g["s_se"], g["s_ph"]

    s_sc = bld.sem("s_sc")
    s_ex = bld.sem("s_ex")
    s_pv = bld.sem("s_pv")
    s_sm = bld.sem("s_sm")
    s_pp = bld.sem("s_pp")
    s_wp = [bld.sem("s_wpA"), bld.sem("s_wpB")]
    s_gu = [bld.sem("s_guA"), bld.sem("s_guB")]
    s_dn = [bld.sem("s_dnA"), bld.sem("s_dnB")]

    def sbufB(st_, name, cols, dt=BF16, rows=128):
        return st_.enter_context(nc.sbuf_tensor(name, [rows, cols], dt))

    lvlB = ExitStack()
    xres = [sbufB(lvlB, f"xres{i}", DM, F32) for i in range(2)]
    sq_f = sbufB(lvlB, "sq_f", DM, F32)
    br_sub = sbufB(lvlB, "br_sub", DM, BF16)
    stt = sbufB(lvlB, "stt", 32, F32)
    inv1 = sbufB(lvlB, "inv1", SUB, F32)
    cs2 = sbufB(lvlB, "cs2", SUB, F32)
    s2c = sbufB(lvlB, "s2c", SUB, F32)
    inv1B = sbufB(lvlB, "inv1B", 512, F32)
    s2B = sbufB(lvlB, "s2B", 512, F32)
    cs2B = sbufB(lvlB, "cs2B", 512, F32)

    C_SX, C_SX2, C_MEAN, C_EX2, C_M2T, C_VAR, C_LNV, C_RSTD, C_NMR = range(9)
    C_MRAW, C_M, C_RS, C_SACT = 9, 10, 11, 12

    def col(c):
        return stt[:, c:c + 1]

    def pp(eng, fn):
        bld.wait(eng, s_pp)
        return bld.op(eng, fn, s_pp)

    dbgF, dbgS = g.get("dbgF"), g.get("dbgS")
    dump_tt = g.get("dump_tt", -1)
    global DUMP_MASK
    cur_tt = [0]

    def spoint(k):
        if k not in SERIAL_POINTS:
            return
        for eng in ("vector", "scalar", "tensor"):
            sem = {"vector": s_ve, "scalar": s_se, "tensor": s_te}[eng]
            bld.op(eng, lambda e: e.nop(), sem)
        for eng in ("vector", "scalar", "tensor", "sync"):
            bld.wait(eng, s_ve)
            bld.wait(eng, s_se)
            bld.wait(eng, s_te)

    def _dump_on(k):
        return dbgF is not None and (cur_tt[0] == dump_tt or dump_tt == -2) and k in DUMP_MASK

    def dump_slots(k, pos_list):
        if not _dump_on(k):
            return
        for eng in ("vector", "scalar", "tensor"):
            pass
        bld.wait("sync", s_ve)
        bld.wait("sync", s_se)
        bld.wait("sync", s_te)
        for oc in range(NC_FF):
            bld.dma("sync", dbgF[k][:, oc * 512:(oc + 1) * 512],
                    Xs_cur[0][:, pos_list[oc] * 512:(pos_list[oc] + 1) * 512], s_st)
        bld.wait("sync", s_st)
        for eng in ("vector", "scalar", "tensor"):
            bld.wait(eng, s_st)

    def dump_tile(k, ap_src, width):
        if not _dump_on(k):
            return
        bld.wait("sync", s_ve)
        bld.wait("sync", s_se)
        bld.wait("sync", s_te)
        bld.dma("sync", dbgF[k][:, 0:width], ap_src, s_st)
        bld.wait("sync", s_st)
        for eng in ("vector", "scalar", "tensor"):
            bld.wait(eng, s_st)

    Xs_cur = [None]

    def ln_block(raw_slice, dst_slice, sub_i):
        bld.wait("scalar", s_ve)
        bld.op("scalar", lambda e: e.activation(out=xres[sub_i % 2][:], in_=raw_slice, func=AF.Square), s_se)
        bld.op("vector", lambda e: e.tensor_reduce(out=col(C_SX), in_=raw_slice, axis=AX.X, op=ALU.add), s_ve)
        bld.wait("vector", s_se)
        bld.op("vector", lambda e: e.tensor_reduce(out=col(C_SX2), in_=xres[sub_i % 2][:], axis=AX.X, op=ALU.add), s_ve)
        pp("vector", lambda e: e.tensor_scalar(out=col(C_MEAN), in0=col(C_SX), scalar1=1.0 / DM, scalar2=None, op0=ALU.mult))
        pp("vector", lambda e: e.tensor_scalar(out=col(C_EX2), in0=col(C_SX2), scalar1=1.0 / DM, scalar2=None, op0=ALU.mult))
        pp("vector", lambda e: e.tensor_scalar(out=col(C_M2T), in0=col(C_MEAN), scalar1=col(C_MEAN), scalar2=None, op0=ALU.mult))
        pp("vector", lambda e: e.tensor_tensor(out=col(C_VAR), in0=col(C_EX2), in1=col(C_M2T), op=ALU.subtract))
        bld.wait("scalar", s_pp)
        pp("scalar", lambda e: e.activation(out=col(C_LNV), in_=col(C_VAR), func=AF.Ln, bias=eps_t[:]))
        pp("scalar", lambda e: e.activation(out=col(C_RSTD), in_=col(C_LNV), func=AF.Exp, scale=-0.5))
        bld.wait("vector", s_pp)
        pp("vector", lambda e: e.tensor_scalar(out=col(C_NMR), in0=col(C_RSTD), scalar1=col(C_MEAN), scalar2=-1.0,
                                               op0=ALU.mult, op1=ALU.mult))
        bld.wait("scalar", s_pp)
        bld.op("scalar", lambda e: e.activation(out=dst_slice, in_=raw_slice, func=AF.Identity,
                                                scale=col(C_RSTD), bias=col(C_NMR)), s_se)

    def emit_attn(TT, stream1, v_sb, k_sb, q_sb, pT, sumC, OT, wos, brF, recipB):
        ncol0 = TT
        for kt in range(KT):
            bld.dma("sync", v_sb[:, kt * 512:(kt + 1) * 512], v_scr[kt], s_ld)
        for kv in range(NKV):
            for st in range(NST_ALL):
                bld.dma("scalar", k_sb[:, kv * S + st * 512: kv * S + (st + 1) * 512], k_scr[kv, st], s_ld2)
        for h in range(NH):
            bld.dma("gpsimd", q_sb[:, h * 512:(h + 1) * 512], q_scr[h, TT], s_ld3)
        bld.wait("tensor", s_ld)
        bld.wait("tensor", s_ld2)
        bld.wait("tensor", s_ld3)

        for kv in range(NKV):
            for gq in range(4):
                h = kv * 4 + gq
                ncol = h * NT + TT
                bld.wait("tensor", s_se)
                bld.wait("tensor", s_ve)
                bld.wait("vector", s_te)
                bld.wait("scalar", s_pv)
                bld.wait("scalar", s_sm)
                sc_t = [None] * KT
                ex_t = [None] * KT
                pv_t = [None] * KT
                sm_t = [None] * KT

                def mm_sc(kt, kv=kv, h=h):
                    sc_t[kt] = bld.op("tensor", lambda e, kt=kt: e.matmul(
                        pb[kt % 2][:], k_sb[:, kv * S + kt * 128: kv * S + (kt + 1) * 128],
                        q_sb[:, h * 512:(h + 1) * 512], start=True, stop=True,
                        skip_group_check=True), s_sc)

                mm_sc(0)
                mm_sc(1)
                for kt in range(KT):
                    bld.wait("scalar", s_sc, sc_t[kt])
                    if kt >= 2:
                        bld.wait("scalar", s_pv, pv_t[kt - 2])
                        bld.wait("scalar", s_sm, sm_t[kt - 2])
                    ex_t[kt] = bld.op("scalar", lambda e, kt=kt, ncol=ncol: e.activation(
                        out=pT[:, (kt % 2) * 512:(kt % 2 + 1) * 512], in_=pb[kt % 2][:],
                        func=AF.Exp, scale=SM, bias=negB[:, ncol:ncol + 1]), s_ex)
                    bld.wait("tensor", s_ex, ex_t[kt])
                    pv_t[kt] = bld.op("tensor", lambda e, kt=kt, kv=kv: e.matmul(
                        pb[2][:], v_sb[:, kt * 512 + kv * 128: kt * 512 + (kv + 1) * 128],
                        pT[:, (kt % 2) * 512:(kt % 2 + 1) * 512],
                        start=(kt == 0), stop=(kt == KT - 1), skip_group_check=True), s_pv)
                    if kt + 2 < KT:
                        mm_sc(kt + 2)
                    bld.wait("vector", s_ex, ex_t[kt])
                    if kt == 0:
                        sm_t[kt] = bld.op("vector", lambda e: e.tensor_copy(
                            out=sumC[:], in_=pT[:, 0:512]), s_sm)
                    else:
                        sm_t[kt] = bld.op("vector", lambda e, kt=kt: e.tensor_tensor(
                            out=sumC[:], in0=sumC[:], in1=pT[:, (kt % 2) * 512:(kt % 2 + 1) * 512],
                            op=ALU.add), s_sm)
                bld.wait("tensor", s_sm, sm_t[KT - 1])
                bld.wait("tensor", s_ve)
                t_ones = bld.op("tensor", lambda e: e.matmul(
                    p1[0:1, :], onesb[:, 0:1], sumC[:], start=True, stop=True), s_te)
                bld.wait("vector", s_te, t_ones)
                t_rc = bld.op("vector", lambda e: e.reciprocal(out=strow[0:1, :], in_=p1[0:1, :]), s_ve)
                bld.wait("tensor", s_ve, t_rc)
                t_bc = bld.op("tensor", lambda e: e.matmul(
                    pb[4][:], onesf[0:1, :], strow[0:1, :], start=True, stop=True), s_te)
                bld.wait("vector", s_te, t_bc)
                bld.op("vector", lambda e: e.tensor_copy(out=recipB[:], in_=pb[4][:]), s_ve)
                bld.wait("vector", s_pv, pv_t[KT - 1])
                bld.op("vector", lambda e, h=h: e.tensor_tensor(
                    out=OT[:, h * 512:(h + 1) * 512], in0=pb[2][:], in1=recipB[:], op=ALU.mult), s_ve)

        def _emit_outproj_dm(dm, ld_tgt):
            bld.wait("tensor", s_wp[dm % 2], ld_tgt)
            if dm >= 2:
                bld.wait("tensor", s_se, ev_t[dm - 2])
            for hd in range(NC_DM):
                tv = bld.op("tensor", lambda e, dm=dm, hd=hd: e.matmul(
                    pb[5 if dm % 2 == 0 else 3][:],
                    wos[:, (dm % 2) * DM + hd * 128:(dm % 2) * DM + (hd + 1) * 128],
                    OT[:, hd * 512:(hd + 1) * 512],
                    start=(hd == 0), stop=(hd == NC_DM - 1), skip_group_check=True),
                    s_te if hd == NC_DM - 1 else None)
                if hd == NC_DM - 1:
                    mm_t[dm] = tv
            bld.wait("scalar", s_te, mm_t[dm])
            ev_t[dm] = bld.op("scalar", lambda e, dm=dm: e.activation(
                out=brF[:, dm * 512:(dm + 1) * 512], in_=pb[5 if dm % 2 == 0 else 3][:],
                func=AF.Identity, bias=bo_t[:, dm:dm + 1]), s_se)

        mm_t = [None] * NC_DM
        ev_t = [None] * NC_DM
        ld_t_arr = [None] * NC_DM
        bld.wait("sync", s_ve)
        for dm in range(NC_DM):
            if dm >= 2:
                bld.wait("sync", s_te, mm_t[dm - 2])
            ld_t_arr[dm] = bld.dma("sync", wos[:, (dm % 2) * DM:(dm % 2 + 1) * DM], woB_scr[dm], s_wp[dm % 2])
            _emit_outproj_dm(dm, ld_t_arr[dm])


        for i in range(SUB):
            bld.wait("sync", s_ve)
            t_x = bld.dma("sync", xres[i % 2][:], x_own[(TT * TILE + i * 128):(TT * TILE + (i + 1) * 128), :], s_ld)
            bld.wait("tensor", s_se)
            for cg in range(4):
                bld.wait("tensor", s_ve)
                for j in range(4):
                    dm = cg * 4 + j
                    bld.op("tensor", lambda e, dm=dm, j=j, i=i: e.transpose(
                        pbf[:, j * 128:(j + 1) * 128],
                        brF[:, dm * 512 + i * 128: dm * 512 + (i + 1) * 128], eyeb[:]),
                        s_te if j == 3 else None)
                bld.wait("vector", s_te)
                bld.op("vector", lambda e, cg=cg: e.tensor_copy(
                    out=br_sub.rearrange("p (c f) -> p c f", c=NC_DM)[:, cg * 4:(cg + 1) * 4, :],
                    in_=pbf.rearrange("p (j f) -> p j f", j=4)[:]), s_ve)
            bld.wait("vector", s_ld, t_x)
            bld.wait("vector", s_se)
            bld.op("vector", lambda e, i=i: e.tensor_tensor(
                out=sq_f[:], in0=xres[i % 2][:], in1=br_sub[:], op=ALU.add), s_ve)
            ln_block(sq_f[:], stream1[:, i * DM:(i + 1) * DM], i)
            bld.wait("sync", s_se)
            bld.dma("sync", s1_scr[i], stream1[:, i * DM:(i + 1) * DM], s_st)



    def emit_ffn(TT, Xs, xq_f, arena, dslab2, k0i_t, s1f, uq, k0q, dq, gs, tu, maxC, brD):
        for i in range(SUB):
            bld.wait("sync", s_ve)
            t_s1 = bld.dma("sync", s1f[:], s1_scr[i], s_ld)
            bld.wait("vector", s_ld, t_s1)
            pp("vector", lambda e: e.tensor_reduce(
                out=col(C_MRAW), in_=s1f[:], axis=AX.X, op=ALU.max, apply_absolute_value=True))
            pp("vector", lambda e: e.tensor_scalar(out=col(C_M), in0=col(C_MRAW), scalar1=EPS_Q,
                                                   scalar2=None, op0=ALU.max))
            pp("vector", lambda e: e.reciprocal(out=col(C_RS), in_=col(C_M)))
            pp("vector", lambda e: e.tensor_scalar(out=col(C_SACT), in0=col(C_RS), scalar1=128.0,
                                                   scalar2=None, op0=ALU.mult))
            pp("vector", lambda e, i=i: e.tensor_scalar(out=inv1[:, i:i + 1], in0=col(C_M),
                                                        scalar1=1.0 / 128.0, scalar2=None, op0=ALU.mult))
            bld.wait("vector", s_te)
            bld.op("vector", lambda e: e.tensor_scalar(
                out=sq_f[:], in0=s1f[:], scalar1=col(C_SACT), scalar2=None, op0=ALU.mult))
            bld.op("vector", lambda e: e.tensor_copy(out=k0i_t[:], in_=sq_f[:]))
            bld.op("vector", lambda e, i=i: e.tensor_tensor(
                out=xres[i % 2][:], in0=sq_f[:], in1=k0i_t[:], op=ALU.subtract))
            bld.op("vector", lambda e, i=i: e.scalar_tensor_tensor(
                out=sq_f[:], in0=xres[i % 2][:], scalar=0.5, in1=k0i_t[:], op0=ALU.is_gt, op1=ALU.add))
            bld.op("vector", lambda e, i=i: e.scalar_tensor_tensor(
                out=br_sub[:], in0=xres[i % 2][:], scalar=-0.5, in1=sq_f[:], op0=ALU.is_ge, op1=ALU.add))
            bld.op("vector", lambda e: e.tensor_scalar(
                out=sq_f[:], in0=br_sub[:], scalar1=128.0, scalar2=-127.0, op0=ALU.min, op1=ALU.max), s_ve)
            bld.wait("scalar", s_ve)
            bld.op("scalar", lambda e: e.activation(out=br_sub[:], in_=sq_f[:], func=AF.Identity,
                                                    bias=neg1_t[:]), s_se)
            bld.wait("tensor", s_se)
            for cg in range(4):
                bld.wait("tensor", s_ve)
                for j in range(4):
                    c = cg * 4 + j
                    bld.op("tensor", lambda e, c=c, j=j: e.transpose(
                        pbf[:, j * 128:(j + 1) * 128], br_sub[:, c * 128:(c + 1) * 128], eyeb[:]),
                        s_te if j == 3 else None)
                bld.wait("vector", s_te)
                bld.op("vector", lambda e, cg=cg, i=i: e.tensor_copy(
                    out=xq_f.rearrange("p (c f) -> p c f", c=NC_DM)[:, cg * 4:(cg + 1) * 4, i * 128:(i + 1) * 128],
                    in_=pbf.rearrange("p (j f) -> p j f", j=4)[:]), s_ve)

        spoint(0)
        dump_tile(0, xq_f[:], NC_DM * 512)
        bld.wait("tensor", s_pp)
        bld.wait("tensor", s_ve)
        for i in range(SUB):
            bld.op("tensor", lambda e, i=i: e.transpose(
                p1[0:1, i * 128:(i + 1) * 128], inv1[:, i:i + 1], eyef[:]),
                s_te if i == SUB - 1 else None)
        bld.wait("vector", s_te)
        t_cp = bld.op("vector", lambda e: e.tensor_copy(out=strow[0:1, :], in_=p1[0:1, :]), s_ve)
        bld.wait("tensor", s_ve, t_cp)
        t_bc = bld.op("tensor", lambda e: e.matmul(pb[4][:], onesf[0:1, :], strow[0:1, :],
                                                   start=True, stop=True), s_te)
        bld.wait("vector", s_te, t_bc)
        t_i1 = bld.op("vector", lambda e: e.tensor_copy(out=inv1B[:], in_=pb[4][:]), s_ve)
        bld.wait("tensor", s_ve, t_i1)

        gmm = [None] * NC_FF
        umm = [None] * NC_FF
        gev = [None] * NC_FF
        uev = [None] * NC_FF
        xev = [None] * NC_FF
        bld.wait("vector", s_ve)
        gu_ld = [None] * NC_FF
        for oc in range(NC_FF):
            sl = oc % 2
            if oc >= 2:
                bld.wait("sync", s_te, umm[oc - 2])
            bld.dma("sync", arena[:, sl * 2048:(sl + 1) * 2048], wgT_scr[oc], s_gu[sl])
            gu_ld[oc] = bld.dma("sync", arena[:, (2 + sl) * 2048:(3 + sl) * 2048], wuT_scr[oc], s_gu[sl])
            bld.wait("tensor", s_gu[sl], gu_ld[oc])
            if oc >= 2:
                bld.wait("tensor", s_ve, xev[oc - 2])
            for hc in range(NC_DM):
                tv = bld.op("tensor", lambda e, hc=hc, sl=sl: e.matmul(
                    pb[sl][:], arena[:, sl * 2048 + hc * 128: sl * 2048 + (hc + 1) * 128],
                    xq_f[:, hc * 512:(hc + 1) * 512],
                    start=(hc == 0), stop=(hc == NC_DM - 1), skip_group_check=True),
                    s_te if hc == NC_DM - 1 else None)
                if hc == NC_DM - 1:
                    gmm[oc] = tv
            for hc in range(NC_DM):
                tv = bld.op("tensor", lambda e, hc=hc, sl=sl: e.matmul(
                    pb[2 + sl][:], arena[:, (2 + sl) * 2048 + hc * 128: (2 + sl) * 2048 + (hc + 1) * 128],
                    xq_f[:, hc * 512:(hc + 1) * 512],
                    start=(hc == 0), stop=(hc == NC_DM - 1), skip_group_check=True),
                    s_te if hc == NC_DM - 1 else None)
                if hc == NC_DM - 1:
                    umm[oc] = tv
            bld.wait("vector", s_te, gmm[oc])
            bld.wait("vector", s_se)
            bld.op("vector", lambda e, sl=sl: e.scalar_tensor_tensor(
                out=tu[:, sl * 512:(sl + 1) * 512], in0=pb[sl][:], scalar=isw_g[:], in1=inv1B[:],
                op0=ALU.mult, op1=ALU.mult), s_ve)
            bld.wait("scalar", s_ve)
            gev[oc] = bld.op("scalar", lambda e, sl=sl: e.activation(
                out=gs[:, sl * 512:(sl + 1) * 512], in_=tu[:, sl * 512:(sl + 1) * 512], func=AF.Silu), s_se)
            bld.wait("vector", s_te, umm[oc])
            bld.wait("vector", s_se, gev[oc])
            uev[oc] = bld.op("vector", lambda e, sl=sl: e.scalar_tensor_tensor(
                out=tu[:, sl * 512:(sl + 1) * 512], in0=pb[2 + sl][:], scalar=isw_u[:], in1=inv1B[:],
                op0=ALU.mult, op1=ALU.mult), s_ve)
            xev[oc] = bld.op("vector", lambda e, oc=oc, sl=sl: e.tensor_tensor(
                out=Xs[:, oc * 512:(oc + 1) * 512], in0=gs[:, sl * 512:(sl + 1) * 512],
                in1=tu[:, sl * 512:(sl + 1) * 512], op=ALU.mult), s_ve)

        spoint(1)
        dump_slots(1, list(range(NC_FF)))
        sa_ev = [None] * NC_FF
        for oc in range(NC_FF):
            bank = 4 + (oc % 2)
            if oc >= 2:
                bld.wait("tensor", s_ve, sa_ev[oc - 2])
            bld.wait("tensor", s_ve, xev[oc])
            bld.op("tensor", lambda e, oc=oc, bank=bank: e.matmul(
                pb[bank][:], h128[:], Xs[:, oc * 512:(oc + 1) * 512],
                start=True, stop=True, skip_group_check=True), s_te)
            bld.wait("vector", s_te)
            sa_ev[oc] = bld.op("vector", lambda e, oc=oc, bank=bank: e.tensor_copy(
                out=Xs[:, oc * 512:(oc + 1) * 512], in_=pb[bank][:]), s_ve)

        spoint(2)
        dump_slots(2, list(range(NC_FF)))
        def slot(pos_idx):
            return Xs[:, pos_idx * 512:(pos_idx + 1) * 512]

        pos = list(range(NC_FF))
        spare = [NC_FF, NC_FF + 1]
        for sstride in (32, 16, 8, 4, 2, 1):
            for i0 in range(NC_FF):
                if i0 & sstride:
                    continue
                j0 = i0 | sstride
                pa, pbx = pos[i0], pos[j0]
                ns0, ns1 = spare
                bld.op("vector", lambda e, a=pa, b=pbx, o=ns0: e.tensor_tensor(
                    out=slot(o), in0=slot(a), in1=slot(b), op=ALU.add))
                bld.op("vector", lambda e, a=pa, b=pbx, o=ns1: e.tensor_tensor(
                    out=slot(o), in0=slot(a), in1=slot(b), op=ALU.subtract))
                pos[i0], pos[j0] = ns0, ns1
                spare = [pa, pbx]
        bld.op("vector", lambda e: e.engine_nop(), s_ve)
        spoint(3)

        if True:
            dump_slots(3, list(pos))
        bld.op("vector", lambda e: e.memset(maxC[:], 0.0))
        ab_t = [None] * NC_FF
        mx_t = [None] * NC_FF
        for oc in range(NC_FF):
            sl = oc % 2
            if oc >= 2:
                bld.wait("scalar", s_sm, mx_t[oc - 2])
            bld.wait("scalar", s_ve)
            ab_t[oc] = bld.op("scalar", lambda e, oc=oc, sl=sl: e.activation(
                out=gs[:, sl * 512:(sl + 1) * 512], in_=slot(pos[oc]), func=AF.Abs), s_se)
            bld.wait("vector", s_se, ab_t[oc])
            mx_t[oc] = bld.op("vector", lambda e, sl=sl: e.tensor_tensor(
                out=maxC[:], in0=maxC[:], in1=gs[:, sl * 512:(sl + 1) * 512], op=ALU.max), s_sm)
        bld.op("vector", lambda e: e.engine_nop(), s_ve)
        bld.wait("tensor", s_ve)
        for i in range(SUB):
            bld.op("tensor", lambda e, i=i: e.transpose(
                pb[4][:, i * 128:(i + 1) * 128], maxC[:, i * 128:(i + 1) * 128], eyef[:]),
                s_te if i == SUB - 1 else None)
        bld.wait("vector", s_te)
        for i in range(SUB):
            pp("vector", lambda e, i=i: e.tensor_reduce(
                out=col(C_MRAW), in_=pb[4][:, i * 128:(i + 1) * 128], axis=AX.X, op=ALU.max))
            pp("vector", lambda e: e.tensor_scalar(out=col(C_M), in0=col(C_MRAW), scalar1=CF,
                                                   scalar2=EPS_Q, op0=ALU.mult, op1=ALU.max))
            pp("vector", lambda e: e.reciprocal(out=col(C_RS), in_=col(C_M)))
            pp("vector", lambda e, i=i: e.tensor_scalar(out=s2c[:, i:i + 1], in0=col(C_RS), scalar1=128.0 * CF,
                                                        scalar2=None, op0=ALU.mult))
            pp("vector", lambda e: e.tensor_scalar(out=col(C_SACT), in0=col(C_M), scalar1=1.0 / 128.0,
                                                   scalar2=None, op0=ALU.mult))
            pp("vector", lambda e, i=i: e.tensor_scalar(out=cs2[:, i:i + 1], in0=col(C_SACT),
                                                        scalar1=isw_d[:], scalar2=None, op0=ALU.mult))
        bld.wait("tensor", s_pp)
        bld.wait("tensor", s_ve)
        for i in range(SUB):
            bld.op("tensor", lambda e, i=i: e.transpose(
                p1[0:1, i * 128:(i + 1) * 128], s2c[:, i:i + 1], eyef[:]))
        for i in range(SUB):
            bld.op("tensor", lambda e, i=i: e.transpose(
                pb[5][0:1, i * 128:(i + 1) * 128], cs2[:, i:i + 1], eyef[:]),
                s_te if i == SUB - 1 else None)
        bld.wait("vector", s_te)
        t_c1 = bld.op("vector", lambda e: e.tensor_copy(out=strow[0:1, :], in_=p1[0:1, :]), s_ve)
        bld.wait("tensor", s_ve, t_c1)
        t_b1 = bld.op("tensor", lambda e: e.matmul(pb[4][:], onesf[0:1, :], strow[0:1, :],
                                                   start=True, stop=True), s_te)
        bld.wait("vector", s_te, t_b1)
        bld.op("vector", lambda e: e.tensor_copy(out=s2B[:], in_=pb[4][:]))
        t_c2 = bld.op("vector", lambda e: e.tensor_copy(out=strow[0:1, :], in_=pb[5][0:1, :]), s_ve)
        bld.wait("tensor", s_ve, t_c2)
        t_b2 = bld.op("tensor", lambda e: e.matmul(pb[4][:], onesf[0:1, :], strow[0:1, :],
                                                   start=True, stop=True), s_te)
        bld.wait("vector", s_te, t_b2)
        bld.op("vector", lambda e: e.tensor_copy(out=cs2B[:], in_=pb[4][:]), s_ve)

        for oc in range(NC_FF):
            bld.op("vector", lambda e, oc=oc: e.tensor_tensor(
                out=uq[:], in0=slot(pos[oc]), in1=s2B[:], op=ALU.mult))
            bld.op("vector", lambda e: e.tensor_copy(out=k0q[:], in_=uq[:]))
            bld.op("vector", lambda e: e.tensor_tensor(out=dq[:], in0=uq[:], in1=k0q[:], op=ALU.subtract))
            bld.op("vector", lambda e: e.scalar_tensor_tensor(
                out=tu[:, 0:512], in0=dq[:], scalar=0.5, in1=k0q[:], op0=ALU.is_gt, op1=ALU.add))
            bld.op("vector", lambda e: e.scalar_tensor_tensor(
                out=uq[:], in0=dq[:], scalar=-0.5, in1=tu[:, 0:512], op0=ALU.is_ge, op1=ALU.add))
            bld.op("vector", lambda e: e.tensor_scalar(
                out=dq[:], in0=uq[:], scalar1=128.0, scalar2=-127.0, op0=ALU.min, op1=ALU.max))
            bld.op("vector", lambda e, oc=oc: e.tensor_scalar(
                out=slot(pos[oc]), in0=dq[:], scalar1=1.0, scalar2=None, op0=ALU.subtract), s_ve)

        spoint(4)
        dump_slots(4, list(pos))
        if dbgS is not None and (cur_tt[0] == dump_tt or dump_tt == -2) and 7 in DUMP_MASK:
            bld.wait("vector", s_pp)
            bld.op("vector", lambda e: e.tensor_copy(out=stt[:, 20:21], in_=isw_g[:]))
            bld.op("vector", lambda e: e.tensor_copy(out=stt[:, 21:22], in_=isw_u[:]))
            bld.op("vector", lambda e: e.tensor_copy(out=stt[:, 22:23], in_=isw_d[:]), s_ve)
            bld.wait("sync", s_ve)
            bld.dma("sync", dbgS[:, 0:4], inv1[:], s_st)
            bld.dma("sync", dbgS[:, 4:8], cs2[:], s_st)
            bld.dma("sync", dbgS[:, 8:12], s2c[:], s_st)
            bld.dma("sync", dbgS[:, 12:16], inv1B[:, 0:4], s_st)
            bld.dma("sync", dbgS[:, 16:20], s2B[:, 0:4], s_st)
            bld.dma("sync", dbgS[:, 20:24], stt[:, 20:24], s_st)
            bld.dma("sync", dbgS[:, 24:28], cs2B[:, 0:4], s_st)
            bld.wait("sync", s_st)
        spoint(5)
        dmm = [None] * NC_DM
        dev = [None] * NC_DM
        bld.wait("sync", s_te)
        bld.wait("tensor", s_ve)
        dn_ld = [None] * NC_DM
        for dm in range(NC_DM):
            sl = dm % 2
            dbuf = (arena[:, 0:FF] if sl == 0 else dslab2[:, 0:FF])
            if dm >= 2:
                bld.wait("sync", s_te, dmm[dm - 2])
            dn_ld[dm] = bld.dma("sync", dbuf, wdT_scr[dm], s_dn[dm % 2])
            bld.wait("tensor", s_dn[dm % 2], dn_ld[dm])
            if dm >= 2:
                bld.wait("tensor", s_ve, dev[dm - 2])
            for ic in range(NC_FF):
                tv = bld.op("tensor", lambda e, ic=ic, sl=sl, dbuf=dbuf: e.matmul(
                    pb[sl][:], dbuf[:, ic * 128:(ic + 1) * 128],
                    slot(pos[ic]), start=(ic == 0), stop=(ic == NC_FF - 1), skip_group_check=True),
                    s_te if ic == NC_FF - 1 else None)
                if ic == NC_FF - 1:
                    dmm[dm] = tv
            bld.wait("vector", s_te, dmm[dm])
            dev[dm] = bld.op("vector", lambda e, dm=dm, sl=sl: e.tensor_tensor(
                out=brD[:, dm * 512:(dm + 1) * 512], in0=pb[sl][:], in1=cs2B[:], op=ALU.mult), s_ve)

        spoint(6)
        if _dump_on(5):
            bld.wait("sync", s_ve)
            bld.dma("sync", dbgF[5][:, 0:NC_DM * 512], brD[:], s_st)
            bld.wait("sync", s_st)
            for eng in ("vector", "scalar", "tensor"):
                bld.wait(eng, s_st)
        for i in range(SUB):
            bld.wait("sync", s_ve)
            bld.wait("sync", s_st)
            bld.wait("sync", s_se)
            t_s1 = bld.dma("sync", s1f[:], s1_scr[i], s_ld)
            bld.wait("tensor", s_ve)
            for cg in range(4):
                bld.wait("tensor", s_ve)
                for j in range(4):
                    dm = cg * 4 + j
                    bld.op("tensor", lambda e, dm=dm, j=j, i=i: e.transpose(
                        pbf[:, j * 128:(j + 1) * 128],
                        brD[:, dm * 512 + i * 128: dm * 512 + (i + 1) * 128], eyeb[:]),
                        s_te if j == 3 else None)
                bld.wait("vector", s_te)
                bld.op("vector", lambda e, cg=cg: e.tensor_copy(
                    out=br_sub.rearrange("p (c f) -> p c f", c=NC_DM)[:, cg * 4:(cg + 1) * 4, :],
                    in_=pbf.rearrange("p (j f) -> p j f", j=4)[:]), s_ve)
            bld.wait("vector", s_ld, t_s1)
            bld.wait("vector", s_se)
            bld.op("vector", lambda e: e.tensor_tensor(
                out=sq_f[:], in0=s1f[:], in1=br_sub[:], op=ALU.add), s_ve)
            ln_block(sq_f[:], s1f[:], i)
            bld.wait("sync", s_se)
            bld.dma("sync", out[(TT * TILE + i * 128):(TT * TILE + (i + 1) * 128), :], s1f[:], s_st)



    for TT in range(g.get("n_tiles", NT)):
        cur_tt[0] = TT
        lvlAt = ExitStack()
        stream1 = sbufB(lvlAt, f"stream1_{TT}", SUB * DM, F32)
        v_sb = sbufB(lvlAt, f"v_sb_{TT}", KT * 512)
        k_sb = sbufB(lvlAt, f"k_sb_{TT}", NKV * S)
        q_sb = sbufB(lvlAt, f"q_sb_{TT}", NH * 512)
        pT = sbufB(lvlAt, f"pT_{TT}", 2 * 512)
        sumC = sbufB(lvlAt, f"sumC_{TT}", 512)
        OT = sbufB(lvlAt, f"OT_{TT}", NH * 512)
        wos = sbufB(lvlAt, f"wos_{TT}", 2 * DM)
        brF = sbufB(lvlAt, f"brF_{TT}", NC_DM * 512)
        recipB = sbufB(lvlAt, f"recipB_{TT}", 512, F32)
        emit_attn(TT, stream1, v_sb, k_sb, q_sb, pT, sumC, OT, wos, brF, recipB)
        bld.wait("sync", s_st)
        bld.barrier(s_ph)
        lvlAt.close()

        lvlF = ExitStack()
        Xs = sbufB(lvlF, f"Xs_{TT}", 66 * 512)
        Xs_cur[0] = Xs
        xq_f = sbufB(lvlF, f"xq_f_{TT}", NC_DM * 512)
        arena = sbufB(lvlF, f"arena_{TT}", 4 * 2048)
        dslab2 = sbufB(lvlF, f"dslab2_{TT}", FF)
        k0i_t = sbufB(lvlF, f"k0i_t_{TT}", DM, I32)
        s1f = sbufB(lvlF, f"s1f_{TT}", DM, F32)
        uq = sbufB(lvlF, f"uq_{TT}", 512, F32)
        k0q = sbufB(lvlF, f"k0q_{TT}", 512, I32)
        dq = sbufB(lvlF, f"dq_{TT}", 512, F32)
        gs = sbufB(lvlF, f"gs_{TT}", 2 * 512)
        tu = sbufB(lvlF, f"tu_{TT}", 2 * 512, F32)
        maxC = sbufB(lvlF, f"maxC_{TT}", 512, F32)
        brD = sbufB(lvlF, f"brD_{TT}", NC_DM * 512)
        emit_ffn(TT, Xs, xq_f, arena, dslab2, k0i_t, s1f, uq, k0q, dq, gs, tu, maxC, brD)
        bld.wait("sync", s_st)
        bld.barrier(s_ph)
        lvlF.close()

    bld.wait("sync", s_st)
    lvlB.close()



def _consts():
    h128 = _hadamard(128).astype(ml_dtypes.bfloat16)
    eyeb = np.eye(128).astype(ml_dtypes.bfloat16)
    eyef = np.eye(128, dtype=np.float32)
    onesb = np.ones((128, 128), dtype=ml_dtypes.bfloat16)
    onesf = np.ones((128, 128), dtype=np.float32)
    return h128, eyeb, eyef, onesb, onesf


def make_in_maps(x, wq_, wk_, wv_, wo_, bq_, bk_, bv_, bo_, wg_, wu_, wd_):
    h128, eyeb, eyef, onesb, onesf = _consts()
    common = dict(
        wq=np.ascontiguousarray(wq_), wk=np.ascontiguousarray(wk_),
        wv=np.ascontiguousarray(wv_), wo=np.ascontiguousarray(wo_),
        bq=np.ascontiguousarray(bq_), bk=np.ascontiguousarray(bk_),
        bv=np.ascontiguousarray(bv_), bo=np.ascontiguousarray(bo_),
        wg=np.ascontiguousarray(wg_), wu=np.ascontiguousarray(wu_),
        wd=np.ascontiguousarray(wd_),
        h128=h128, eyeb=eyeb, eyef=eyef, onesb=onesb, onesf=onesf,
    )
    in_maps = []
    for core in range(8):
        b, half = core // 2, core % 2
        m = dict(common)
        m["x_all"] = np.ascontiguousarray(x[b])
        m["x_own"] = np.ascontiguousarray(x[b, half * TOK:(half + 1) * TOK, :])
        in_maps.append(m)
    return in_maps


def kernel(L0_attn_inputs_q, L0_attn_wq, L0_attn_bq, L0_attn_wk, L0_attn_bk,
           L0_attn_wv, L0_attn_bv, L0_attn_wo, L0_attn_bo,
           L0_ffn_w_gate, L0_ffn_w_up, L0_ffn_w_down):
    x = np.ascontiguousarray(L0_attn_inputs_q)
    in_maps = make_in_maps(x, L0_attn_wq, L0_attn_wk, L0_attn_wv, L0_attn_wo,
                           L0_attn_bq, L0_attn_bk, L0_attn_bv, L0_attn_bo,
                           L0_ffn_w_gate, L0_ffn_w_up, L0_ffn_w_down)
    nc = build_program()
    res = run_bass_kernel_spmd(nc, in_maps, core_ids=list(range(8)))
    outp = np.empty((4, S, DM), dtype=np.float32)
    for core in range(8):
        b, half = core // 2, core % 2
        outp[b, half * TOK:(half + 1) * TOK, :] = np.asarray(res.results[core]["out"])
    return outp


# version 10
# speedup vs baseline: 1.0199x; 1.0182x over previous
import numpy as np
import ml_dtypes
from contextlib import ExitStack

import concourse.bass as bass
import concourse.mybir as mybir
from concourse.bass_utils import run_bass_kernel_spmd

F32 = mybir.dt.float32
BF16 = mybir.dt.bfloat16
I32 = mybir.dt.int32
AF = mybir.ActivationFunctionType
ALU = mybir.AluOpType
AX = mybir.AxisListType

S = 4096
DM = 2048
NH = 16
NKV = 4
HD = 128
FF = 8192
TOK = 2048
TILE = 512
NT = TOK // TILE
SUB = TILE // 128
NST_ALL = S // TILE
NC_DM = DM // 128
NC_FF = FF // 128
KT = S // 128

SERIAL_POINTS = set()
DUMP_MASK = set(range(7))

SM = 1.0 / float(np.sqrt(128.0))
CF = float(np.float32(1.0 / np.sqrt(FF)))
EPS_Q = 1e-5
EPS_LN = 1e-5


def _hadamard(n):
    h = np.array([[1.0]])
    while h.shape[0] < n:
        h = np.block([[h, h], [h, -h]])
    return h


class Bld:
    ENGS = ("tensor", "vector", "scalar", "sync", "gpsimd")

    def __init__(self, nc, stack):
        self.nc = nc
        self.stack = stack
        self.ops = []
        self.semv = {}
        self.ord_sem = {}
        self.ord_cnt = {}
        for eng in ("vector", "scalar", "tensor"):
            self.ord_sem[eng] = None

    def sem(self, name):
        h = self.stack.enter_context(self.nc.semaphore(name))
        self.semv[id(h)] = 0
        return h

    def on(self, eng, fn):
        self.ops.append((eng, fn))

    def bump(self, sem, k=1):
        self.semv[id(sem)] += k
        return self.semv[id(sem)]

    def val(self, sem):
        return self.semv[id(sem)]

    def dma(self, q, out, in_, sem):
        tgt = self.bump(sem, 16)
        self.on(q, lambda e, o=out, i=in_, s=sem: e.dma_start(out=o, in_=i).then_inc(s, 16))
        return tgt

    def wait(self, eng, sem, v=None):
        v = self.val(sem) if v is None else v
        if v > 0:
            self.on(eng, lambda e, s=sem, vv=v: e.wait_ge(s, vv))
        return v

    def init_ord(self):
        self.last_marker = {}
        for eng in ("vector", "scalar"):
            self.ord_sem[eng] = self.sem(f"s_ord_{eng}")
            self.last_marker[eng] = None

    def op(self, eng, fn, sem=None, k=1):
        osem = self.ord_sem.get(eng)
        prev = self.last_marker.get(eng) if osem is not None else None
        if sem is not None:
            tgt = self.bump(sem, k)
            marker = (sem, tgt)
        else:
            tgt = None
            marker = None
        if osem is not None:
            if marker is None:
                marker = (osem, self.bump(osem, 1))
            self.last_marker[eng] = marker
            msem, mval = marker
            use_ord = (sem is None)

        def g(e, f=fn, s=sem, kk=k, prev=prev, osem=osem):
            if prev is not None:
                e.wait_ge(prev[0], prev[1])
            ins = f(e)
            if osem is not None and s is None:
                ins.then_inc(osem, 1)
            if s is not None:
                ins.then_inc(s, kk)
        self.on(eng, g)
        return tgt

    def barrier(self, sem):
        tgt = self.val(sem) + 5
        for eng in self.ENGS:
            self.bump(sem, 1)
            self.on(eng, lambda e, s=sem: e.nop().then_inc(s, 1))
        for eng in self.ENGS:
            self.wait(eng, sem, tgt)
        return tgt

    def run(self, block):
        def body(eng_name):
            def f(e):
                for name, fn in self.ops:
                    if name == eng_name:
                        fn(e)
            return f
        block.tensor(body("tensor"))
        block.vector(body("vector"))
        block.scalar(body("scalar"))
        block.sync(body("sync"))
        block.gpsimd(body("gpsimd"))


def build_program(debug_phase=None):
    nc = bass.Bass("TRN2", target_bir_lowering=False)
    stack = ExitStack()

    x_all = nc.dram_tensor("x_all", [S, DM], F32, kind="ExternalInput").ap()
    x_own = nc.dram_tensor("x_own", [TOK, DM], F32, kind="ExternalInput").ap()
    wq = nc.dram_tensor("wq", [DM, DM], F32, kind="ExternalInput").ap()
    wk = nc.dram_tensor("wk", [DM, 512], F32, kind="ExternalInput").ap()
    wv = nc.dram_tensor("wv", [DM, 512], F32, kind="ExternalInput").ap()
    wo = nc.dram_tensor("wo", [DM, DM], F32, kind="ExternalInput").ap()
    bq = nc.dram_tensor("bq", [DM], F32, kind="ExternalInput").ap()
    bk = nc.dram_tensor("bk", [512], F32, kind="ExternalInput").ap()
    bv = nc.dram_tensor("bv", [512], F32, kind="ExternalInput").ap()
    bo = nc.dram_tensor("bo", [DM], F32, kind="ExternalInput").ap()
    wg = nc.dram_tensor("wg", [FF, DM], F32, kind="ExternalInput").ap()
    wu = nc.dram_tensor("wu", [FF, DM], F32, kind="ExternalInput").ap()
    wd = nc.dram_tensor("wd", [DM, FF], F32, kind="ExternalInput").ap()
    h128_in = nc.dram_tensor("h128", [128, 128], BF16, kind="ExternalInput").ap()
    eyeb_in = nc.dram_tensor("eyeb", [128, 128], BF16, kind="ExternalInput").ap()
    eyef_in = nc.dram_tensor("eyef", [128, 128], F32, kind="ExternalInput").ap()
    onesb_in = nc.dram_tensor("onesb", [128, 128], BF16, kind="ExternalInput").ap()
    onesf_in = nc.dram_tensor("onesf", [128, 128], F32, kind="ExternalInput").ap()

    out = nc.dram_tensor("out", [TOK, DM], F32, kind="ExternalOutput").ap()

    dbg = "ExternalOutput" if debug_phase == "A" else "Internal"
    q_scr = nc.dram_tensor("q_scr", [NH, NT, 128, TILE], BF16, kind=dbg).ap()
    k_scr = nc.dram_tensor("k_scr", [NKV, NST_ALL, 128, TILE], BF16, kind=dbg).ap()
    v_scr = nc.dram_tensor("v_scr", [KT, 128, 512], BF16, kind=dbg).ap()
    wgT_scr = nc.dram_tensor("wgT_scr", [NC_FF, 128, DM], BF16, kind=dbg).ap()
    wuT_scr = nc.dram_tensor("wuT_scr", [NC_FF, 128, DM], BF16, kind=dbg).ap()
    wdT_scr = nc.dram_tensor("wdT_scr", [NC_DM, 128, FF], BF16, kind=dbg).ap()
    woB_scr = nc.dram_tensor("woB_scr", [NC_DM, 128, DM], BF16, kind=dbg).ap()
    s1_scr = nc.dram_tensor("s1_scr", [SUB, 128, DM], F32,
                            kind="ExternalOutput" if debug_phase in ("B1", "B4", "BX") else "Internal").ap()
    if debug_phase == "A":
        dbgA = nc.dram_tensor("dbgA", [128, 256], F32, kind="ExternalOutput").ap()
    dbgF = None
    dbgS = None
    if debug_phase in ("B1", "B4", "BX"):
        dbgF = nc.dram_tensor("dbgF", [6, 128, 66 * 512], BF16, kind="ExternalOutput").ap()
        dbgS = nc.dram_tensor("dbgS", [128, 64], F32, kind="ExternalOutput").ap()

    with stack:
        bld = Bld(nc, stack)
        s_ld = bld.sem("s_ld")
        s_ld2 = bld.sem("s_ld2")
        s_ld3 = bld.sem("s_ld3")
        s_st = bld.sem("s_st")
        s_te = bld.sem("s_te")
        s_ve = bld.sem("s_ve")
        s_se = bld.sem("s_se")
        s_ph = bld.sem("s_ph")
        s_sp = [bld.sem("s_spA"), bld.sem("s_spB")]
        bld.init_ord()

        def sbuf(name, cols, dt=BF16, rows=128, st=stack):
            return st.enter_context(nc.sbuf_tensor(name, [rows, cols], dt))

        h128 = sbuf("c_h128", 128)
        eyeb = sbuf("c_eyeb", 128)
        eyef = sbuf("c_eyef", 128, F32)
        onesb = sbuf("c_onesb", 128)
        onesf = sbuf("c_onesf", 128, F32)
        bq_t = sbuf("bq_t", NH, F32)
        bk_t = sbuf("bk_t", NKV, F32)
        bo_t = sbuf("bo_t", NC_DM, F32)
        bv_bc = sbuf("bv_bc", 512, F32)
        negB = sbuf("negB", NH * NT, F32)
        eps_t = sbuf("eps_t", 1, F32)
        neg1_t = sbuf("neg1_t", 1, F32)
        sw_g = sbuf("sw_g", 1, F32)
        sw_u = sbuf("sw_u", 1, F32)
        sw_d = sbuf("sw_d", 1, F32)
        isw_g = sbuf("isw_g", 1, F32)
        isw_u = sbuf("isw_u", 1, F32)
        isw_d = sbuf("isw_d", 1, F32)
        strow = sbuf("strow", 512, F32, rows=1)
        st1 = sbuf("st1", 16, F32)
        qmax2 = sbuf("qmax2", NH * NT, F32, rows=1)
        kmax2 = sbuf("kmax2", NKV, F32, rows=1)

        pb = [stack.enter_context(nc.psum_tensor(f"pb{i}", [128, 512], F32)) for i in range(6)]
        pbf = stack.enter_context(nc.psum_tensor("pbf", [128, 512], BF16))
        p1 = stack.enter_context(nc.psum_tensor("p1", [1, 512], F32))

        for t in (h128_in, eyeb_in, eyef_in, onesb_in, onesf_in):
            pass
        bld.dma("sync", h128[:], h128_in, s_ld)
        bld.dma("sync", eyeb[:], eyeb_in, s_ld)
        bld.dma("sync", eyef[:], eyef_in, s_ld)
        bld.dma("sync", onesb[:], onesb_in, s_ld)
        bld.dma("sync", onesf[:], onesf_in, s_ld)
        for h in range(NH):
            bld.dma("sync", bq_t[:, h:h + 1], bq[h * 128:(h + 1) * 128].rearrange("(p o) -> p o", o=1), s_ld)
            bld.dma("sync", bo_t[:, h:h + 1], bo[h * 128:(h + 1) * 128].rearrange("(p o) -> p o", o=1), s_ld)
        for h in range(NKV):
            bld.dma("sync", bk_t[:, h:h + 1], bk[h * 128:(h + 1) * 128].rearrange("(p o) -> p o", o=1), s_ld)
        bld.dma("sync", strow[0:1, :], bv.rearrange("(o f) -> o f", o=1), s_ld)
        tgt_consts = bld.val(s_ld)
        bld.wait("vector", s_ld, tgt_consts)
        bld.op("vector", lambda e: e.memset(eps_t[:], EPS_LN))
        bld.op("vector", lambda e: e.memset(neg1_t[:], -1.0))
        bld.op("vector", lambda e: e.memset(kmax2[:], 0.0), s_ve)
        bld.wait("scalar", s_ld, tgt_consts)
        bld.wait("tensor", s_ld, tgt_consts)
        bld.op("tensor", lambda e: e.matmul(pb[0][:], onesf[0:1, :], strow[0:1, :], start=True, stop=True), s_te)
        bld.wait("vector", s_te)
        bld.op("vector", lambda e: e.tensor_copy(out=bv_bc[:], in_=pb[0][:]), s_ve)

        lvlA = ExitStack()
        wqb = lvlA.enter_context(nc.sbuf_tensor("wqb", [128, NC_DM * DM], BF16))
        wkb = lvlA.enter_context(nc.sbuf_tensor("wkb", [128, NC_DM * 512], BF16))
        wvb = lvlA.enter_context(nc.sbuf_tensor("wvb", [128, NC_DM * 512], BF16))
        xa_f = [lvlA.enter_context(nc.sbuf_tensor(f"xa_f{i}", [128, DM], F32)) for i in range(2)]
        xa_b = lvlA.enter_context(nc.sbuf_tensor("xa_b", [128, DM], BF16))
        xf = lvlA.enter_context(nc.sbuf_tensor("xf", [128, NC_DM * TILE], BF16))
        ev_b = [lvlA.enter_context(nc.sbuf_tensor(f"ev_b{i}", [128, 512], BF16)) for i in range(2)]
        sq_b = lvlA.enter_context(nc.sbuf_tensor("sq_b", [128, 512], BF16))
        xf_view = xf.rearrange("p (c f) -> p c f", c=NC_DM)
        pbf_view = pbf.rearrange("p (j f) -> p j f", j=4)

        for c in range(NC_DM):
            bld.dma("gpsimd", wqb[:, c * DM:(c + 1) * DM], wq[c * 128:(c + 1) * 128, :], s_ld3)
            bld.dma("gpsimd", wkb[:, c * 512:(c + 1) * 512], wk[c * 128:(c + 1) * 128, :], s_ld3)
            bld.dma("gpsimd", wvb[:, c * 512:(c + 1) * 512], wv[c * 128:(c + 1) * 128, :], s_ld3)
        tgt_w = bld.val(s_ld3)
        bld.wait("tensor", s_ld3, tgt_w)

        def emit_x_sub(src_rows, i_sub, buf_i):
            bld.wait("sync", s_ve)
            t_l = bld.dma("sync", xa_f[buf_i][:], src_rows, s_ld)
            bld.wait("vector", s_ld, t_l)
            bld.wait("vector", s_te)
            bld.op("vector", lambda e, b=buf_i: e.tensor_copy(out=xa_b[:], in_=xa_f[b][:]), s_ve)
            bld.wait("tensor", s_ve)
            for cg in range(4):
                bld.wait("tensor", s_ve)
                for j in range(4):
                    c = cg * 4 + j
                    bld.op("tensor", lambda e, c=c, j=j: e.transpose(
                        pbf[:, j * 128:(j + 1) * 128], xa_b[:, c * 128:(c + 1) * 128], eyeb[:]),
                        s_te if j == 3 else None)
                bld.wait("vector", s_te)
                bld.op("vector", lambda e, cg=cg, i=i_sub: e.tensor_copy(
                    out=xf_view[:, cg * 4:(cg + 1) * 4, i * 128:(i + 1) * 128], in_=pbf_view[:]), s_ve)

        ev_i = [0]

        def proj_group(bank, emit_mms, bias_col_ap, dst, norm_col, kind):
            bld.wait("tensor", s_se)
            bld.wait("tensor", s_ve)
            emit_mms(bank)
            b_ev = ev_b[ev_i[0] % 2]
            ev_i[0] += 1
            bld.wait("scalar", s_te)
            bld.wait("scalar", s_st)
            if kind == "v":
                bld.wait("vector", s_te)
                bld.wait("vector", s_st)
                bld.op("vector", lambda e, b=bank, o=b_ev: e.tensor_tensor(
                    out=o[:], in0=b[:], in1=bv_bc[:], op=ALU.add), s_ve)
                bld.wait("sync", s_ve)
                bld.dma("sync", dst, b_ev[:], s_st)
                return
            bld.op("scalar", lambda e, b=bank, o=b_ev, ba=bias_col_ap: e.activation(
                out=o[:], in_=b[:], func=AF.Identity, bias=ba), s_se)
            bld.wait("sync", s_se)
            bld.dma("sync", dst, b_ev[:], s_st)
            bld.wait("scalar", s_te)
            bld.op("scalar", lambda e, i=b_ev: e.activation(out=sq_b[:], in_=i[:], func=AF.Square), s_se)
            bld.wait("tensor", s_se)
            bld.wait("tensor", s_ve)
            bld.op("tensor", lambda e: e.matmul(p1[0:1, :], onesb[:, 0:1], sq_b[:], start=True, stop=True), s_te)
            bld.wait("vector", s_te)
            if kind == "q":
                bld.op("vector", lambda e, col=norm_col: e.tensor_reduce(
                    out=qmax2[0:1, col:col + 1], in_=p1[0:1, :], axis=AX.X, op=ALU.max), s_ve)
            else:
                bld.op("vector", lambda e: e.tensor_reduce(
                    out=st1[0:1, 0:1], in_=p1[0:1, :], axis=AX.X, op=ALU.max), s_ve)
                bld.wait("vector", s_ve)
                bld.op("vector", lambda e, col=norm_col: e.tensor_tensor(
                    out=kmax2[0:1, col:col + 1], in0=kmax2[0:1, col:col + 1], in1=st1[0:1, 0:1],
                    op=ALU.max), s_ve)

        rot = [0]

        def next_bank():
            b = pb[rot[0] % 4]
            rot[0] += 1
            return b

        for st in range(NST_ALL):
            for i in range(SUB):
                emit_x_sub(x_all[(st * TILE + i * 128):(st * TILE + (i + 1) * 128), :], i, i % 2)
            for kv in range(NKV):
                def mm_k(bank, kv=kv):
                    for c in range(NC_DM):
                        bld.op("tensor", lambda e, b=bank, c=c, kv=kv: e.matmul(
                            b[:], wkb[:, c * 512 + kv * 128: c * 512 + (kv + 1) * 128], xf_view[:, c, :],
                            start=(c == 0), stop=(c == NC_DM - 1)), s_te if c == NC_DM - 1 else None)
                proj_group(next_bank(), mm_k, bk_t[:, kv:kv + 1], k_scr[kv, st], kv, "k")
            for i in range(SUB):
                def mm_v(bank, i=i):
                    for c in range(NC_DM):
                        bld.op("tensor", lambda e, b=bank, c=c, i=i: e.matmul(
                            b[:], xf_view[:, c, i * 128:(i + 1) * 128], wvb[:, c * 512:(c + 1) * 512],
                            start=(c == 0), stop=(c == NC_DM - 1)), s_te if c == NC_DM - 1 else None)
                proj_group(next_bank(), mm_v, None, v_scr[st * SUB + i], None, "v")

        for st in range(NT):
            for i in range(SUB):
                emit_x_sub(x_own[(st * TILE + i * 128):(st * TILE + (i + 1) * 128), :], i, i % 2)
            for h in range(NH):
                def mm_q(bank, h=h):
                    for c in range(NC_DM):
                        bld.op("tensor", lambda e, b=bank, c=c, h=h: e.matmul(
                            b[:], wqb[:, c * DM + h * 128: c * DM + (h + 1) * 128], xf_view[:, c, :],
                            start=(c == 0), stop=(c == NC_DM - 1)), s_te if c == NC_DM - 1 else None)
                proj_group(next_bank(), mm_q, bq_t[:, h:h + 1], q_scr[h, st], h * NT + st, "q")

        bld.wait("vector", s_ve)
        for kv in range(NKV):
            bld.op("vector", lambda e, kv=kv: e.tensor_scalar(
                out=qmax2[0:1, kv * 4 * NT:(kv + 1) * 4 * NT], in0=qmax2[0:1, kv * 4 * NT:(kv + 1) * 4 * NT],
                scalar1=kmax2[0:1, kv:kv + 1], scalar2=None, op0=ALU.mult), s_ve)
        bld.wait("scalar", s_ve)
        bld.op("scalar", lambda e: e.activation(out=strow[0:1, 0:NH * NT], in_=qmax2[0:1, :], func=AF.Ln), s_se)
        bld.wait("scalar", s_se)
        bld.op("scalar", lambda e: e.activation(out=strow[0:1, 0:NH * NT], in_=strow[0:1, 0:NH * NT],
                                                func=AF.Exp, scale=0.5), s_se)
        bld.wait("vector", s_se)
        bld.op("vector", lambda e: e.tensor_scalar(
            out=strow[0:1, 0:NH * NT], in0=strow[0:1, 0:NH * NT], scalar1=-SM, scalar2=None,
            op0=ALU.mult), s_ve)
        bld.wait("tensor", s_ve)
        bld.op("tensor", lambda e: e.matmul(pb[4][:, 0:NH * NT], onesf[0:1, :], strow[0:1, 0:NH * NT],
                                            start=True, stop=True), s_te)
        bld.wait("vector", s_te)
        bld.op("vector", lambda e: e.tensor_copy(out=negB[:], in_=pb[4][:, 0:NH * NT]), s_ve)

        bld.wait("sync", s_st)
        bld.wait("sync", s_ld)
        bld.wait("sync", s_ld2)
        bld.wait("sync", s_ld3)
        bld.barrier(s_ph)
        lvlA.close()
        lvlA = ExitStack()

        wo_stage = lvlA.enter_context(nc.sbuf_tensor("wo_stage", [128, DM], F32))
        wo_bf = lvlA.enter_context(nc.sbuf_tensor("wo_bf", [128, DM], BF16))
        for dm in range(NC_DM):
            bld.wait("scalar", s_ve)
            t_l = 0
            for hd in range(NC_DM):
                t_l = bld.dma("scalar", wo_stage[:, hd * 128:(hd + 1) * 128],
                              wo[hd * 128:(hd + 1) * 128, dm * 128:(dm + 1) * 128], s_ld2)
            bld.wait("vector", s_ld2, t_l)
            bld.wait("vector", s_st)
            bld.op("vector", lambda e: e.tensor_copy(out=wo_bf[:], in_=wo_stage[:]), s_ve)
            bld.wait("sync", s_ve)
            bld.dma("sync", woB_scr[dm], wo_bf[:], s_st)

        st_f = [lvlA.enter_context(nc.sbuf_tensor(f"st_f{i}", [128, DM], F32)) for i in range(2)]
        ab_b = [lvlA.enter_context(nc.sbuf_tensor(f"ab_b{i}", [128, DM], BF16)) for i in range(2)]

        def mat_pieces(wsrc):
            n_rows, width = wsrc.shape
            for si in range(n_rows // 128):
                for qj in range(width // DM):
                    yield wsrc[si * 128:(si + 1) * 128, qj * DM:(qj + 1) * DM], si, qj

        mats = [("g", wg, FF * DM, sw_g, isw_g),
                ("u", wu, FF * DM, sw_u, isw_u),
                ("d", wd, DM * FF, sw_d, isw_d)]

        for tag, wsrc, nel, sw_pp, isw_pp in mats:
            pieces = list(mat_pieces(wsrc))
            ab_mm = [None, None]
            ab_t = [None, None]
            for pi, (rows, si, qj) in enumerate(pieces):
                bld.wait("sync", s_se)
                t_l = bld.dma("sync", st_f[pi % 2][:], rows, s_sp[pi % 2])
                bld.wait("scalar", s_sp[pi % 2], t_l)
                if ab_mm[pi % 2] is not None:
                    bld.wait("scalar", s_te, ab_mm[pi % 2])
                ab_t[pi % 2] = bld.op("scalar", lambda e, b=pi % 2: e.activation(
                    out=ab_b[b][:], in_=st_f[b][:], func=AF.Abs), s_se)
                bld.wait("tensor", s_se, ab_t[pi % 2])
                for q in range(4):
                    fq = (pi == 0 and q == 0)
                    lq = (pi == len(pieces) - 1 and q == 3)
                    tv = bld.op("tensor", lambda e, b=pi % 2, q=q, f=fq, l=lq: e.matmul(
                        p1[0:1, :], onesb[:, 0:1], ab_b[b][:, q * 512:(q + 1) * 512],
                        start=f, stop=l), s_te if q == 3 else None)
                    if q == 3:
                        ab_mm[pi % 2] = tv
            bld.wait("vector", s_te)
            bld.op("vector", lambda e: e.tensor_reduce(out=st1[0:1, 0:1], in_=p1[0:1, :], axis=AX.X, op=ALU.add), s_ve)
            bld.wait("vector", s_ve)
            bld.op("vector", lambda e, n=nel: e.tensor_scalar(
                out=st1[0:1, 1:2], in0=st1[0:1, 0:1], scalar1=1.0 / n, scalar2=EPS_Q,
                op0=ALU.mult, op1=ALU.max), s_ve)
            bld.wait("vector", s_ve)
            bld.op("vector", lambda e: e.reciprocal(out=st1[0:1, 2:3], in_=st1[0:1, 1:2]), s_ve)
            bld.wait("vector", s_ve)
            bld.op("vector", lambda e: e.reciprocal(out=st1[0:1, 3:4], in_=st1[0:1, 2:3]), s_ve)
            bld.wait("tensor", s_ve)
            bld.op("tensor", lambda e: e.matmul(pb[4][:, 0:1], onesf[0:1, :], st1[0:1, 2:3], start=True, stop=True))
            bld.op("tensor", lambda e: e.matmul(pb[5][:, 0:1], onesf[0:1, :], st1[0:1, 3:4], start=True, stop=True), s_te)
            bld.wait("vector", s_te)
            bld.op("vector", lambda e, d=sw_pp: e.tensor_copy(out=d[:], in_=pb[4][:, 0:1]))
            bld.op("vector", lambda e, d=isw_pp: e.tensor_copy(out=d[:], in_=pb[5][:, 0:1]), s_ve)
            bld.wait("tensor", s_ve)

        v_f = lvlA.enter_context(nc.sbuf_tensor("v_f", [128, DM], F32))
        a_f = lvlA.enter_context(nc.sbuf_tensor("a_f", [128, DM], F32))
        t_bt = [lvlA.enter_context(nc.sbuf_tensor(f"t_bt{i}", [128, DM], BF16)) for i in range(2)]
        tT_b = [lvlA.enter_context(nc.sbuf_tensor(f"tT_b{i}", [128, DM], BF16)) for i in range(2)]
        guard_sw = bld.val(s_ve)
        bld.wait("vector", s_ve, guard_sw)
        tern_state = {"stt": [None, None], "tr": [None, None], "st": [None, None], "ev": [None, None]}

        def tern_piece(rows, sw_pp, dst, buf_i):
            ts_ = tern_state
            bld.wait("sync", s_ve)
            t_l = bld.dma("sync", st_f[buf_i][:], rows, s_sp[buf_i])
            bld.wait("vector", s_sp[buf_i], t_l)
            bld.op("vector", lambda e, b=buf_i, s=sw_pp: e.tensor_scalar(
                out=v_f[:], in0=st_f[b][:], scalar1=s[:], scalar2=None, op0=ALU.mult))
            bld.op("vector", lambda e: e.tensor_scalar(
                out=a_f[:], in0=v_f[:], scalar1=-0.5, scalar2=None, op0=ALU.is_le))
            if ts_["tr"][buf_i] is not None:
                bld.wait("vector", s_te, ts_["tr"][buf_i])
            ts_["stt"][buf_i] = bld.op("vector", lambda e, b=buf_i: e.scalar_tensor_tensor(
                out=t_bt[b][:], in0=v_f[:], scalar=0.5, in1=a_f[:],
                op0=ALU.is_ge, op1=ALU.subtract), s_ve)
            bld.wait("tensor", s_ve, ts_["stt"][buf_i])
            for cg in range(4):
                if cg > 0:
                    bld.wait("tensor", s_ve)
                for j in range(4):
                    c = cg * 4 + j
                    tv = bld.op("tensor", lambda e, b=buf_i, c=c, j=j: e.transpose(
                        pbf[:, j * 128:(j + 1) * 128], t_bt[b][:, c * 128:(c + 1) * 128], eyeb[:]),
                        s_te if j == 3 else None)
                    if j == 3:
                        ts_["tr"][buf_i] = tv
                bld.wait("vector", s_te, ts_["tr"][buf_i])
                if ts_["st"][buf_i] is not None:
                    bld.wait("vector", s_st, ts_["st"][buf_i])
                ts_["ev"][buf_i] = bld.op("vector", lambda e, b=buf_i, cg=cg: e.tensor_copy(
                    out=tT_b[b][:, cg * 512:(cg + 1) * 512], in_=pbf[:]), s_ve)
            bld.wait("sync", s_ve, ts_["ev"][buf_i])
            ts_["st"][buf_i] = bld.dma("sync", dst, tT_b[buf_i][:], s_st)

        for rows, si, qj in mat_pieces(wg):
            tern_piece(rows, sw_g, wgT_scr[si], (si) % 2)
        for rows, si, qj in mat_pieces(wu):
            tern_piece(rows, sw_u, wuT_scr[si], (si) % 2)
        for rows, si, qj in mat_pieces(wd):
            tern_piece(rows, sw_d, wdT_scr[si].rearrange("p (q f) -> p q f", q=4)[:, qj, :], (si * 4 + qj) % 2)

        t_all_st = bld.val(s_st)
        for eng in Bld.ENGS:
            bld.wait(eng, s_st, t_all_st)
        lvlA.close()
        bld.barrier(s_ph)

        if debug_phase == "A":
            dbg_t = sbuf("dbg_t", 256, F32)
            bld.op("vector", lambda e: e.memset(dbg_t[:], 0.0))
            bld.op("vector", lambda e: e.tensor_copy(out=dbg_t[0:1, 128:192], in_=qmax2[0:1, :]))
            bld.op("vector", lambda e: e.tensor_copy(out=dbg_t[0:1, 192:196], in_=kmax2[0:1, :]))
            bld.op("vector", lambda e: e.tensor_copy(out=dbg_t[:, 0:NH * NT], in_=negB[:]))
            bld.op("vector", lambda e: e.tensor_copy(out=dbg_t[:, 64:65], in_=sw_g[:]))
            bld.op("vector", lambda e: e.tensor_copy(out=dbg_t[:, 65:66], in_=sw_u[:]))
            bld.op("vector", lambda e: e.tensor_copy(out=dbg_t[:, 66:67], in_=sw_d[:]))
            bld.op("vector", lambda e: e.tensor_copy(out=dbg_t[:, 67:68], in_=isw_d[:]))
            bld.op("vector", lambda e: e.tensor_copy(out=dbg_t[:, 68:69], in_=bv_bc[:, 3:4]), s_ve)
            bld.wait("sync", s_ve)
            bld.dma("sync", dbgA, dbg_t[:], s_st)
            bld.wait("sync", s_st)
            z_t = sbuf("z_t", DM, F32)
            bld.op("vector", lambda e: e.memset(z_t[:], 0.0), s_ve)
            bld.wait("sync", s_ve)
            for i in range(TOK // 128):
                bld.dma("sync", out[i * 128:(i + 1) * 128, :], z_t[:], s_st)
            bld.wait("sync", s_st)
            block = stack.enter_context(nc.Block())
            bld.run(block)
            return nc

        build_phase_b(nc, stack, bld, dict(
            n_tiles=(1 if debug_phase == "B1" else NT),
            dump_tt=(0 if debug_phase == "B1" else (NT - 1 if debug_phase == "B4" else (-2 if debug_phase == "BX" else -1))),
            dbgF=dbgF, dbgS=dbgS,
            out=out, x_own=x_own, q_scr=q_scr, k_scr=k_scr, v_scr=v_scr,
            wgT_scr=wgT_scr, wuT_scr=wuT_scr, wdT_scr=wdT_scr, woB_scr=woB_scr,
            s1_scr=s1_scr,
            h128=h128, eyeb=eyeb, eyef=eyef, onesb=onesb, onesf=onesf,
            bo_t=bo_t, negB=negB, eps_t=eps_t, neg1_t=neg1_t,
            isw_g=isw_g, isw_u=isw_u, isw_d=isw_d,
            strow=strow, st1=st1, pb=pb, pbf=pbf, p1=p1,
            s_ld=s_ld, s_ld2=s_ld2, s_ld3=s_ld3, s_st=s_st,
            s_te=s_te, s_ve=s_ve, s_se=s_se, s_ph=s_ph,
        ))
        block = stack.enter_context(nc.Block())
        bld.run(block)
    return nc


def build_phase_b(nc, stack, bld, env):
    g = env
    out, x_own = g["out"], g["x_own"]
    q_scr, k_scr, v_scr = g["q_scr"], g["k_scr"], g["v_scr"]
    wgT_scr, wuT_scr, wdT_scr, woB_scr = g["wgT_scr"], g["wuT_scr"], g["wdT_scr"], g["woB_scr"]
    s1_scr = g["s1_scr"]
    h128, eyeb, eyef, onesb, onesf = g["h128"], g["eyeb"], g["eyef"], g["onesb"], g["onesf"]
    bo_t, negB, eps_t, neg1_t = g["bo_t"], g["negB"], g["eps_t"], g["neg1_t"]
    isw_g, isw_u, isw_d = g["isw_g"], g["isw_u"], g["isw_d"]
    strow, st1, pb, pbf, p1 = g["strow"], g["st1"], g["pb"], g["pbf"], g["p1"]
    s_ld, s_ld2, s_ld3, s_st = g["s_ld"], g["s_ld2"], g["s_ld3"], g["s_st"]
    s_te, s_ve, s_se, s_ph = g["s_te"], g["s_ve"], g["s_se"], g["s_ph"]

    s_sc = bld.sem("s_sc")
    s_ex = bld.sem("s_ex")
    s_pv = bld.sem("s_pv")
    s_sm = bld.sem("s_sm")
    s_pp = bld.sem("s_pp")
    s_wp = [bld.sem("s_wpA"), bld.sem("s_wpB")]
    s_gu = [bld.sem("s_guA"), bld.sem("s_guB")]
    s_dn = [bld.sem("s_dnA"), bld.sem("s_dnB")]

    def sbufB(st_, name, cols, dt=BF16, rows=128):
        return st_.enter_context(nc.sbuf_tensor(name, [rows, cols], dt))

    lvlB = ExitStack()
    xres = [sbufB(lvlB, f"xres{i}", DM, F32) for i in range(2)]
    sq_f = sbufB(lvlB, "sq_f", DM, F32)
    br_sub = sbufB(lvlB, "br_sub", DM, BF16)
    stt = sbufB(lvlB, "stt", 32, F32)
    inv1 = sbufB(lvlB, "inv1", SUB, F32)
    cs2 = sbufB(lvlB, "cs2", SUB, F32)
    s2c = sbufB(lvlB, "s2c", SUB, F32)
    inv1B = sbufB(lvlB, "inv1B", 512, F32)
    s2B = sbufB(lvlB, "s2B", 512, F32)
    cs2B = sbufB(lvlB, "cs2B", 512, F32)

    C_SX, C_SX2, C_MEAN, C_EX2, C_M2T, C_VAR, C_LNV, C_RSTD, C_NMR = range(9)
    C_MRAW, C_M, C_RS, C_SACT = 9, 10, 11, 12

    def col(c):
        return stt[:, c:c + 1]

    def pp(eng, fn):
        bld.wait(eng, s_pp)
        return bld.op(eng, fn, s_pp)

    dbgF, dbgS = g.get("dbgF"), g.get("dbgS")
    dump_tt = g.get("dump_tt", -1)
    global DUMP_MASK
    cur_tt = [0]

    def spoint(k):
        if k not in SERIAL_POINTS:
            return
        for eng in ("vector", "scalar", "tensor"):
            sem = {"vector": s_ve, "scalar": s_se, "tensor": s_te}[eng]
            bld.op(eng, lambda e: e.nop(), sem)
        for eng in ("vector", "scalar", "tensor", "sync"):
            bld.wait(eng, s_ve)
            bld.wait(eng, s_se)
            bld.wait(eng, s_te)

    def _dump_on(k):
        return dbgF is not None and (cur_tt[0] == dump_tt or dump_tt == -2) and k in DUMP_MASK

    def dump_slots(k, pos_list):
        if not _dump_on(k):
            return
        for eng in ("vector", "scalar", "tensor"):
            pass
        bld.wait("sync", s_ve)
        bld.wait("sync", s_se)
        bld.wait("sync", s_te)
        for oc in range(NC_FF):
            bld.dma("sync", dbgF[k][:, oc * 512:(oc + 1) * 512],
                    Xs_cur[0][:, pos_list[oc] * 512:(pos_list[oc] + 1) * 512], s_st)
        bld.wait("sync", s_st)
        for eng in ("vector", "scalar", "tensor"):
            bld.wait(eng, s_st)

    def dump_tile(k, ap_src, width):
        if not _dump_on(k):
            return
        bld.wait("sync", s_ve)
        bld.wait("sync", s_se)
        bld.wait("sync", s_te)
        bld.dma("sync", dbgF[k][:, 0:width], ap_src, s_st)
        bld.wait("sync", s_st)
        for eng in ("vector", "scalar", "tensor"):
            bld.wait(eng, s_st)

    Xs_cur = [None]

    def ln_block(raw_slice, dst_slice, sub_i):
        bld.wait("scalar", s_ve)
        bld.op("scalar", lambda e: e.activation(out=xres[sub_i % 2][:], in_=raw_slice, func=AF.Square), s_se)
        bld.op("vector", lambda e: e.tensor_reduce(out=col(C_SX), in_=raw_slice, axis=AX.X, op=ALU.add), s_ve)
        bld.wait("vector", s_se)
        bld.op("vector", lambda e: e.tensor_reduce(out=col(C_SX2), in_=xres[sub_i % 2][:], axis=AX.X, op=ALU.add), s_ve)
        pp("vector", lambda e: e.tensor_scalar(out=col(C_MEAN), in0=col(C_SX), scalar1=1.0 / DM, scalar2=None, op0=ALU.mult))
        pp("vector", lambda e: e.tensor_scalar(out=col(C_EX2), in0=col(C_SX2), scalar1=1.0 / DM, scalar2=None, op0=ALU.mult))
        pp("vector", lambda e: e.tensor_scalar(out=col(C_M2T), in0=col(C_MEAN), scalar1=col(C_MEAN), scalar2=None, op0=ALU.mult))
        pp("vector", lambda e: e.tensor_tensor(out=col(C_VAR), in0=col(C_EX2), in1=col(C_M2T), op=ALU.subtract))
        bld.wait("scalar", s_pp)
        pp("scalar", lambda e: e.activation(out=col(C_LNV), in_=col(C_VAR), func=AF.Ln, bias=eps_t[:]))
        pp("scalar", lambda e: e.activation(out=col(C_RSTD), in_=col(C_LNV), func=AF.Exp, scale=-0.5))
        bld.wait("vector", s_pp)
        pp("vector", lambda e: e.tensor_scalar(out=col(C_NMR), in0=col(C_RSTD), scalar1=col(C_MEAN), scalar2=-1.0,
                                               op0=ALU.mult, op1=ALU.mult))
        bld.wait("scalar", s_pp)
        bld.op("scalar", lambda e: e.activation(out=dst_slice, in_=raw_slice, func=AF.Identity,
                                                scale=col(C_RSTD), bias=col(C_NMR)), s_se)

    def emit_attn(TT, stream1, v_sb, k_sb, q_sb, pT, sumC, OT, wos, brF, recipB):
        ncol0 = TT
        for kt in range(KT):
            bld.dma("sync", v_sb[:, kt * 512:(kt + 1) * 512], v_scr[kt], s_ld)
        for kv in range(NKV):
            for st in range(NST_ALL):
                bld.dma("scalar", k_sb[:, kv * S + st * 512: kv * S + (st + 1) * 512], k_scr[kv, st], s_ld2)
        for h in range(NH):
            bld.dma("gpsimd", q_sb[:, h * 512:(h + 1) * 512], q_scr[h, TT], s_ld3)
        bld.wait("tensor", s_ld)
        bld.wait("tensor", s_ld2)
        bld.wait("tensor", s_ld3)

        for kv in range(NKV):
            for gq in range(4):
                h = kv * 4 + gq
                ncol = h * NT + TT
                bld.wait("tensor", s_se)
                bld.wait("tensor", s_ve)
                bld.wait("vector", s_te)
                bld.wait("scalar", s_pv)
                bld.wait("scalar", s_sm)
                sc_t = [None] * KT
                ex_t = [None] * KT
                pv_t = [None] * KT
                sm_t = [None] * KT

                SBK = (0, 1, 3)

                def mm_sc(kt, kv=kv, h=h):
                    sc_t[kt] = bld.op("tensor", lambda e, kt=kt: e.matmul(
                        pb[SBK[kt % 3]][:], k_sb[:, kv * S + kt * 128: kv * S + (kt + 1) * 128],
                        q_sb[:, h * 512:(h + 1) * 512], start=True, stop=True,
                        skip_group_check=True), s_sc)

                mm_sc(0)
                mm_sc(1)
                mm_sc(2)
                for kt in range(KT):
                    bld.wait("scalar", s_sc, sc_t[kt])
                    if kt >= 3:
                        bld.wait("scalar", s_pv, pv_t[kt - 3])
                        bld.wait("scalar", s_sm, sm_t[kt - 3])
                    ex_t[kt] = bld.op("scalar", lambda e, kt=kt, ncol=ncol: e.activation(
                        out=pT[:, (kt % 3) * 512:(kt % 3 + 1) * 512], in_=pb[SBK[kt % 3]][:],
                        func=AF.Exp, scale=SM, bias=negB[:, ncol:ncol + 1]), s_ex)
                    bld.wait("tensor", s_ex, ex_t[kt])
                    pv_t[kt] = bld.op("tensor", lambda e, kt=kt, kv=kv: e.matmul(
                        pb[2][:], v_sb[:, kt * 512 + kv * 128: kt * 512 + (kv + 1) * 128],
                        pT[:, (kt % 3) * 512:(kt % 3 + 1) * 512],
                        start=(kt == 0), stop=(kt == KT - 1), skip_group_check=True), s_pv)
                    if kt + 3 < KT:
                        mm_sc(kt + 3)
                    bld.wait("vector", s_ex, ex_t[kt])
                    if kt == 0:
                        sm_t[kt] = bld.op("vector", lambda e: e.tensor_copy(
                            out=sumC[:], in_=pT[:, 0:512]), s_sm)
                    else:
                        sm_t[kt] = bld.op("vector", lambda e, kt=kt: e.tensor_tensor(
                            out=sumC[:], in0=sumC[:], in1=pT[:, (kt % 3) * 512:(kt % 3 + 1) * 512],
                            op=ALU.add), s_sm)
                bld.wait("tensor", s_sm, sm_t[KT - 1])
                bld.wait("tensor", s_ve)
                t_ones = bld.op("tensor", lambda e: e.matmul(
                    p1[0:1, :], onesb[:, 0:1], sumC[:], start=True, stop=True), s_te)
                bld.wait("vector", s_te, t_ones)
                t_rc = bld.op("vector", lambda e: e.reciprocal(out=strow[0:1, :], in_=p1[0:1, :]), s_ve)
                bld.wait("tensor", s_ve, t_rc)
                t_bc = bld.op("tensor", lambda e: e.matmul(
                    pb[4][:], onesf[0:1, :], strow[0:1, :], start=True, stop=True), s_te)
                bld.wait("vector", s_te, t_bc)
                bld.op("vector", lambda e: e.tensor_copy(out=recipB[:], in_=pb[4][:]), s_ve)
                bld.wait("vector", s_pv, pv_t[KT - 1])
                bld.op("vector", lambda e, h=h: e.tensor_tensor(
                    out=OT[:, h * 512:(h + 1) * 512], in0=pb[2][:], in1=recipB[:], op=ALU.mult), s_ve)

        def _emit_outproj_dm(dm, ld_tgt):
            bld.wait("tensor", s_wp[dm % 2], ld_tgt)
            if dm >= 2:
                bld.wait("tensor", s_se, ev_t[dm - 2])
            for hd in range(NC_DM):
                tv = bld.op("tensor", lambda e, dm=dm, hd=hd: e.matmul(
                    pb[5 if dm % 2 == 0 else 3][:],
                    wos[:, (dm % 2) * DM + hd * 128:(dm % 2) * DM + (hd + 1) * 128],
                    OT[:, hd * 512:(hd + 1) * 512],
                    start=(hd == 0), stop=(hd == NC_DM - 1), skip_group_check=True),
                    s_te if hd == NC_DM - 1 else None)
                if hd == NC_DM - 1:
                    mm_t[dm] = tv
            bld.wait("scalar", s_te, mm_t[dm])
            ev_t[dm] = bld.op("scalar", lambda e, dm=dm: e.activation(
                out=brF[:, dm * 512:(dm + 1) * 512], in_=pb[5 if dm % 2 == 0 else 3][:],
                func=AF.Identity, bias=bo_t[:, dm:dm + 1]), s_se)

        mm_t = [None] * NC_DM
        ev_t = [None] * NC_DM
        ld_t_arr = [None] * NC_DM
        bld.wait("sync", s_ve)
        for dm in range(NC_DM):
            if dm >= 2:
                bld.wait("sync", s_te, mm_t[dm - 2])
            ld_t_arr[dm] = bld.dma("sync", wos[:, (dm % 2) * DM:(dm % 2 + 1) * DM], woB_scr[dm], s_wp[dm % 2])
            _emit_outproj_dm(dm, ld_t_arr[dm])


        for i in range(SUB):
            bld.wait("sync", s_ve)
            t_x = bld.dma("sync", xres[i % 2][:], x_own[(TT * TILE + i * 128):(TT * TILE + (i + 1) * 128), :], s_ld)
            bld.wait("tensor", s_se)
            for cg in range(4):
                bld.wait("tensor", s_ve)
                for j in range(4):
                    dm = cg * 4 + j
                    bld.op("tensor", lambda e, dm=dm, j=j, i=i: e.transpose(
                        pbf[:, j * 128:(j + 1) * 128],
                        brF[:, dm * 512 + i * 128: dm * 512 + (i + 1) * 128], eyeb[:]),
                        s_te if j == 3 else None)
                bld.wait("vector", s_te)
                bld.op("vector", lambda e, cg=cg: e.tensor_copy(
                    out=br_sub.rearrange("p (c f) -> p c f", c=NC_DM)[:, cg * 4:(cg + 1) * 4, :],
                    in_=pbf.rearrange("p (j f) -> p j f", j=4)[:]), s_ve)
            bld.wait("vector", s_ld, t_x)
            bld.wait("vector", s_se)
            bld.op("vector", lambda e, i=i: e.tensor_tensor(
                out=sq_f[:], in0=xres[i % 2][:], in1=br_sub[:], op=ALU.add), s_ve)
            ln_block(sq_f[:], stream1[:, i * DM:(i + 1) * DM], i)
            bld.wait("sync", s_se)
            bld.dma("sync", s1_scr[i], stream1[:, i * DM:(i + 1) * DM], s_st)



    def emit_ffn(TT, Xs, xq_f, arena, dslab2, k0i_t, s1f, uq, k0q, dq, gs, tu, maxC, brD):
        for i in range(SUB):
            bld.wait("sync", s_ve)
            t_s1 = bld.dma("sync", s1f[:], s1_scr[i], s_ld)
            bld.wait("vector", s_ld, t_s1)
            pp("vector", lambda e: e.tensor_reduce(
                out=col(C_MRAW), in_=s1f[:], axis=AX.X, op=ALU.max, apply_absolute_value=True))
            pp("vector", lambda e: e.tensor_scalar(out=col(C_M), in0=col(C_MRAW), scalar1=EPS_Q,
                                                   scalar2=None, op0=ALU.max))
            pp("vector", lambda e: e.reciprocal(out=col(C_RS), in_=col(C_M)))
            pp("vector", lambda e: e.tensor_scalar(out=col(C_SACT), in0=col(C_RS), scalar1=128.0,
                                                   scalar2=None, op0=ALU.mult))
            pp("vector", lambda e, i=i: e.tensor_scalar(out=inv1[:, i:i + 1], in0=col(C_M),
                                                        scalar1=1.0 / 128.0, scalar2=None, op0=ALU.mult))
            bld.wait("vector", s_te)
            bld.op("vector", lambda e: e.tensor_scalar(
                out=sq_f[:], in0=s1f[:], scalar1=col(C_SACT), scalar2=None, op0=ALU.mult))
            bld.op("vector", lambda e: e.tensor_copy(out=k0i_t[:], in_=sq_f[:]))
            bld.op("vector", lambda e, i=i: e.tensor_tensor(
                out=xres[i % 2][:], in0=sq_f[:], in1=k0i_t[:], op=ALU.subtract))
            bld.op("vector", lambda e, i=i: e.scalar_tensor_tensor(
                out=sq_f[:], in0=xres[i % 2][:], scalar=0.5, in1=k0i_t[:], op0=ALU.is_gt, op1=ALU.add))
            bld.op("vector", lambda e, i=i: e.scalar_tensor_tensor(
                out=br_sub[:], in0=xres[i % 2][:], scalar=-0.5, in1=sq_f[:], op0=ALU.is_ge, op1=ALU.add))
            bld.op("vector", lambda e: e.tensor_scalar(
                out=sq_f[:], in0=br_sub[:], scalar1=128.0, scalar2=-127.0, op0=ALU.min, op1=ALU.max), s_ve)
            bld.wait("scalar", s_ve)
            bld.op("scalar", lambda e: e.activation(out=br_sub[:], in_=sq_f[:], func=AF.Identity,
                                                    bias=neg1_t[:]), s_se)
            bld.wait("tensor", s_se)
            for cg in range(4):
                bld.wait("tensor", s_ve)
                for j in range(4):
                    c = cg * 4 + j
                    bld.op("tensor", lambda e, c=c, j=j: e.transpose(
                        pbf[:, j * 128:(j + 1) * 128], br_sub[:, c * 128:(c + 1) * 128], eyeb[:]),
                        s_te if j == 3 else None)
                bld.wait("vector", s_te)
                bld.op("vector", lambda e, cg=cg, i=i: e.tensor_copy(
                    out=xq_f.rearrange("p (c f) -> p c f", c=NC_DM)[:, cg * 4:(cg + 1) * 4, i * 128:(i + 1) * 128],
                    in_=pbf.rearrange("p (j f) -> p j f", j=4)[:]), s_ve)

        spoint(0)
        dump_tile(0, xq_f[:], NC_DM * 512)
        bld.wait("tensor", s_pp)
        bld.wait("tensor", s_ve)
        for i in range(SUB):
            bld.op("tensor", lambda e, i=i: e.transpose(
                p1[0:1, i * 128:(i + 1) * 128], inv1[:, i:i + 1], eyef[:]),
                s_te if i == SUB - 1 else None)
        bld.wait("vector", s_te)
        t_cp = bld.op("vector", lambda e: e.tensor_copy(out=strow[0:1, :], in_=p1[0:1, :]), s_ve)
        bld.wait("tensor", s_ve, t_cp)
        t_bc = bld.op("tensor", lambda e: e.matmul(pb[4][:], onesf[0:1, :], strow[0:1, :],
                                                   start=True, stop=True), s_te)
        bld.wait("vector", s_te, t_bc)
        t_i1 = bld.op("vector", lambda e: e.tensor_copy(out=inv1B[:], in_=pb[4][:]), s_ve)
        bld.wait("tensor", s_ve, t_i1)

        gmm = [None] * NC_FF
        umm = [None] * NC_FF
        gev = [None] * NC_FF
        uev = [None] * NC_FF
        xev = [None] * NC_FF
        bld.wait("vector", s_ve)
        gu_ld = [None] * NC_FF
        for oc in range(NC_FF):
            sl = oc % 2
            if oc >= 2:
                bld.wait("sync", s_te, umm[oc - 2])
            bld.dma("sync", arena[:, sl * 2048:(sl + 1) * 2048], wgT_scr[oc], s_gu[sl])
            gu_ld[oc] = bld.dma("sync", arena[:, (2 + sl) * 2048:(3 + sl) * 2048], wuT_scr[oc], s_gu[sl])
            bld.wait("tensor", s_gu[sl], gu_ld[oc])
            if oc >= 2:
                bld.wait("tensor", s_ve, xev[oc - 2])
            for hc in range(NC_DM):
                tv = bld.op("tensor", lambda e, hc=hc, sl=sl: e.matmul(
                    pb[sl][:], arena[:, sl * 2048 + hc * 128: sl * 2048 + (hc + 1) * 128],
                    xq_f[:, hc * 512:(hc + 1) * 512],
                    start=(hc == 0), stop=(hc == NC_DM - 1), skip_group_check=True),
                    s_te if hc == NC_DM - 1 else None)
                if hc == NC_DM - 1:
                    gmm[oc] = tv
            for hc in range(NC_DM):
                tv = bld.op("tensor", lambda e, hc=hc, sl=sl: e.matmul(
                    pb[2 + sl][:], arena[:, (2 + sl) * 2048 + hc * 128: (2 + sl) * 2048 + (hc + 1) * 128],
                    xq_f[:, hc * 512:(hc + 1) * 512],
                    start=(hc == 0), stop=(hc == NC_DM - 1), skip_group_check=True),
                    s_te if hc == NC_DM - 1 else None)
                if hc == NC_DM - 1:
                    umm[oc] = tv
            bld.wait("vector", s_te, gmm[oc])
            bld.wait("vector", s_se)
            bld.op("vector", lambda e, sl=sl: e.scalar_tensor_tensor(
                out=tu[:, sl * 512:(sl + 1) * 512], in0=pb[sl][:], scalar=isw_g[:], in1=inv1B[:],
                op0=ALU.mult, op1=ALU.mult), s_ve)
            bld.wait("scalar", s_ve)
            gev[oc] = bld.op("scalar", lambda e, sl=sl: e.activation(
                out=gs[:, sl * 512:(sl + 1) * 512], in_=tu[:, sl * 512:(sl + 1) * 512], func=AF.Silu), s_se)
            bld.wait("vector", s_te, umm[oc])
            bld.wait("vector", s_se, gev[oc])
            uev[oc] = bld.op("vector", lambda e, sl=sl: e.scalar_tensor_tensor(
                out=tu[:, sl * 512:(sl + 1) * 512], in0=pb[2 + sl][:], scalar=isw_u[:], in1=inv1B[:],
                op0=ALU.mult, op1=ALU.mult), s_ve)
            xev[oc] = bld.op("vector", lambda e, oc=oc, sl=sl: e.tensor_tensor(
                out=Xs[:, oc * 512:(oc + 1) * 512], in0=gs[:, sl * 512:(sl + 1) * 512],
                in1=tu[:, sl * 512:(sl + 1) * 512], op=ALU.mult), s_ve)

        spoint(1)
        dump_slots(1, list(range(NC_FF)))
        sa_ev = [None] * NC_FF
        for oc in range(NC_FF):
            bank = 4 + (oc % 2)
            if oc >= 2:
                bld.wait("tensor", s_ve, sa_ev[oc - 2])
            bld.wait("tensor", s_ve, xev[oc])
            bld.op("tensor", lambda e, oc=oc, bank=bank: e.matmul(
                pb[bank][:], h128[:], Xs[:, oc * 512:(oc + 1) * 512],
                start=True, stop=True, skip_group_check=True), s_te)
            bld.wait("vector", s_te)
            sa_ev[oc] = bld.op("vector", lambda e, oc=oc, bank=bank: e.tensor_copy(
                out=Xs[:, oc * 512:(oc + 1) * 512], in_=pb[bank][:]), s_ve)

        spoint(2)
        dump_slots(2, list(range(NC_FF)))
        def slot(pos_idx):
            return Xs[:, pos_idx * 512:(pos_idx + 1) * 512]

        pos = list(range(NC_FF))
        spare = [NC_FF, NC_FF + 1]
        for sstride in (32, 16, 8, 4, 2, 1):
            for i0 in range(NC_FF):
                if i0 & sstride:
                    continue
                j0 = i0 | sstride
                pa, pbx = pos[i0], pos[j0]
                ns0, ns1 = spare
                bld.op("vector", lambda e, a=pa, b=pbx, o=ns0: e.tensor_tensor(
                    out=slot(o), in0=slot(a), in1=slot(b), op=ALU.add))
                bld.op("vector", lambda e, a=pa, b=pbx, o=ns1: e.tensor_tensor(
                    out=slot(o), in0=slot(a), in1=slot(b), op=ALU.subtract))
                pos[i0], pos[j0] = ns0, ns1
                spare = [pa, pbx]
        bld.op("vector", lambda e: e.engine_nop(), s_ve)
        spoint(3)

        if True:
            dump_slots(3, list(pos))
        bld.op("vector", lambda e: e.memset(maxC[:], 0.0))
        ab_t = [None] * NC_FF
        mx_t = [None] * NC_FF
        for oc in range(NC_FF):
            sl = oc % 2
            if oc >= 2:
                bld.wait("scalar", s_sm, mx_t[oc - 2])
            bld.wait("scalar", s_ve)
            ab_t[oc] = bld.op("scalar", lambda e, oc=oc, sl=sl: e.activation(
                out=gs[:, sl * 512:(sl + 1) * 512], in_=slot(pos[oc]), func=AF.Abs), s_se)
            bld.wait("vector", s_se, ab_t[oc])
            mx_t[oc] = bld.op("vector", lambda e, sl=sl: e.tensor_tensor(
                out=maxC[:], in0=maxC[:], in1=gs[:, sl * 512:(sl + 1) * 512], op=ALU.max), s_sm)
        bld.op("vector", lambda e: e.engine_nop(), s_ve)
        bld.wait("tensor", s_ve)
        for i in range(SUB):
            bld.op("tensor", lambda e, i=i: e.transpose(
                pb[4][:, i * 128:(i + 1) * 128], maxC[:, i * 128:(i + 1) * 128], eyef[:]),
                s_te if i == SUB - 1 else None)
        bld.wait("vector", s_te)
        for i in range(SUB):
            pp("vector", lambda e, i=i: e.tensor_reduce(
                out=col(C_MRAW), in_=pb[4][:, i * 128:(i + 1) * 128], axis=AX.X, op=ALU.max))
            pp("vector", lambda e: e.tensor_scalar(out=col(C_M), in0=col(C_MRAW), scalar1=CF,
                                                   scalar2=EPS_Q, op0=ALU.mult, op1=ALU.max))
            pp("vector", lambda e: e.reciprocal(out=col(C_RS), in_=col(C_M)))
            pp("vector", lambda e, i=i: e.tensor_scalar(out=s2c[:, i:i + 1], in0=col(C_RS), scalar1=128.0 * CF,
                                                        scalar2=None, op0=ALU.mult))
            pp("vector", lambda e: e.tensor_scalar(out=col(C_SACT), in0=col(C_M), scalar1=1.0 / 128.0,
                                                   scalar2=None, op0=ALU.mult))
            pp("vector", lambda e, i=i: e.tensor_scalar(out=cs2[:, i:i + 1], in0=col(C_SACT),
                                                        scalar1=isw_d[:], scalar2=None, op0=ALU.mult))
        bld.wait("tensor", s_pp)
        bld.wait("tensor", s_ve)
        for i in range(SUB):
            bld.op("tensor", lambda e, i=i: e.transpose(
                p1[0:1, i * 128:(i + 1) * 128], s2c[:, i:i + 1], eyef[:]))
        for i in range(SUB):
            bld.op("tensor", lambda e, i=i: e.transpose(
                pb[5][0:1, i * 128:(i + 1) * 128], cs2[:, i:i + 1], eyef[:]),
                s_te if i == SUB - 1 else None)
        bld.wait("vector", s_te)
        t_c1 = bld.op("vector", lambda e: e.tensor_copy(out=strow[0:1, :], in_=p1[0:1, :]), s_ve)
        bld.wait("tensor", s_ve, t_c1)
        t_b1 = bld.op("tensor", lambda e: e.matmul(pb[4][:], onesf[0:1, :], strow[0:1, :],
                                                   start=True, stop=True), s_te)
        bld.wait("vector", s_te, t_b1)
        bld.op("vector", lambda e: e.tensor_copy(out=s2B[:], in_=pb[4][:]))
        t_c2 = bld.op("vector", lambda e: e.tensor_copy(out=strow[0:1, :], in_=pb[5][0:1, :]), s_ve)
        bld.wait("tensor", s_ve, t_c2)
        t_b2 = bld.op("tensor", lambda e: e.matmul(pb[4][:], onesf[0:1, :], strow[0:1, :],
                                                   start=True, stop=True), s_te)
        bld.wait("vector", s_te, t_b2)
        bld.op("vector", lambda e: e.tensor_copy(out=cs2B[:], in_=pb[4][:]), s_ve)

        for oc in range(NC_FF):
            bld.op("vector", lambda e, oc=oc: e.tensor_tensor(
                out=uq[:], in0=slot(pos[oc]), in1=s2B[:], op=ALU.mult))
            bld.op("vector", lambda e: e.tensor_copy(out=k0q[:], in_=uq[:]))
            bld.op("vector", lambda e: e.tensor_tensor(out=dq[:], in0=uq[:], in1=k0q[:], op=ALU.subtract))
            bld.op("vector", lambda e: e.scalar_tensor_tensor(
                out=tu[:, 0:512], in0=dq[:], scalar=0.5, in1=k0q[:], op0=ALU.is_gt, op1=ALU.add))
            bld.op("vector", lambda e: e.scalar_tensor_tensor(
                out=uq[:], in0=dq[:], scalar=-0.5, in1=tu[:, 0:512], op0=ALU.is_ge, op1=ALU.add))
            bld.op("vector", lambda e: e.tensor_scalar(
                out=dq[:], in0=uq[:], scalar1=128.0, scalar2=-127.0, op0=ALU.min, op1=ALU.max))
            bld.op("vector", lambda e, oc=oc: e.tensor_scalar(
                out=slot(pos[oc]), in0=dq[:], scalar1=1.0, scalar2=None, op0=ALU.subtract), s_ve)

        spoint(4)
        dump_slots(4, list(pos))
        if dbgS is not None and (cur_tt[0] == dump_tt or dump_tt == -2) and 7 in DUMP_MASK:
            bld.wait("vector", s_pp)
            bld.op("vector", lambda e: e.tensor_copy(out=stt[:, 20:21], in_=isw_g[:]))
            bld.op("vector", lambda e: e.tensor_copy(out=stt[:, 21:22], in_=isw_u[:]))
            bld.op("vector", lambda e: e.tensor_copy(out=stt[:, 22:23], in_=isw_d[:]), s_ve)
            bld.wait("sync", s_ve)
            bld.dma("sync", dbgS[:, 0:4], inv1[:], s_st)
            bld.dma("sync", dbgS[:, 4:8], cs2[:], s_st)
            bld.dma("sync", dbgS[:, 8:12], s2c[:], s_st)
            bld.dma("sync", dbgS[:, 12:16], inv1B[:, 0:4], s_st)
            bld.dma("sync", dbgS[:, 16:20], s2B[:, 0:4], s_st)
            bld.dma("sync", dbgS[:, 20:24], stt[:, 20:24], s_st)
            bld.dma("sync", dbgS[:, 24:28], cs2B[:, 0:4], s_st)
            bld.wait("sync", s_st)
        spoint(5)
        dmm = [None] * NC_DM
        dev = [None] * NC_DM
        bld.wait("sync", s_te)
        bld.wait("tensor", s_ve)
        dn_ld = [None] * NC_DM
        for dm in range(NC_DM):
            sl = dm % 2
            dbuf = (arena[:, 0:FF] if sl == 0 else dslab2[:, 0:FF])
            if dm >= 2:
                bld.wait("sync", s_te, dmm[dm - 2])
            dn_ld[dm] = bld.dma("sync", dbuf, wdT_scr[dm], s_dn[dm % 2])
            bld.wait("tensor", s_dn[dm % 2], dn_ld[dm])
            if dm >= 2:
                bld.wait("tensor", s_ve, dev[dm - 2])
            for ic in range(NC_FF):
                tv = bld.op("tensor", lambda e, ic=ic, sl=sl, dbuf=dbuf: e.matmul(
                    pb[sl][:], dbuf[:, ic * 128:(ic + 1) * 128],
                    slot(pos[ic]), start=(ic == 0), stop=(ic == NC_FF - 1), skip_group_check=True),
                    s_te if ic == NC_FF - 1 else None)
                if ic == NC_FF - 1:
                    dmm[dm] = tv
            bld.wait("vector", s_te, dmm[dm])
            dev[dm] = bld.op("vector", lambda e, dm=dm, sl=sl: e.tensor_tensor(
                out=brD[:, dm * 512:(dm + 1) * 512], in0=pb[sl][:], in1=cs2B[:], op=ALU.mult), s_ve)

        spoint(6)
        if _dump_on(5):
            bld.wait("sync", s_ve)
            bld.dma("sync", dbgF[5][:, 0:NC_DM * 512], brD[:], s_st)
            bld.wait("sync", s_st)
            for eng in ("vector", "scalar", "tensor"):
                bld.wait(eng, s_st)
        for i in range(SUB):
            bld.wait("sync", s_ve)
            bld.wait("sync", s_st)
            bld.wait("sync", s_se)
            t_s1 = bld.dma("sync", s1f[:], s1_scr[i], s_ld)
            bld.wait("tensor", s_ve)
            for cg in range(4):
                bld.wait("tensor", s_ve)
                for j in range(4):
                    dm = cg * 4 + j
                    bld.op("tensor", lambda e, dm=dm, j=j, i=i: e.transpose(
                        pbf[:, j * 128:(j + 1) * 128],
                        brD[:, dm * 512 + i * 128: dm * 512 + (i + 1) * 128], eyeb[:]),
                        s_te if j == 3 else None)
                bld.wait("vector", s_te)
                bld.op("vector", lambda e, cg=cg: e.tensor_copy(
                    out=br_sub.rearrange("p (c f) -> p c f", c=NC_DM)[:, cg * 4:(cg + 1) * 4, :],
                    in_=pbf.rearrange("p (j f) -> p j f", j=4)[:]), s_ve)
            bld.wait("vector", s_ld, t_s1)
            bld.wait("vector", s_se)
            bld.op("vector", lambda e: e.tensor_tensor(
                out=sq_f[:], in0=s1f[:], in1=br_sub[:], op=ALU.add), s_ve)
            ln_block(sq_f[:], s1f[:], i)
            bld.wait("sync", s_se)
            bld.dma("sync", out[(TT * TILE + i * 128):(TT * TILE + (i + 1) * 128), :], s1f[:], s_st)



    for TT in range(g.get("n_tiles", NT)):
        cur_tt[0] = TT
        lvlAt = ExitStack()
        stream1 = sbufB(lvlAt, f"stream1_{TT}", SUB * DM, F32)
        v_sb = sbufB(lvlAt, f"v_sb_{TT}", KT * 512)
        k_sb = sbufB(lvlAt, f"k_sb_{TT}", NKV * S)
        q_sb = sbufB(lvlAt, f"q_sb_{TT}", NH * 512)
        pT = sbufB(lvlAt, f"pT_{TT}", 3 * 512)
        sumC = sbufB(lvlAt, f"sumC_{TT}", 512)
        OT = sbufB(lvlAt, f"OT_{TT}", NH * 512)
        wos = sbufB(lvlAt, f"wos_{TT}", 2 * DM)
        brF = sbufB(lvlAt, f"brF_{TT}", NC_DM * 512)
        recipB = sbufB(lvlAt, f"recipB_{TT}", 512, F32)
        emit_attn(TT, stream1, v_sb, k_sb, q_sb, pT, sumC, OT, wos, brF, recipB)
        bld.wait("sync", s_st)
        bld.barrier(s_ph)
        lvlAt.close()

        lvlF = ExitStack()
        Xs = sbufB(lvlF, f"Xs_{TT}", 66 * 512)
        Xs_cur[0] = Xs
        xq_f = sbufB(lvlF, f"xq_f_{TT}", NC_DM * 512)
        arena = sbufB(lvlF, f"arena_{TT}", 4 * 2048)
        dslab2 = sbufB(lvlF, f"dslab2_{TT}", FF)
        k0i_t = sbufB(lvlF, f"k0i_t_{TT}", DM, I32)
        s1f = sbufB(lvlF, f"s1f_{TT}", DM, F32)
        uq = sbufB(lvlF, f"uq_{TT}", 512, F32)
        k0q = sbufB(lvlF, f"k0q_{TT}", 512, I32)
        dq = sbufB(lvlF, f"dq_{TT}", 512, F32)
        gs = sbufB(lvlF, f"gs_{TT}", 2 * 512)
        tu = sbufB(lvlF, f"tu_{TT}", 2 * 512, F32)
        maxC = sbufB(lvlF, f"maxC_{TT}", 512, F32)
        brD = sbufB(lvlF, f"brD_{TT}", NC_DM * 512)
        emit_ffn(TT, Xs, xq_f, arena, dslab2, k0i_t, s1f, uq, k0q, dq, gs, tu, maxC, brD)
        bld.wait("sync", s_st)
        bld.barrier(s_ph)
        lvlF.close()

    bld.wait("sync", s_st)
    lvlB.close()



def _consts():
    h128 = _hadamard(128).astype(ml_dtypes.bfloat16)
    eyeb = np.eye(128).astype(ml_dtypes.bfloat16)
    eyef = np.eye(128, dtype=np.float32)
    onesb = np.ones((128, 128), dtype=ml_dtypes.bfloat16)
    onesf = np.ones((128, 128), dtype=np.float32)
    return h128, eyeb, eyef, onesb, onesf


def make_in_maps(x, wq_, wk_, wv_, wo_, bq_, bk_, bv_, bo_, wg_, wu_, wd_):
    h128, eyeb, eyef, onesb, onesf = _consts()
    common = dict(
        wq=np.ascontiguousarray(wq_), wk=np.ascontiguousarray(wk_),
        wv=np.ascontiguousarray(wv_), wo=np.ascontiguousarray(wo_),
        bq=np.ascontiguousarray(bq_), bk=np.ascontiguousarray(bk_),
        bv=np.ascontiguousarray(bv_), bo=np.ascontiguousarray(bo_),
        wg=np.ascontiguousarray(wg_), wu=np.ascontiguousarray(wu_),
        wd=np.ascontiguousarray(wd_),
        h128=h128, eyeb=eyeb, eyef=eyef, onesb=onesb, onesf=onesf,
    )
    in_maps = []
    for core in range(8):
        b, half = core // 2, core % 2
        m = dict(common)
        m["x_all"] = np.ascontiguousarray(x[b])
        m["x_own"] = np.ascontiguousarray(x[b, half * TOK:(half + 1) * TOK, :])
        in_maps.append(m)
    return in_maps


def kernel(L0_attn_inputs_q, L0_attn_wq, L0_attn_bq, L0_attn_wk, L0_attn_bk,
           L0_attn_wv, L0_attn_bv, L0_attn_wo, L0_attn_bo,
           L0_ffn_w_gate, L0_ffn_w_up, L0_ffn_w_down):
    x = np.ascontiguousarray(L0_attn_inputs_q)
    in_maps = make_in_maps(x, L0_attn_wq, L0_attn_wk, L0_attn_wv, L0_attn_wo,
                           L0_attn_bq, L0_attn_bk, L0_attn_bv, L0_attn_bo,
                           L0_ffn_w_gate, L0_ffn_w_up, L0_ffn_w_down)
    nc = build_program()
    res = run_bass_kernel_spmd(nc, in_maps, core_ids=list(range(8)))
    outp = np.empty((4, S, DM), dtype=np.float32)
    for core in range(8):
        b, half = core // 2, core % 2
        outp[b, half * TOK:(half + 1) * TOK, :] = np.asarray(res.results[core]["out"])
    return outp


# version 12
# speedup vs baseline: 1.0514x; 1.0309x over previous
import numpy as np
import ml_dtypes
from contextlib import ExitStack

import concourse.bass as bass
import concourse.mybir as mybir
from concourse.bass_utils import run_bass_kernel_spmd

F32 = mybir.dt.float32
BF16 = mybir.dt.bfloat16
I32 = mybir.dt.int32
AF = mybir.ActivationFunctionType
ALU = mybir.AluOpType
AX = mybir.AxisListType

S = 4096
DM = 2048
NH = 16
NKV = 4
HD = 128
FF = 8192
TOK = 2048
TILE = 512
NT = TOK // TILE
SUB = TILE // 128
NST_ALL = S // TILE
NC_DM = DM // 128
NC_FF = FF // 128
KT = S // 128

SERIAL_POINTS = set()
DUMP_MASK = set(range(7))

SM = 1.0 / float(np.sqrt(128.0))
CF = float(np.float32(1.0 / np.sqrt(FF)))
EPS_Q = 1e-5
EPS_LN = 1e-5


def _hadamard(n):
    h = np.array([[1.0]])
    while h.shape[0] < n:
        h = np.block([[h, h], [h, -h]])
    return h


class Bld:
    ENGS = ("tensor", "vector", "scalar", "sync", "gpsimd")

    def __init__(self, nc, stack):
        self.nc = nc
        self.stack = stack
        self.ops = []
        self.semv = {}
        self.ord_sem = {}
        self.ord_cnt = {}
        for eng in ("vector", "scalar", "tensor"):
            self.ord_sem[eng] = None

    def sem(self, name):
        h = self.stack.enter_context(self.nc.semaphore(name))
        self.semv[id(h)] = 0
        return h

    def on(self, eng, fn):
        self.ops.append((eng, fn))

    def bump(self, sem, k=1):
        self.semv[id(sem)] += k
        return self.semv[id(sem)]

    def val(self, sem):
        return self.semv[id(sem)]

    def dma(self, q, out, in_, sem):
        tgt = self.bump(sem, 16)
        self.on(q, lambda e, o=out, i=in_, s=sem: e.dma_start(out=o, in_=i).then_inc(s, 16))
        return tgt

    def wait(self, eng, sem, v=None):
        v = self.val(sem) if v is None else v
        if v > 0:
            self.on(eng, lambda e, s=sem, vv=v: e.wait_ge(s, vv))
        return v

    def init_ord(self):
        self.last_marker = {}
        for eng in ("vector", "scalar"):
            self.ord_sem[eng] = self.sem(f"s_ord_{eng}")
            self.last_marker[eng] = None

    def op(self, eng, fn, sem=None, k=1):
        osem = self.ord_sem.get(eng)
        prev = self.last_marker.get(eng) if osem is not None else None
        if sem is not None:
            tgt = self.bump(sem, k)
            marker = (sem, tgt)
        else:
            tgt = None
            marker = None
        if osem is not None:
            if marker is None:
                marker = (osem, self.bump(osem, 1))
            self.last_marker[eng] = marker
            msem, mval = marker
            use_ord = (sem is None)

        def g(e, f=fn, s=sem, kk=k, prev=prev, osem=osem):
            if prev is not None:
                e.wait_ge(prev[0], prev[1])
            ins = f(e)
            if osem is not None and s is None:
                ins.then_inc(osem, 1)
            if s is not None:
                ins.then_inc(s, kk)
        self.on(eng, g)
        return tgt

    def barrier(self, sem):
        tgt = self.val(sem) + 5
        for eng in self.ENGS:
            self.bump(sem, 1)
            self.on(eng, lambda e, s=sem: e.nop().then_inc(s, 1))
        for eng in self.ENGS:
            self.wait(eng, sem, tgt)
        return tgt

    def run(self, block):
        def body(eng_name):
            def f(e):
                for name, fn in self.ops:
                    if name == eng_name:
                        fn(e)
            return f
        block.tensor(body("tensor"))
        block.vector(body("vector"))
        block.scalar(body("scalar"))
        block.sync(body("sync"))
        block.gpsimd(body("gpsimd"))


def build_program(debug_phase=None):
    nc = bass.Bass("TRN2", target_bir_lowering=False)
    stack = ExitStack()

    x_all = nc.dram_tensor("x_all", [S, DM], F32, kind="ExternalInput").ap()
    x_own = nc.dram_tensor("x_own", [TOK, DM], F32, kind="ExternalInput").ap()
    wq = nc.dram_tensor("wq", [DM, DM], F32, kind="ExternalInput").ap()
    wk = nc.dram_tensor("wk", [DM, 512], F32, kind="ExternalInput").ap()
    wv = nc.dram_tensor("wv", [DM, 512], F32, kind="ExternalInput").ap()
    wo = nc.dram_tensor("wo", [DM, DM], F32, kind="ExternalInput").ap()
    bq = nc.dram_tensor("bq", [DM], F32, kind="ExternalInput").ap()
    bk = nc.dram_tensor("bk", [512], F32, kind="ExternalInput").ap()
    bv = nc.dram_tensor("bv", [512], F32, kind="ExternalInput").ap()
    bo = nc.dram_tensor("bo", [DM], F32, kind="ExternalInput").ap()
    wg = nc.dram_tensor("wg", [FF, DM], F32, kind="ExternalInput").ap()
    wu = nc.dram_tensor("wu", [FF, DM], F32, kind="ExternalInput").ap()
    wd = nc.dram_tensor("wd", [DM, FF], F32, kind="ExternalInput").ap()
    h128_in = nc.dram_tensor("h128", [128, 128], BF16, kind="ExternalInput").ap()
    eyeb_in = nc.dram_tensor("eyeb", [128, 128], BF16, kind="ExternalInput").ap()
    eyef_in = nc.dram_tensor("eyef", [128, 128], F32, kind="ExternalInput").ap()
    onesb_in = nc.dram_tensor("onesb", [128, 128], BF16, kind="ExternalInput").ap()
    onesf_in = nc.dram_tensor("onesf", [128, 128], F32, kind="ExternalInput").ap()

    out = nc.dram_tensor("out", [TOK, DM], F32, kind="ExternalOutput").ap()

    dbg = "ExternalOutput" if debug_phase == "A" else "Internal"
    q_scr = nc.dram_tensor("q_scr", [NH, NT, 128, TILE], BF16, kind=dbg).ap()
    k_scr = nc.dram_tensor("k_scr", [NKV, NST_ALL, 128, TILE], BF16, kind=dbg).ap()
    v_scr = nc.dram_tensor("v_scr", [KT, 128, 512], BF16, kind=dbg).ap()
    wgT_scr = nc.dram_tensor("wgT_scr", [NC_FF, 128, DM], BF16, kind=dbg).ap()
    wuT_scr = nc.dram_tensor("wuT_scr", [NC_FF, 128, DM], BF16, kind=dbg).ap()
    wdT_scr = nc.dram_tensor("wdT_scr", [NC_DM, 128, FF], BF16, kind=dbg).ap()
    woB_scr = nc.dram_tensor("woB_scr", [NC_DM, 128, DM], BF16, kind=dbg).ap()
    s1_scr = nc.dram_tensor("s1_scr", [SUB, 128, DM], F32,
                            kind="ExternalOutput" if debug_phase in ("B1", "B4", "BX") else "Internal").ap()
    if debug_phase == "A":
        dbgA = nc.dram_tensor("dbgA", [128, 256], F32, kind="ExternalOutput").ap()
    dbgF = None
    dbgS = None
    if debug_phase in ("B1", "B4", "BX"):
        dbgF = nc.dram_tensor("dbgF", [6, 128, 66 * 512], BF16, kind="ExternalOutput").ap()
        dbgS = nc.dram_tensor("dbgS", [128, 64], F32, kind="ExternalOutput").ap()

    with stack:
        bld = Bld(nc, stack)
        s_ld = bld.sem("s_ld")
        s_ld2 = bld.sem("s_ld2")
        s_ld3 = bld.sem("s_ld3")
        s_st = bld.sem("s_st")
        s_te = bld.sem("s_te")
        s_ve = bld.sem("s_ve")
        s_se = bld.sem("s_se")
        s_ph = bld.sem("s_ph")
        s_sp = [bld.sem("s_spA"), bld.sem("s_spB")]
        bld.init_ord()

        def sbuf(name, cols, dt=BF16, rows=128, st=stack):
            return st.enter_context(nc.sbuf_tensor(name, [rows, cols], dt))

        h128 = sbuf("c_h128", 128)
        eyeb = sbuf("c_eyeb", 128)
        eyef = sbuf("c_eyef", 128, F32)
        onesb = sbuf("c_onesb", 128)
        onesf = sbuf("c_onesf", 128, F32)
        bq_t = sbuf("bq_t", NH, F32)
        bk_t = sbuf("bk_t", NKV, F32)
        bo_t = sbuf("bo_t", NC_DM, F32)
        bv_bc = sbuf("bv_bc", 512, F32)
        negB = sbuf("negB", NH * NT, F32)
        eps_t = sbuf("eps_t", 1, F32)
        neg1_t = sbuf("neg1_t", 1, F32)
        sw_g = sbuf("sw_g", 1, F32)
        sw_u = sbuf("sw_u", 1, F32)
        sw_d = sbuf("sw_d", 1, F32)
        isw_g = sbuf("isw_g", 1, F32)
        isw_u = sbuf("isw_u", 1, F32)
        isw_d = sbuf("isw_d", 1, F32)
        strow = sbuf("strow", 512, F32, rows=1)
        st1 = sbuf("st1", 16, F32)
        qmax2 = sbuf("qmax2", NH * NT, F32, rows=1)
        kmax2 = sbuf("kmax2", NKV, F32, rows=1)

        pb = [stack.enter_context(nc.psum_tensor(f"pb{i}", [128, 512], F32)) for i in range(6)]
        pbf = stack.enter_context(nc.psum_tensor("pbf", [128, 512], BF16))
        p1 = stack.enter_context(nc.psum_tensor("p1", [1, 512], F32))

        for t in (h128_in, eyeb_in, eyef_in, onesb_in, onesf_in):
            pass
        bld.dma("sync", h128[:], h128_in, s_ld)
        bld.dma("sync", eyeb[:], eyeb_in, s_ld)
        bld.dma("sync", eyef[:], eyef_in, s_ld)
        bld.dma("sync", onesb[:], onesb_in, s_ld)
        bld.dma("sync", onesf[:], onesf_in, s_ld)
        for h in range(NH):
            bld.dma("sync", bq_t[:, h:h + 1], bq[h * 128:(h + 1) * 128].rearrange("(p o) -> p o", o=1), s_ld)
            bld.dma("sync", bo_t[:, h:h + 1], bo[h * 128:(h + 1) * 128].rearrange("(p o) -> p o", o=1), s_ld)
        for h in range(NKV):
            bld.dma("sync", bk_t[:, h:h + 1], bk[h * 128:(h + 1) * 128].rearrange("(p o) -> p o", o=1), s_ld)
        bld.dma("sync", strow[0:1, :], bv.rearrange("(o f) -> o f", o=1), s_ld)
        tgt_consts = bld.val(s_ld)
        bld.wait("vector", s_ld, tgt_consts)
        bld.op("vector", lambda e: e.memset(eps_t[:], EPS_LN))
        bld.op("vector", lambda e: e.memset(neg1_t[:], -1.0))
        bld.op("vector", lambda e: e.memset(kmax2[:], 0.0), s_ve)
        bld.wait("scalar", s_ld, tgt_consts)
        bld.wait("tensor", s_ld, tgt_consts)
        bld.op("tensor", lambda e: e.matmul(pb[0][:], onesf[0:1, :], strow[0:1, :], start=True, stop=True), s_te)
        bld.wait("vector", s_te)
        bld.op("vector", lambda e: e.tensor_copy(out=bv_bc[:], in_=pb[0][:]), s_ve)

        lvlA = ExitStack()
        wqb = lvlA.enter_context(nc.sbuf_tensor("wqb", [128, NC_DM * DM], BF16))
        wkb = lvlA.enter_context(nc.sbuf_tensor("wkb", [128, NC_DM * 512], BF16))
        wvb = lvlA.enter_context(nc.sbuf_tensor("wvb", [128, NC_DM * 512], BF16))
        xa_f = [lvlA.enter_context(nc.sbuf_tensor(f"xa_f{i}", [128, DM], F32)) for i in range(2)]
        xa_b = lvlA.enter_context(nc.sbuf_tensor("xa_b", [128, DM], BF16))
        xf = lvlA.enter_context(nc.sbuf_tensor("xf", [128, NC_DM * TILE], BF16))
        ev_b = [lvlA.enter_context(nc.sbuf_tensor(f"ev_b{i}", [128, 512], BF16)) for i in range(2)]
        sq_b = lvlA.enter_context(nc.sbuf_tensor("sq_b", [128, 512], BF16))
        xf_view = xf.rearrange("p (c f) -> p c f", c=NC_DM)
        pbf_view = pbf.rearrange("p (j f) -> p j f", j=4)

        for c in range(NC_DM):
            bld.dma("gpsimd", wqb[:, c * DM:(c + 1) * DM], wq[c * 128:(c + 1) * 128, :], s_ld3)
            bld.dma("gpsimd", wkb[:, c * 512:(c + 1) * 512], wk[c * 128:(c + 1) * 128, :], s_ld3)
            bld.dma("gpsimd", wvb[:, c * 512:(c + 1) * 512], wv[c * 128:(c + 1) * 128, :], s_ld3)
        tgt_w = bld.val(s_ld3)
        bld.wait("tensor", s_ld3, tgt_w)

        def emit_x_sub(src_rows, i_sub, buf_i):
            bld.wait("sync", s_ve)
            t_l = bld.dma("sync", xa_f[buf_i][:], src_rows, s_ld)
            bld.wait("vector", s_ld, t_l)
            bld.wait("vector", s_te)
            bld.op("vector", lambda e, b=buf_i: e.tensor_copy(out=xa_b[:], in_=xa_f[b][:]), s_ve)
            bld.wait("tensor", s_ve)
            for cg in range(4):
                bld.wait("tensor", s_ve)
                for j in range(4):
                    c = cg * 4 + j
                    bld.op("tensor", lambda e, c=c, j=j: e.transpose(
                        pbf[:, j * 128:(j + 1) * 128], xa_b[:, c * 128:(c + 1) * 128], eyeb[:]),
                        s_te if j == 3 else None)
                bld.wait("vector", s_te)
                bld.op("vector", lambda e, cg=cg, i=i_sub: e.tensor_copy(
                    out=xf_view[:, cg * 4:(cg + 1) * 4, i * 128:(i + 1) * 128], in_=pbf_view[:]), s_ve)

        ev_i = [0]

        def proj_group(bank, emit_mms, bias_col_ap, dst, norm_col, kind):
            bld.wait("tensor", s_se)
            bld.wait("tensor", s_ve)
            emit_mms(bank)
            b_ev = ev_b[ev_i[0] % 2]
            ev_i[0] += 1
            bld.wait("scalar", s_te)
            bld.wait("scalar", s_st)
            if kind == "v":
                bld.wait("vector", s_te)
                bld.wait("vector", s_st)
                bld.op("vector", lambda e, b=bank, o=b_ev: e.tensor_tensor(
                    out=o[:], in0=b[:], in1=bv_bc[:], op=ALU.add), s_ve)
                bld.wait("sync", s_ve)
                bld.dma("sync", dst, b_ev[:], s_st)
                return
            bld.op("scalar", lambda e, b=bank, o=b_ev, ba=bias_col_ap: e.activation(
                out=o[:], in_=b[:], func=AF.Identity, bias=ba), s_se)
            bld.wait("sync", s_se)
            bld.dma("sync", dst, b_ev[:], s_st)
            bld.wait("scalar", s_te)
            bld.op("scalar", lambda e, i=b_ev: e.activation(out=sq_b[:], in_=i[:], func=AF.Square), s_se)
            bld.wait("tensor", s_se)
            bld.wait("tensor", s_ve)
            bld.op("tensor", lambda e: e.matmul(p1[0:1, :], onesb[:, 0:1], sq_b[:], start=True, stop=True), s_te)
            bld.wait("vector", s_te)
            if kind == "q":
                bld.op("vector", lambda e, col=norm_col: e.tensor_reduce(
                    out=qmax2[0:1, col:col + 1], in_=p1[0:1, :], axis=AX.X, op=ALU.max), s_ve)
            else:
                bld.op("vector", lambda e: e.tensor_reduce(
                    out=st1[0:1, 0:1], in_=p1[0:1, :], axis=AX.X, op=ALU.max), s_ve)
                bld.wait("vector", s_ve)
                bld.op("vector", lambda e, col=norm_col: e.tensor_tensor(
                    out=kmax2[0:1, col:col + 1], in0=kmax2[0:1, col:col + 1], in1=st1[0:1, 0:1],
                    op=ALU.max), s_ve)

        rot = [0]

        def next_bank():
            b = pb[rot[0] % 4]
            rot[0] += 1
            return b

        for st in range(NST_ALL):
            for i in range(SUB):
                emit_x_sub(x_all[(st * TILE + i * 128):(st * TILE + (i + 1) * 128), :], i, i % 2)
            for kv in range(NKV):
                def mm_k(bank, kv=kv):
                    for c in range(NC_DM):
                        bld.op("tensor", lambda e, b=bank, c=c, kv=kv: e.matmul(
                            b[:], wkb[:, c * 512 + kv * 128: c * 512 + (kv + 1) * 128], xf_view[:, c, :],
                            start=(c == 0), stop=(c == NC_DM - 1)), s_te if c == NC_DM - 1 else None)
                proj_group(next_bank(), mm_k, bk_t[:, kv:kv + 1], k_scr[kv, st], kv, "k")
            for i in range(SUB):
                def mm_v(bank, i=i):
                    for c in range(NC_DM):
                        bld.op("tensor", lambda e, b=bank, c=c, i=i: e.matmul(
                            b[:], xf_view[:, c, i * 128:(i + 1) * 128], wvb[:, c * 512:(c + 1) * 512],
                            start=(c == 0), stop=(c == NC_DM - 1)), s_te if c == NC_DM - 1 else None)
                proj_group(next_bank(), mm_v, None, v_scr[st * SUB + i], None, "v")

        for st in range(NT):
            for i in range(SUB):
                emit_x_sub(x_own[(st * TILE + i * 128):(st * TILE + (i + 1) * 128), :], i, i % 2)
            for h in range(NH):
                def mm_q(bank, h=h):
                    for c in range(NC_DM):
                        bld.op("tensor", lambda e, b=bank, c=c, h=h: e.matmul(
                            b[:], wqb[:, c * DM + h * 128: c * DM + (h + 1) * 128], xf_view[:, c, :],
                            start=(c == 0), stop=(c == NC_DM - 1)), s_te if c == NC_DM - 1 else None)
                proj_group(next_bank(), mm_q, bq_t[:, h:h + 1], q_scr[h, st], h * NT + st, "q")

        bld.wait("vector", s_ve)
        for kv in range(NKV):
            bld.op("vector", lambda e, kv=kv: e.tensor_scalar(
                out=qmax2[0:1, kv * 4 * NT:(kv + 1) * 4 * NT], in0=qmax2[0:1, kv * 4 * NT:(kv + 1) * 4 * NT],
                scalar1=kmax2[0:1, kv:kv + 1], scalar2=None, op0=ALU.mult), s_ve)
        bld.wait("scalar", s_ve)
        bld.op("scalar", lambda e: e.activation(out=strow[0:1, 0:NH * NT], in_=qmax2[0:1, :], func=AF.Ln), s_se)
        bld.wait("scalar", s_se)
        bld.op("scalar", lambda e: e.activation(out=strow[0:1, 0:NH * NT], in_=strow[0:1, 0:NH * NT],
                                                func=AF.Exp, scale=0.5), s_se)
        bld.wait("vector", s_se)
        bld.op("vector", lambda e: e.tensor_scalar(
            out=strow[0:1, 0:NH * NT], in0=strow[0:1, 0:NH * NT], scalar1=-SM, scalar2=None,
            op0=ALU.mult), s_ve)
        bld.wait("tensor", s_ve)
        bld.op("tensor", lambda e: e.matmul(pb[4][:, 0:NH * NT], onesf[0:1, :], strow[0:1, 0:NH * NT],
                                            start=True, stop=True), s_te)
        bld.wait("vector", s_te)
        bld.op("vector", lambda e: e.tensor_copy(out=negB[:], in_=pb[4][:, 0:NH * NT]), s_ve)

        bld.wait("sync", s_st)
        bld.wait("sync", s_ld)
        bld.wait("sync", s_ld2)
        bld.wait("sync", s_ld3)
        bld.barrier(s_ph)
        lvlA.close()
        lvlA = ExitStack()

        wo_stage = lvlA.enter_context(nc.sbuf_tensor("wo_stage", [128, DM], F32))
        wo_bf = lvlA.enter_context(nc.sbuf_tensor("wo_bf", [128, DM], BF16))
        for dm in range(NC_DM):
            bld.wait("scalar", s_ve)
            t_l = 0
            for hd in range(NC_DM):
                t_l = bld.dma("scalar", wo_stage[:, hd * 128:(hd + 1) * 128],
                              wo[hd * 128:(hd + 1) * 128, dm * 128:(dm + 1) * 128], s_ld2)
            bld.wait("vector", s_ld2, t_l)
            bld.wait("vector", s_st)
            bld.op("vector", lambda e: e.tensor_copy(out=wo_bf[:], in_=wo_stage[:]), s_ve)
            bld.wait("sync", s_ve)
            bld.dma("sync", woB_scr[dm], wo_bf[:], s_st)

        st_f = [lvlA.enter_context(nc.sbuf_tensor(f"st_f{i}", [128, DM], F32)) for i in range(2)]
        ab_b = [lvlA.enter_context(nc.sbuf_tensor(f"ab_b{i}", [128, DM], BF16)) for i in range(2)]

        def mat_pieces(wsrc):
            n_rows, width = wsrc.shape
            for si in range(n_rows // 128):
                for qj in range(width // DM):
                    yield wsrc[si * 128:(si + 1) * 128, qj * DM:(qj + 1) * DM], si, qj

        mats = [("g", wg, FF * DM, sw_g, isw_g),
                ("u", wu, FF * DM, sw_u, isw_u),
                ("d", wd, DM * FF, sw_d, isw_d)]

        for tag, wsrc, nel, sw_pp, isw_pp in mats:
            pieces = list(mat_pieces(wsrc))
            ab_mm = [None, None]
            ab_t = [None, None]
            for pi, (rows, si, qj) in enumerate(pieces):
                if ab_t[pi % 2] is not None:
                    bld.wait("sync", s_se, ab_t[pi % 2])
                else:
                    bld.wait("sync", s_se)
                t_l = bld.dma("sync", st_f[pi % 2][:], rows, s_sp[pi % 2])
                bld.wait("scalar", s_sp[pi % 2], t_l)
                if ab_mm[pi % 2] is not None:
                    bld.wait("scalar", s_te, ab_mm[pi % 2])
                ab_t[pi % 2] = bld.op("scalar", lambda e, b=pi % 2: e.activation(
                    out=ab_b[b][:], in_=st_f[b][:], func=AF.Abs), s_se)
                bld.wait("tensor", s_se, ab_t[pi % 2])
                for q in range(4):
                    fq = (pi == 0 and q == 0)
                    lq = (pi == len(pieces) - 1 and q == 3)
                    tv = bld.op("tensor", lambda e, b=pi % 2, q=q, f=fq, l=lq: e.matmul(
                        p1[0:1, :], onesb[:, 0:1], ab_b[b][:, q * 512:(q + 1) * 512],
                        start=f, stop=l), s_te if q == 3 else None)
                    if q == 3:
                        ab_mm[pi % 2] = tv
            bld.wait("vector", s_te)
            bld.op("vector", lambda e: e.tensor_reduce(out=st1[0:1, 0:1], in_=p1[0:1, :], axis=AX.X, op=ALU.add), s_ve)
            bld.wait("vector", s_ve)
            bld.op("vector", lambda e, n=nel: e.tensor_scalar(
                out=st1[0:1, 1:2], in0=st1[0:1, 0:1], scalar1=1.0 / n, scalar2=EPS_Q,
                op0=ALU.mult, op1=ALU.max), s_ve)
            bld.wait("vector", s_ve)
            bld.op("vector", lambda e: e.reciprocal(out=st1[0:1, 2:3], in_=st1[0:1, 1:2]), s_ve)
            bld.wait("vector", s_ve)
            bld.op("vector", lambda e: e.reciprocal(out=st1[0:1, 3:4], in_=st1[0:1, 2:3]), s_ve)
            bld.wait("tensor", s_ve)
            bld.op("tensor", lambda e: e.matmul(pb[4][:, 0:1], onesf[0:1, :], st1[0:1, 2:3], start=True, stop=True))
            bld.op("tensor", lambda e: e.matmul(pb[5][:, 0:1], onesf[0:1, :], st1[0:1, 3:4], start=True, stop=True), s_te)
            bld.wait("vector", s_te)
            bld.op("vector", lambda e, d=sw_pp: e.tensor_copy(out=d[:], in_=pb[4][:, 0:1]))
            bld.op("vector", lambda e, d=isw_pp: e.tensor_copy(out=d[:], in_=pb[5][:, 0:1]), s_ve)
            bld.wait("tensor", s_ve)

        v_f = lvlA.enter_context(nc.sbuf_tensor("v_f", [128, DM], F32))
        a_f = lvlA.enter_context(nc.sbuf_tensor("a_f", [128, DM], F32))
        t_bt = [lvlA.enter_context(nc.sbuf_tensor(f"t_bt{i}", [128, DM], BF16)) for i in range(2)]
        tT_b = [lvlA.enter_context(nc.sbuf_tensor(f"tT_b{i}", [128, DM], BF16)) for i in range(2)]
        guard_sw = bld.val(s_ve)
        bld.wait("vector", s_ve, guard_sw)
        tern_state = {"stt": [None, None], "tr": [None, None], "st": [None, None], "ev": [None, None]}

        def tern_piece(rows, sw_pp, dst, buf_i):
            ts_ = tern_state
            bld.wait("sync", s_ve)
            t_l = bld.dma("sync", st_f[buf_i][:], rows, s_sp[buf_i])
            bld.wait("vector", s_sp[buf_i], t_l)
            bld.op("vector", lambda e, b=buf_i, s=sw_pp: e.tensor_scalar(
                out=v_f[:], in0=st_f[b][:], scalar1=s[:], scalar2=None, op0=ALU.mult))
            bld.op("vector", lambda e: e.tensor_scalar(
                out=a_f[:], in0=v_f[:], scalar1=-0.5, scalar2=None, op0=ALU.is_le))
            if ts_["tr"][buf_i] is not None:
                bld.wait("vector", s_te, ts_["tr"][buf_i])
            ts_["stt"][buf_i] = bld.op("vector", lambda e, b=buf_i: e.scalar_tensor_tensor(
                out=t_bt[b][:], in0=v_f[:], scalar=0.5, in1=a_f[:],
                op0=ALU.is_ge, op1=ALU.subtract), s_ve)
            bld.wait("tensor", s_ve, ts_["stt"][buf_i])
            for cg in range(4):
                if cg > 0:
                    bld.wait("tensor", s_ve)
                for j in range(4):
                    c = cg * 4 + j
                    tv = bld.op("tensor", lambda e, b=buf_i, c=c, j=j: e.transpose(
                        pbf[:, j * 128:(j + 1) * 128], t_bt[b][:, c * 128:(c + 1) * 128], eyeb[:]),
                        s_te if j == 3 else None)
                    if j == 3:
                        ts_["tr"][buf_i] = tv
                bld.wait("vector", s_te, ts_["tr"][buf_i])
                if ts_["st"][buf_i] is not None:
                    bld.wait("vector", s_st, ts_["st"][buf_i])
                ts_["ev"][buf_i] = bld.op("vector", lambda e, b=buf_i, cg=cg: e.tensor_copy(
                    out=tT_b[b][:, cg * 512:(cg + 1) * 512], in_=pbf[:]), s_ve)
            bld.wait("sync", s_ve, ts_["ev"][buf_i])
            ts_["st"][buf_i] = bld.dma("sync", dst, tT_b[buf_i][:], s_st)

        for rows, si, qj in mat_pieces(wg):
            tern_piece(rows, sw_g, wgT_scr[si], (si) % 2)
        for rows, si, qj in mat_pieces(wu):
            tern_piece(rows, sw_u, wuT_scr[si], (si) % 2)
        for rows, si, qj in mat_pieces(wd):
            tern_piece(rows, sw_d, wdT_scr[si].rearrange("p (q f) -> p q f", q=4)[:, qj, :], (si * 4 + qj) % 2)

        t_all_st = bld.val(s_st)
        for eng in Bld.ENGS:
            bld.wait(eng, s_st, t_all_st)
        lvlA.close()
        bld.barrier(s_ph)

        if debug_phase == "A":
            dbg_t = sbuf("dbg_t", 256, F32)
            bld.op("vector", lambda e: e.memset(dbg_t[:], 0.0))
            bld.op("vector", lambda e: e.tensor_copy(out=dbg_t[0:1, 128:192], in_=qmax2[0:1, :]))
            bld.op("vector", lambda e: e.tensor_copy(out=dbg_t[0:1, 192:196], in_=kmax2[0:1, :]))
            bld.op("vector", lambda e: e.tensor_copy(out=dbg_t[:, 0:NH * NT], in_=negB[:]))
            bld.op("vector", lambda e: e.tensor_copy(out=dbg_t[:, 64:65], in_=sw_g[:]))
            bld.op("vector", lambda e: e.tensor_copy(out=dbg_t[:, 65:66], in_=sw_u[:]))
            bld.op("vector", lambda e: e.tensor_copy(out=dbg_t[:, 66:67], in_=sw_d[:]))
            bld.op("vector", lambda e: e.tensor_copy(out=dbg_t[:, 67:68], in_=isw_d[:]))
            bld.op("vector", lambda e: e.tensor_copy(out=dbg_t[:, 68:69], in_=bv_bc[:, 3:4]), s_ve)
            bld.wait("sync", s_ve)
            bld.dma("sync", dbgA, dbg_t[:], s_st)
            bld.wait("sync", s_st)
            z_t = sbuf("z_t", DM, F32)
            bld.op("vector", lambda e: e.memset(z_t[:], 0.0), s_ve)
            bld.wait("sync", s_ve)
            for i in range(TOK // 128):
                bld.dma("sync", out[i * 128:(i + 1) * 128, :], z_t[:], s_st)
            bld.wait("sync", s_st)
            block = stack.enter_context(nc.Block())
            bld.run(block)
            return nc

        build_phase_b(nc, stack, bld, dict(
            n_tiles=(1 if debug_phase == "B1" else NT),
            dump_tt=(0 if debug_phase == "B1" else (NT - 1 if debug_phase == "B4" else (-2 if debug_phase == "BX" else -1))),
            dbgF=dbgF, dbgS=dbgS,
            out=out, x_own=x_own, q_scr=q_scr, k_scr=k_scr, v_scr=v_scr,
            wgT_scr=wgT_scr, wuT_scr=wuT_scr, wdT_scr=wdT_scr, woB_scr=woB_scr,
            s1_scr=s1_scr,
            h128=h128, eyeb=eyeb, eyef=eyef, onesb=onesb, onesf=onesf,
            bo_t=bo_t, negB=negB, eps_t=eps_t, neg1_t=neg1_t,
            isw_g=isw_g, isw_u=isw_u, isw_d=isw_d,
            strow=strow, st1=st1, pb=pb, pbf=pbf, p1=p1,
            s_ld=s_ld, s_ld2=s_ld2, s_ld3=s_ld3, s_st=s_st,
            s_te=s_te, s_ve=s_ve, s_se=s_se, s_ph=s_ph,
        ))
        block = stack.enter_context(nc.Block())
        bld.run(block)
    return nc


def build_phase_b(nc, stack, bld, env):
    g = env
    out, x_own = g["out"], g["x_own"]
    q_scr, k_scr, v_scr = g["q_scr"], g["k_scr"], g["v_scr"]
    wgT_scr, wuT_scr, wdT_scr, woB_scr = g["wgT_scr"], g["wuT_scr"], g["wdT_scr"], g["woB_scr"]
    s1_scr = g["s1_scr"]
    h128, eyeb, eyef, onesb, onesf = g["h128"], g["eyeb"], g["eyef"], g["onesb"], g["onesf"]
    bo_t, negB, eps_t, neg1_t = g["bo_t"], g["negB"], g["eps_t"], g["neg1_t"]
    isw_g, isw_u, isw_d = g["isw_g"], g["isw_u"], g["isw_d"]
    strow, st1, pb, pbf, p1 = g["strow"], g["st1"], g["pb"], g["pbf"], g["p1"]
    s_ld, s_ld2, s_ld3, s_st = g["s_ld"], g["s_ld2"], g["s_ld3"], g["s_st"]
    s_te, s_ve, s_se, s_ph = g["s_te"], g["s_ve"], g["s_se"], g["s_ph"]

    s_sc = bld.sem("s_sc")
    s_ex = bld.sem("s_ex")
    s_pv = bld.sem("s_pv")
    s_sm = bld.sem("s_sm")
    s_pp = bld.sem("s_pp")
    s_wp = [bld.sem("s_wpA"), bld.sem("s_wpB")]
    s_gu = [bld.sem("s_guA"), bld.sem("s_guB")]
    s_dn = [bld.sem("s_dnA"), bld.sem("s_dnB")]

    def sbufB(st_, name, cols, dt=BF16, rows=128):
        return st_.enter_context(nc.sbuf_tensor(name, [rows, cols], dt))

    lvlB = ExitStack()
    xres = [sbufB(lvlB, f"xres{i}", DM, F32) for i in range(2)]
    sq_f = sbufB(lvlB, "sq_f", DM, F32)
    br_sub = sbufB(lvlB, "br_sub", DM, BF16)
    stt = sbufB(lvlB, "stt", 32, F32)
    inv1 = sbufB(lvlB, "inv1", SUB, F32)
    cs2 = sbufB(lvlB, "cs2", SUB, F32)
    s2c = sbufB(lvlB, "s2c", SUB, F32)
    inv1B = sbufB(lvlB, "inv1B", 512, F32)
    s2B = sbufB(lvlB, "s2B", 512, F32)
    cs2B = sbufB(lvlB, "cs2B", 512, F32)

    C_SX, C_SX2, C_MEAN, C_EX2, C_M2T, C_VAR, C_LNV, C_RSTD, C_NMR = range(9)
    C_MRAW, C_M, C_RS, C_SACT = 9, 10, 11, 12

    def col(c):
        return stt[:, c:c + 1]

    def pp(eng, fn):
        bld.wait(eng, s_pp)
        return bld.op(eng, fn, s_pp)

    dbgF, dbgS = g.get("dbgF"), g.get("dbgS")
    dump_tt = g.get("dump_tt", -1)
    global DUMP_MASK
    cur_tt = [0]

    def spoint(k):
        if k not in SERIAL_POINTS:
            return
        for eng in ("vector", "scalar", "tensor"):
            sem = {"vector": s_ve, "scalar": s_se, "tensor": s_te}[eng]
            bld.op(eng, lambda e: e.nop(), sem)
        for eng in ("vector", "scalar", "tensor", "sync"):
            bld.wait(eng, s_ve)
            bld.wait(eng, s_se)
            bld.wait(eng, s_te)

    def _dump_on(k):
        return dbgF is not None and (cur_tt[0] == dump_tt or dump_tt == -2) and k in DUMP_MASK

    def dump_slots(k, pos_list):
        if not _dump_on(k):
            return
        for eng in ("vector", "scalar", "tensor"):
            pass
        bld.wait("sync", s_ve)
        bld.wait("sync", s_se)
        bld.wait("sync", s_te)
        for oc in range(NC_FF):
            bld.dma("sync", dbgF[k][:, oc * 512:(oc + 1) * 512],
                    Xs_cur[0][:, pos_list[oc] * 512:(pos_list[oc] + 1) * 512], s_st)
        bld.wait("sync", s_st)
        for eng in ("vector", "scalar", "tensor"):
            bld.wait(eng, s_st)

    def dump_tile(k, ap_src, width):
        if not _dump_on(k):
            return
        bld.wait("sync", s_ve)
        bld.wait("sync", s_se)
        bld.wait("sync", s_te)
        bld.dma("sync", dbgF[k][:, 0:width], ap_src, s_st)
        bld.wait("sync", s_st)
        for eng in ("vector", "scalar", "tensor"):
            bld.wait(eng, s_st)

    Xs_cur = [None]

    def ln_block(raw_slice, dst_slice, sub_i):
        bld.wait("scalar", s_ve)
        bld.op("scalar", lambda e: e.activation(out=xres[sub_i % 2][:], in_=raw_slice, func=AF.Square), s_se)
        bld.op("vector", lambda e: e.tensor_reduce(out=col(C_SX), in_=raw_slice, axis=AX.X, op=ALU.add), s_ve)
        bld.wait("vector", s_se)
        bld.op("vector", lambda e: e.tensor_reduce(out=col(C_SX2), in_=xres[sub_i % 2][:], axis=AX.X, op=ALU.add), s_ve)
        pp("vector", lambda e: e.tensor_scalar(out=col(C_MEAN), in0=col(C_SX), scalar1=1.0 / DM, scalar2=None, op0=ALU.mult))
        pp("vector", lambda e: e.tensor_scalar(out=col(C_EX2), in0=col(C_SX2), scalar1=1.0 / DM, scalar2=None, op0=ALU.mult))
        pp("vector", lambda e: e.tensor_scalar(out=col(C_M2T), in0=col(C_MEAN), scalar1=col(C_MEAN), scalar2=None, op0=ALU.mult))
        pp("vector", lambda e: e.tensor_tensor(out=col(C_VAR), in0=col(C_EX2), in1=col(C_M2T), op=ALU.subtract))
        bld.wait("scalar", s_pp)
        pp("scalar", lambda e: e.activation(out=col(C_LNV), in_=col(C_VAR), func=AF.Ln, bias=eps_t[:]))
        pp("scalar", lambda e: e.activation(out=col(C_RSTD), in_=col(C_LNV), func=AF.Exp, scale=-0.5))
        bld.wait("vector", s_pp)
        pp("vector", lambda e: e.tensor_scalar(out=col(C_NMR), in0=col(C_RSTD), scalar1=col(C_MEAN), scalar2=-1.0,
                                               op0=ALU.mult, op1=ALU.mult))
        bld.wait("scalar", s_pp)
        bld.op("scalar", lambda e: e.activation(out=dst_slice, in_=raw_slice, func=AF.Identity,
                                                scale=col(C_RSTD), bias=col(C_NMR)), s_se)

    def emit_attn(TT, stream1, v_sb, k_sb, q_sb, pT, sumC, OT, wos, brF, recipB):
        ncol0 = TT
        for kt in range(KT):
            bld.dma("sync", v_sb[:, kt * 512:(kt + 1) * 512], v_scr[kt], s_ld)
        for kv in range(NKV):
            for st in range(NST_ALL):
                bld.dma("scalar", k_sb[:, kv * S + st * 512: kv * S + (st + 1) * 512], k_scr[kv, st], s_ld2)
        for h in range(NH):
            bld.dma("gpsimd", q_sb[:, h * 512:(h + 1) * 512], q_scr[h, TT], s_ld3)
        bld.wait("tensor", s_ld)
        bld.wait("tensor", s_ld2)
        bld.wait("tensor", s_ld3)

        for kv in range(NKV):
            for gq in range(4):
                h = kv * 4 + gq
                ncol = h * NT + TT
                bld.wait("tensor", s_se)
                bld.wait("tensor", s_ve)
                bld.wait("vector", s_te)
                bld.wait("scalar", s_pv)
                bld.wait("scalar", s_sm)
                sc_t = [None] * KT
                ex_t = [None] * KT
                pv_t = [None] * KT
                sm_t = [None] * KT

                SBK = (0, 1, 3)

                def mm_sc(kt, kv=kv, h=h):
                    sc_t[kt] = bld.op("tensor", lambda e, kt=kt: e.matmul(
                        pb[SBK[kt % 3]][:], k_sb[:, kv * S + kt * 128: kv * S + (kt + 1) * 128],
                        q_sb[:, h * 512:(h + 1) * 512], start=True, stop=True,
                        skip_group_check=True), s_sc)

                mm_sc(0)
                mm_sc(1)
                mm_sc(2)
                for kt in range(KT):
                    bld.wait("scalar", s_sc, sc_t[kt])
                    if kt >= 3:
                        bld.wait("scalar", s_pv, pv_t[kt - 3])
                        bld.wait("scalar", s_sm, sm_t[kt - 3])
                    ex_t[kt] = bld.op("scalar", lambda e, kt=kt, ncol=ncol: e.activation(
                        out=pT[:, (kt % 3) * 512:(kt % 3 + 1) * 512], in_=pb[SBK[kt % 3]][:],
                        func=AF.Exp, scale=SM, bias=negB[:, ncol:ncol + 1]), s_ex)
                    bld.wait("tensor", s_ex, ex_t[kt])
                    pv_t[kt] = bld.op("tensor", lambda e, kt=kt, kv=kv: e.matmul(
                        pb[2][:], v_sb[:, kt * 512 + kv * 128: kt * 512 + (kv + 1) * 128],
                        pT[:, (kt % 3) * 512:(kt % 3 + 1) * 512],
                        start=(kt == 0), stop=(kt == KT - 1), skip_group_check=True), s_pv)
                    if kt + 3 < KT:
                        mm_sc(kt + 3)
                    bld.wait("vector", s_ex, ex_t[kt])
                    if kt == 0:
                        sm_t[kt] = bld.op("vector", lambda e: e.tensor_copy(
                            out=sumC[:], in_=pT[:, 0:512]), s_sm)
                    else:
                        sm_t[kt] = bld.op("vector", lambda e, kt=kt: e.tensor_tensor(
                            out=sumC[:], in0=sumC[:], in1=pT[:, (kt % 3) * 512:(kt % 3 + 1) * 512],
                            op=ALU.add), s_sm)
                bld.wait("tensor", s_sm, sm_t[KT - 1])
                bld.wait("tensor", s_ve)
                t_ones = bld.op("tensor", lambda e: e.matmul(
                    p1[0:1, :], onesb[:, 0:1], sumC[:], start=True, stop=True), s_te)
                bld.wait("vector", s_te, t_ones)
                t_rc = bld.op("vector", lambda e: e.reciprocal(out=strow[0:1, :], in_=p1[0:1, :]), s_ve)
                bld.wait("tensor", s_ve, t_rc)
                t_bc = bld.op("tensor", lambda e: e.matmul(
                    pb[4][:], onesf[0:1, :], strow[0:1, :], start=True, stop=True), s_te)
                bld.wait("vector", s_te, t_bc)
                bld.op("vector", lambda e: e.tensor_copy(out=recipB[:], in_=pb[4][:]), s_ve)
                bld.wait("vector", s_pv, pv_t[KT - 1])
                bld.op("vector", lambda e, h=h: e.tensor_tensor(
                    out=OT[:, h * 512:(h + 1) * 512], in0=pb[2][:], in1=recipB[:], op=ALU.mult), s_ve)

        def _emit_outproj_dm(dm, ld_tgt):
            bld.wait("tensor", s_wp[dm % 2], ld_tgt)
            if dm >= 2:
                bld.wait("tensor", s_se, ev_t[dm - 2])
            for hd in range(NC_DM):
                tv = bld.op("tensor", lambda e, dm=dm, hd=hd: e.matmul(
                    pb[5 if dm % 2 == 0 else 3][:],
                    wos[:, (dm % 2) * DM + hd * 128:(dm % 2) * DM + (hd + 1) * 128],
                    OT[:, hd * 512:(hd + 1) * 512],
                    start=(hd == 0), stop=(hd == NC_DM - 1), skip_group_check=True),
                    s_te if hd == NC_DM - 1 else None)
                if hd == NC_DM - 1:
                    mm_t[dm] = tv
            bld.wait("scalar", s_te, mm_t[dm])
            ev_t[dm] = bld.op("scalar", lambda e, dm=dm: e.activation(
                out=brF[:, dm * 512:(dm + 1) * 512], in_=pb[5 if dm % 2 == 0 else 3][:],
                func=AF.Identity, bias=bo_t[:, dm:dm + 1]), s_se)

        mm_t = [None] * NC_DM
        ev_t = [None] * NC_DM
        ld_t_arr = [None] * NC_DM
        bld.wait("sync", s_ve)
        for dm in range(NC_DM):
            if dm >= 2:
                bld.wait("sync", s_te, mm_t[dm - 2])
            ld_t_arr[dm] = bld.dma("sync", wos[:, (dm % 2) * DM:(dm % 2 + 1) * DM], woB_scr[dm], s_wp[dm % 2])
            _emit_outproj_dm(dm, ld_t_arr[dm])


        for i in range(SUB):
            bld.wait("sync", s_ve)
            t_x = bld.dma("sync", xres[i % 2][:], x_own[(TT * TILE + i * 128):(TT * TILE + (i + 1) * 128), :], s_ld)
            bld.wait("tensor", s_se)
            for cg in range(4):
                bld.wait("tensor", s_ve)
                for j in range(4):
                    dm = cg * 4 + j
                    bld.op("tensor", lambda e, dm=dm, j=j, i=i: e.transpose(
                        pbf[:, j * 128:(j + 1) * 128],
                        brF[:, dm * 512 + i * 128: dm * 512 + (i + 1) * 128], eyeb[:]),
                        s_te if j == 3 else None)
                bld.wait("vector", s_te)
                bld.op("vector", lambda e, cg=cg: e.tensor_copy(
                    out=br_sub.rearrange("p (c f) -> p c f", c=NC_DM)[:, cg * 4:(cg + 1) * 4, :],
                    in_=pbf.rearrange("p (j f) -> p j f", j=4)[:]), s_ve)
            bld.wait("vector", s_ld, t_x)
            bld.wait("vector", s_se)
            bld.op("vector", lambda e, i=i: e.tensor_tensor(
                out=sq_f[:], in0=xres[i % 2][:], in1=br_sub[:], op=ALU.add), s_ve)
            ln_block(sq_f[:], stream1[:, i * DM:(i + 1) * DM], i)
            bld.wait("sync", s_se)
            bld.dma("sync", s1_scr[i], stream1[:, i * DM:(i + 1) * DM], s_st)



    def emit_ffn(TT, Xs, xq_f, arena, dslab2, k0i_t, s1f, uq, k0q, dq, gs, tu, maxC, brD):
        for i in range(SUB):
            bld.wait("sync", s_ve)
            t_s1 = bld.dma("sync", s1f[:], s1_scr[i], s_ld)
            bld.wait("vector", s_ld, t_s1)
            pp("vector", lambda e: e.tensor_reduce(
                out=col(C_MRAW), in_=s1f[:], axis=AX.X, op=ALU.max, apply_absolute_value=True))
            pp("vector", lambda e: e.tensor_scalar(out=col(C_M), in0=col(C_MRAW), scalar1=EPS_Q,
                                                   scalar2=None, op0=ALU.max))
            pp("vector", lambda e: e.reciprocal(out=col(C_RS), in_=col(C_M)))
            pp("vector", lambda e: e.tensor_scalar(out=col(C_SACT), in0=col(C_RS), scalar1=128.0,
                                                   scalar2=None, op0=ALU.mult))
            pp("vector", lambda e, i=i: e.tensor_scalar(out=inv1[:, i:i + 1], in0=col(C_M),
                                                        scalar1=1.0 / 128.0, scalar2=None, op0=ALU.mult))
            bld.wait("vector", s_te)
            bld.op("vector", lambda e: e.tensor_scalar(
                out=sq_f[:], in0=s1f[:], scalar1=col(C_SACT), scalar2=None, op0=ALU.mult))
            bld.op("vector", lambda e: e.tensor_copy(out=k0i_t[:], in_=sq_f[:]))
            bld.op("vector", lambda e, i=i: e.tensor_tensor(
                out=xres[i % 2][:], in0=sq_f[:], in1=k0i_t[:], op=ALU.subtract))
            bld.op("vector", lambda e, i=i: e.scalar_tensor_tensor(
                out=sq_f[:], in0=xres[i % 2][:], scalar=0.5, in1=k0i_t[:], op0=ALU.is_gt, op1=ALU.add))
            bld.op("vector", lambda e, i=i: e.scalar_tensor_tensor(
                out=br_sub[:], in0=xres[i % 2][:], scalar=-0.5, in1=sq_f[:], op0=ALU.is_ge, op1=ALU.add))
            bld.op("vector", lambda e: e.tensor_scalar(
                out=sq_f[:], in0=br_sub[:], scalar1=128.0, scalar2=-127.0, op0=ALU.min, op1=ALU.max), s_ve)
            bld.wait("scalar", s_ve)
            bld.op("scalar", lambda e: e.activation(out=br_sub[:], in_=sq_f[:], func=AF.Identity,
                                                    bias=neg1_t[:]), s_se)
            bld.wait("tensor", s_se)
            for cg in range(4):
                bld.wait("tensor", s_ve)
                for j in range(4):
                    c = cg * 4 + j
                    bld.op("tensor", lambda e, c=c, j=j: e.transpose(
                        pbf[:, j * 128:(j + 1) * 128], br_sub[:, c * 128:(c + 1) * 128], eyeb[:]),
                        s_te if j == 3 else None)
                bld.wait("vector", s_te)
                bld.op("vector", lambda e, cg=cg, i=i: e.tensor_copy(
                    out=xq_f.rearrange("p (c f) -> p c f", c=NC_DM)[:, cg * 4:(cg + 1) * 4, i * 128:(i + 1) * 128],
                    in_=pbf.rearrange("p (j f) -> p j f", j=4)[:]), s_ve)

        spoint(0)
        dump_tile(0, xq_f[:], NC_DM * 512)
        bld.wait("tensor", s_pp)
        bld.wait("tensor", s_ve)
        for i in range(SUB):
            bld.op("tensor", lambda e, i=i: e.transpose(
                p1[0:1, i * 128:(i + 1) * 128], inv1[:, i:i + 1], eyef[:]),
                s_te if i == SUB - 1 else None)
        bld.wait("vector", s_te)
        t_cp = bld.op("vector", lambda e: e.tensor_copy(out=strow[0:1, :], in_=p1[0:1, :]), s_ve)
        bld.wait("tensor", s_ve, t_cp)
        t_bc = bld.op("tensor", lambda e: e.matmul(pb[4][:], onesf[0:1, :], strow[0:1, :],
                                                   start=True, stop=True), s_te)
        bld.wait("vector", s_te, t_bc)
        t_i1 = bld.op("vector", lambda e: e.tensor_copy(out=inv1B[:], in_=pb[4][:]), s_ve)
        bld.wait("tensor", s_ve, t_i1)

        gmm = [None] * NC_FF
        umm = [None] * NC_FF
        gev = [None] * NC_FF
        uev = [None] * NC_FF
        xev = [None] * NC_FF
        bld.wait("vector", s_ve)
        gu_ld = [None] * NC_FF
        for oc in range(NC_FF):
            sl = oc % 2
            if oc >= 2:
                bld.wait("sync", s_te, umm[oc - 2])
            bld.dma("sync", arena[:, sl * 2048:(sl + 1) * 2048], wgT_scr[oc], s_gu[sl])
            gu_ld[oc] = bld.dma("sync", arena[:, (2 + sl) * 2048:(3 + sl) * 2048], wuT_scr[oc], s_gu[sl])
            bld.wait("tensor", s_gu[sl], gu_ld[oc])
            if oc >= 2:
                bld.wait("tensor", s_ve, xev[oc - 2])
            for hc in range(NC_DM):
                tv = bld.op("tensor", lambda e, hc=hc, sl=sl: e.matmul(
                    pb[sl][:], arena[:, sl * 2048 + hc * 128: sl * 2048 + (hc + 1) * 128],
                    xq_f[:, hc * 512:(hc + 1) * 512],
                    start=(hc == 0), stop=(hc == NC_DM - 1), skip_group_check=True),
                    s_te if hc == NC_DM - 1 else None)
                if hc == NC_DM - 1:
                    gmm[oc] = tv
            for hc in range(NC_DM):
                tv = bld.op("tensor", lambda e, hc=hc, sl=sl: e.matmul(
                    pb[2 + sl][:], arena[:, (2 + sl) * 2048 + hc * 128: (2 + sl) * 2048 + (hc + 1) * 128],
                    xq_f[:, hc * 512:(hc + 1) * 512],
                    start=(hc == 0), stop=(hc == NC_DM - 1), skip_group_check=True),
                    s_te if hc == NC_DM - 1 else None)
                if hc == NC_DM - 1:
                    umm[oc] = tv
            bld.wait("vector", s_te, gmm[oc])
            bld.wait("vector", s_se)
            bld.op("vector", lambda e, sl=sl: e.scalar_tensor_tensor(
                out=tu[:, sl * 512:(sl + 1) * 512], in0=pb[sl][:], scalar=isw_g[:], in1=inv1B[:],
                op0=ALU.mult, op1=ALU.mult), s_ve)
            bld.wait("scalar", s_ve)
            gev[oc] = bld.op("scalar", lambda e, sl=sl: e.activation(
                out=gs[:, sl * 512:(sl + 1) * 512], in_=tu[:, sl * 512:(sl + 1) * 512], func=AF.Silu), s_se)
            bld.wait("vector", s_te, umm[oc])
            bld.wait("vector", s_se, gev[oc])
            uev[oc] = bld.op("vector", lambda e, sl=sl: e.scalar_tensor_tensor(
                out=tu[:, sl * 512:(sl + 1) * 512], in0=pb[2 + sl][:], scalar=isw_u[:], in1=inv1B[:],
                op0=ALU.mult, op1=ALU.mult), s_ve)
            xev[oc] = bld.op("vector", lambda e, oc=oc, sl=sl: e.tensor_tensor(
                out=Xs[:, oc * 512:(oc + 1) * 512], in0=gs[:, sl * 512:(sl + 1) * 512],
                in1=tu[:, sl * 512:(sl + 1) * 512], op=ALU.mult), s_ve)

        spoint(1)
        dump_slots(1, list(range(NC_FF)))
        sa_ev = [None] * NC_FF
        for oc in range(NC_FF):
            bank = 4 + (oc % 2)
            if oc >= 2:
                bld.wait("tensor", s_ve, sa_ev[oc - 2])
            bld.wait("tensor", s_ve, xev[oc])
            bld.op("tensor", lambda e, oc=oc, bank=bank: e.matmul(
                pb[bank][:], h128[:], Xs[:, oc * 512:(oc + 1) * 512],
                start=True, stop=True, skip_group_check=True), s_te)
            bld.wait("vector", s_te)
            sa_ev[oc] = bld.op("vector", lambda e, oc=oc, bank=bank: e.tensor_copy(
                out=Xs[:, oc * 512:(oc + 1) * 512], in_=pb[bank][:]), s_ve)

        spoint(2)
        dump_slots(2, list(range(NC_FF)))
        def slot(pos_idx):
            return Xs[:, pos_idx * 512:(pos_idx + 1) * 512]

        pos = list(range(NC_FF))
        spare = [NC_FF, NC_FF + 1]
        for sstride in (32, 16, 8, 4, 2, 1):
            for i0 in range(NC_FF):
                if i0 & sstride:
                    continue
                j0 = i0 | sstride
                pa, pbx = pos[i0], pos[j0]
                ns0, ns1 = spare
                bld.op("vector", lambda e, a=pa, b=pbx, o=ns0: e.tensor_tensor(
                    out=slot(o), in0=slot(a), in1=slot(b), op=ALU.add))
                bld.op("vector", lambda e, a=pa, b=pbx, o=ns1: e.tensor_tensor(
                    out=slot(o), in0=slot(a), in1=slot(b), op=ALU.subtract))
                pos[i0], pos[j0] = ns0, ns1
                spare = [pa, pbx]
        bld.op("vector", lambda e: e.engine_nop(), s_ve)
        spoint(3)

        if True:
            dump_slots(3, list(pos))
        bld.op("vector", lambda e: e.memset(maxC[:], 0.0))
        ab_t = [None] * NC_FF
        mx_t = [None] * NC_FF
        for oc in range(NC_FF):
            sl = oc % 2
            if oc >= 2:
                bld.wait("scalar", s_sm, mx_t[oc - 2])
            bld.wait("scalar", s_ve)
            ab_t[oc] = bld.op("scalar", lambda e, oc=oc, sl=sl: e.activation(
                out=gs[:, sl * 512:(sl + 1) * 512], in_=slot(pos[oc]), func=AF.Abs), s_se)
            bld.wait("vector", s_se, ab_t[oc])
            mx_t[oc] = bld.op("vector", lambda e, sl=sl: e.tensor_tensor(
                out=maxC[:], in0=maxC[:], in1=gs[:, sl * 512:(sl + 1) * 512], op=ALU.max), s_sm)
        bld.op("vector", lambda e: e.engine_nop(), s_ve)
        bld.wait("tensor", s_ve)
        for i in range(SUB):
            bld.op("tensor", lambda e, i=i: e.transpose(
                pb[4][:, i * 128:(i + 1) * 128], maxC[:, i * 128:(i + 1) * 128], eyef[:]),
                s_te if i == SUB - 1 else None)
        bld.wait("vector", s_te)
        for i in range(SUB):
            pp("vector", lambda e, i=i: e.tensor_reduce(
                out=col(C_MRAW), in_=pb[4][:, i * 128:(i + 1) * 128], axis=AX.X, op=ALU.max))
            pp("vector", lambda e: e.tensor_scalar(out=col(C_M), in0=col(C_MRAW), scalar1=CF,
                                                   scalar2=EPS_Q, op0=ALU.mult, op1=ALU.max))
            pp("vector", lambda e: e.reciprocal(out=col(C_RS), in_=col(C_M)))
            pp("vector", lambda e, i=i: e.tensor_scalar(out=s2c[:, i:i + 1], in0=col(C_RS), scalar1=128.0 * CF,
                                                        scalar2=None, op0=ALU.mult))
            pp("vector", lambda e: e.tensor_scalar(out=col(C_SACT), in0=col(C_M), scalar1=1.0 / 128.0,
                                                   scalar2=None, op0=ALU.mult))
            pp("vector", lambda e, i=i: e.tensor_scalar(out=cs2[:, i:i + 1], in0=col(C_SACT),
                                                        scalar1=isw_d[:], scalar2=None, op0=ALU.mult))
        bld.wait("tensor", s_pp)
        bld.wait("tensor", s_ve)
        for i in range(SUB):
            bld.op("tensor", lambda e, i=i: e.transpose(
                p1[0:1, i * 128:(i + 1) * 128], s2c[:, i:i + 1], eyef[:]))
        for i in range(SUB):
            bld.op("tensor", lambda e, i=i: e.transpose(
                pb[5][0:1, i * 128:(i + 1) * 128], cs2[:, i:i + 1], eyef[:]),
                s_te if i == SUB - 1 else None)
        bld.wait("vector", s_te)
        t_c1 = bld.op("vector", lambda e: e.tensor_copy(out=strow[0:1, :], in_=p1[0:1, :]), s_ve)
        bld.wait("tensor", s_ve, t_c1)
        t_b1 = bld.op("tensor", lambda e: e.matmul(pb[4][:], onesf[0:1, :], strow[0:1, :],
                                                   start=True, stop=True), s_te)
        bld.wait("vector", s_te, t_b1)
        bld.op("vector", lambda e: e.tensor_copy(out=s2B[:], in_=pb[4][:]))
        t_c2 = bld.op("vector", lambda e: e.tensor_copy(out=strow[0:1, :], in_=pb[5][0:1, :]), s_ve)
        bld.wait("tensor", s_ve, t_c2)
        t_b2 = bld.op("tensor", lambda e: e.matmul(pb[4][:], onesf[0:1, :], strow[0:1, :],
                                                   start=True, stop=True), s_te)
        bld.wait("vector", s_te, t_b2)
        bld.op("vector", lambda e: e.tensor_copy(out=cs2B[:], in_=pb[4][:]), s_ve)

        for oc in range(NC_FF):
            bld.op("vector", lambda e, oc=oc: e.tensor_tensor(
                out=uq[:], in0=slot(pos[oc]), in1=s2B[:], op=ALU.mult))
            bld.op("vector", lambda e: e.tensor_copy(out=k0q[:], in_=uq[:]))
            bld.op("vector", lambda e: e.tensor_tensor(out=dq[:], in0=uq[:], in1=k0q[:], op=ALU.subtract))
            bld.op("vector", lambda e: e.scalar_tensor_tensor(
                out=tu[:, 0:512], in0=dq[:], scalar=0.5, in1=k0q[:], op0=ALU.is_gt, op1=ALU.add))
            bld.op("vector", lambda e: e.scalar_tensor_tensor(
                out=uq[:], in0=dq[:], scalar=-0.5, in1=tu[:, 0:512], op0=ALU.is_ge, op1=ALU.add))
            bld.op("vector", lambda e: e.tensor_scalar(
                out=dq[:], in0=uq[:], scalar1=128.0, scalar2=-127.0, op0=ALU.min, op1=ALU.max))
            bld.op("vector", lambda e, oc=oc: e.tensor_scalar(
                out=slot(pos[oc]), in0=dq[:], scalar1=1.0, scalar2=None, op0=ALU.subtract), s_ve)

        spoint(4)
        dump_slots(4, list(pos))
        if dbgS is not None and (cur_tt[0] == dump_tt or dump_tt == -2) and 7 in DUMP_MASK:
            bld.wait("vector", s_pp)
            bld.op("vector", lambda e: e.tensor_copy(out=stt[:, 20:21], in_=isw_g[:]))
            bld.op("vector", lambda e: e.tensor_copy(out=stt[:, 21:22], in_=isw_u[:]))
            bld.op("vector", lambda e: e.tensor_copy(out=stt[:, 22:23], in_=isw_d[:]), s_ve)
            bld.wait("sync", s_ve)
            bld.dma("sync", dbgS[:, 0:4], inv1[:], s_st)
            bld.dma("sync", dbgS[:, 4:8], cs2[:], s_st)
            bld.dma("sync", dbgS[:, 8:12], s2c[:], s_st)
            bld.dma("sync", dbgS[:, 12:16], inv1B[:, 0:4], s_st)
            bld.dma("sync", dbgS[:, 16:20], s2B[:, 0:4], s_st)
            bld.dma("sync", dbgS[:, 20:24], stt[:, 20:24], s_st)
            bld.dma("sync", dbgS[:, 24:28], cs2B[:, 0:4], s_st)
            bld.wait("sync", s_st)
        spoint(5)
        dmm = [None] * NC_DM
        dev = [None] * NC_DM
        bld.wait("sync", s_te)
        bld.wait("tensor", s_ve)
        dn_ld = [None] * NC_DM
        for dm in range(NC_DM):
            sl = dm % 2
            dbuf = (arena[:, 0:FF] if sl == 0 else dslab2[:, 0:FF])
            if dm >= 2:
                bld.wait("sync", s_te, dmm[dm - 2])
            dn_ld[dm] = bld.dma("sync", dbuf, wdT_scr[dm], s_dn[dm % 2])
            bld.wait("tensor", s_dn[dm % 2], dn_ld[dm])
            if dm >= 2:
                bld.wait("tensor", s_ve, dev[dm - 2])
            for ic in range(NC_FF):
                tv = bld.op("tensor", lambda e, ic=ic, sl=sl, dbuf=dbuf: e.matmul(
                    pb[sl][:], dbuf[:, ic * 128:(ic + 1) * 128],
                    slot(pos[ic]), start=(ic == 0), stop=(ic == NC_FF - 1), skip_group_check=True),
                    s_te if ic == NC_FF - 1 else None)
                if ic == NC_FF - 1:
                    dmm[dm] = tv
            bld.wait("vector", s_te, dmm[dm])
            dev[dm] = bld.op("vector", lambda e, dm=dm, sl=sl: e.tensor_tensor(
                out=brD[:, dm * 512:(dm + 1) * 512], in0=pb[sl][:], in1=cs2B[:], op=ALU.mult), s_ve)

        spoint(6)
        if _dump_on(5):
            bld.wait("sync", s_ve)
            bld.dma("sync", dbgF[5][:, 0:NC_DM * 512], brD[:], s_st)
            bld.wait("sync", s_st)
            for eng in ("vector", "scalar", "tensor"):
                bld.wait(eng, s_st)
        for i in range(SUB):
            bld.wait("sync", s_ve)
            bld.wait("sync", s_st)
            bld.wait("sync", s_se)
            t_s1 = bld.dma("sync", s1f[:], s1_scr[i], s_ld)
            bld.wait("tensor", s_ve)
            for cg in range(4):
                bld.wait("tensor", s_ve)
                for j in range(4):
                    dm = cg * 4 + j
                    bld.op("tensor", lambda e, dm=dm, j=j, i=i: e.transpose(
                        pbf[:, j * 128:(j + 1) * 128],
                        brD[:, dm * 512 + i * 128: dm * 512 + (i + 1) * 128], eyeb[:]),
                        s_te if j == 3 else None)
                bld.wait("vector", s_te)
                bld.op("vector", lambda e, cg=cg: e.tensor_copy(
                    out=br_sub.rearrange("p (c f) -> p c f", c=NC_DM)[:, cg * 4:(cg + 1) * 4, :],
                    in_=pbf.rearrange("p (j f) -> p j f", j=4)[:]), s_ve)
            bld.wait("vector", s_ld, t_s1)
            bld.wait("vector", s_se)
            bld.op("vector", lambda e: e.tensor_tensor(
                out=sq_f[:], in0=s1f[:], in1=br_sub[:], op=ALU.add), s_ve)
            ln_block(sq_f[:], s1f[:], i)
            bld.wait("sync", s_se)
            bld.dma("sync", out[(TT * TILE + i * 128):(TT * TILE + (i + 1) * 128), :], s1f[:], s_st)



    for TT in range(g.get("n_tiles", NT)):
        cur_tt[0] = TT
        lvlAt = ExitStack()
        stream1 = sbufB(lvlAt, f"stream1_{TT}", SUB * DM, F32)
        v_sb = sbufB(lvlAt, f"v_sb_{TT}", KT * 512)
        k_sb = sbufB(lvlAt, f"k_sb_{TT}", NKV * S)
        q_sb = sbufB(lvlAt, f"q_sb_{TT}", NH * 512)
        pT = sbufB(lvlAt, f"pT_{TT}", 3 * 512)
        sumC = sbufB(lvlAt, f"sumC_{TT}", 512)
        OT = sbufB(lvlAt, f"OT_{TT}", NH * 512)
        wos = sbufB(lvlAt, f"wos_{TT}", 2 * DM)
        brF = sbufB(lvlAt, f"brF_{TT}", NC_DM * 512)
        recipB = sbufB(lvlAt, f"recipB_{TT}", 512, F32)
        emit_attn(TT, stream1, v_sb, k_sb, q_sb, pT, sumC, OT, wos, brF, recipB)
        bld.wait("sync", s_st)
        bld.barrier(s_ph)
        lvlAt.close()

        lvlF = ExitStack()
        Xs = sbufB(lvlF, f"Xs_{TT}", 66 * 512)
        Xs_cur[0] = Xs
        xq_f = sbufB(lvlF, f"xq_f_{TT}", NC_DM * 512)
        arena = sbufB(lvlF, f"arena_{TT}", 4 * 2048)
        dslab2 = sbufB(lvlF, f"dslab2_{TT}", FF)
        k0i_t = sbufB(lvlF, f"k0i_t_{TT}", DM, I32)
        s1f = sbufB(lvlF, f"s1f_{TT}", DM, F32)
        uq = sbufB(lvlF, f"uq_{TT}", 512, F32)
        k0q = sbufB(lvlF, f"k0q_{TT}", 512, I32)
        dq = sbufB(lvlF, f"dq_{TT}", 512, F32)
        gs = sbufB(lvlF, f"gs_{TT}", 2 * 512)
        tu = sbufB(lvlF, f"tu_{TT}", 2 * 512, F32)
        maxC = sbufB(lvlF, f"maxC_{TT}", 512, F32)
        brD = sbufB(lvlF, f"brD_{TT}", NC_DM * 512)
        emit_ffn(TT, Xs, xq_f, arena, dslab2, k0i_t, s1f, uq, k0q, dq, gs, tu, maxC, brD)
        bld.wait("sync", s_st)
        bld.barrier(s_ph)
        lvlF.close()

    bld.wait("sync", s_st)
    lvlB.close()



def _consts():
    h128 = _hadamard(128).astype(ml_dtypes.bfloat16)
    eyeb = np.eye(128).astype(ml_dtypes.bfloat16)
    eyef = np.eye(128, dtype=np.float32)
    onesb = np.ones((128, 128), dtype=ml_dtypes.bfloat16)
    onesf = np.ones((128, 128), dtype=np.float32)
    return h128, eyeb, eyef, onesb, onesf


def make_in_maps(x, wq_, wk_, wv_, wo_, bq_, bk_, bv_, bo_, wg_, wu_, wd_):
    h128, eyeb, eyef, onesb, onesf = _consts()
    common = dict(
        wq=np.ascontiguousarray(wq_), wk=np.ascontiguousarray(wk_),
        wv=np.ascontiguousarray(wv_), wo=np.ascontiguousarray(wo_),
        bq=np.ascontiguousarray(bq_), bk=np.ascontiguousarray(bk_),
        bv=np.ascontiguousarray(bv_), bo=np.ascontiguousarray(bo_),
        wg=np.ascontiguousarray(wg_), wu=np.ascontiguousarray(wu_),
        wd=np.ascontiguousarray(wd_),
        h128=h128, eyeb=eyeb, eyef=eyef, onesb=onesb, onesf=onesf,
    )
    in_maps = []
    for core in range(8):
        b, half = core // 2, core % 2
        m = dict(common)
        m["x_all"] = np.ascontiguousarray(x[b])
        m["x_own"] = np.ascontiguousarray(x[b, half * TOK:(half + 1) * TOK, :])
        in_maps.append(m)
    return in_maps


def kernel(L0_attn_inputs_q, L0_attn_wq, L0_attn_bq, L0_attn_wk, L0_attn_bk,
           L0_attn_wv, L0_attn_bv, L0_attn_wo, L0_attn_bo,
           L0_ffn_w_gate, L0_ffn_w_up, L0_ffn_w_down):
    x = np.ascontiguousarray(L0_attn_inputs_q)
    in_maps = make_in_maps(x, L0_attn_wq, L0_attn_wk, L0_attn_wv, L0_attn_wo,
                           L0_attn_bq, L0_attn_bk, L0_attn_bv, L0_attn_bo,
                           L0_ffn_w_gate, L0_ffn_w_up, L0_ffn_w_down)
    nc = build_program()
    res = run_bass_kernel_spmd(nc, in_maps, core_ids=list(range(8)))
    outp = np.empty((4, S, DM), dtype=np.float32)
    for core in range(8):
        b, half = core // 2, core % 2
        outp[b, half * TOK:(half + 1) * TOK, :] = np.asarray(res.results[core]["out"])
    return outp
